# version 03
# baseline (speedup 1.0000x reference)
import numpy as np
import ml_dtypes

import concourse.bass as bass
import concourse.mybir as mybir
from concourse.bass_utils import run_bass_kernel_spmd

F32 = mybir.dt.float32
BF16 = mybir.dt.bfloat16
AF = mybir.ActivationFunctionType
ALU = mybir.AluOpType
BF16NP = ml_dtypes.bfloat16

D = 1024
KC = 8
T = 128
M = 32
TOK = 4096
PAIRS = 8
DH = 2048
HKC = 16
EXP = 8
NE = 8
ET = 512
CAP = 5120.0
EPS1 = 1e-5
CGN = 4096.0 * 64e-5
BTC = 129
BNDC = 128
H1W = 128 + 1 + TOK

DEBUG_TAPS = []


def _host_consts():
    c = {}
    i = np.arange(128, dtype=np.float32)
    c["IDN"] = np.eye(128, dtype=BF16NP)
    c["EYE"] = np.eye(128, dtype=np.float32)
    c["M1"] = np.maximum(i[None, :] - i[:, None] - 1.0, 0.0).astype(np.float32)
    c["TRI"] = (i[None, :] > i[:, None]).astype(BF16NP)
    c["LT"] = (i[:, None] <= i[None, :]).astype(BF16NP)
    c["ONESB"] = np.ones((128, 128), dtype=BF16NP)
    c["REVJ"] = np.ascontiguousarray((127.0 - i)[:, None])
    c["IOTAI"] = np.tile(i[None, :], (128, 1)).astype(np.float32)
    c["PERT"] = np.ascontiguousarray(
        (1.0 - 1e-6 * np.arange(8, dtype=np.float32))[None, :].repeat(128, 0))
    sels = np.zeros((128, 128), dtype=np.float32)
    for cc in range(8):
        sels[0:64, 16 * cc + 2 * cc] = 1.0
        sels[64:128, 16 * cc + 2 * cc + 1] = 1.0
    c["SELS"] = sels.astype(BF16NP)
    selts = np.zeros((16, 8 * 128), dtype=np.float32)
    for cc in range(8):
        selts[2 * cc, 128 * cc:128 * cc + 64] = 1.0
        selts[2 * cc + 1, 128 * cc + 64:128 * cc + 128] = 1.0
    c["SELTS64"] = (64.0 * selts).astype(BF16NP)
    return c


def _shared_weights(inputs):
    w = {}
    w["WrT"] = np.ascontiguousarray(np.asarray(inputs["L0_attn_W_r"], np.float32).T)
    w["WkT"] = np.ascontiguousarray(np.asarray(inputs["L0_attn_W_k"], np.float32).T)
    w["WvT"] = np.ascontiguousarray(np.asarray(inputs["L0_attn_W_v"], np.float32).T)
    w["WgT"] = np.ascontiguousarray(np.asarray(inputs["L0_attn_W_g"], np.float32).T)
    w["WoT"] = np.ascontiguousarray(np.asarray(inputs["L0_attn_W_o"], np.float32).T)
    w["WgateT"] = np.ascontiguousarray(np.asarray(inputs["L0_ffn_w_gate"], np.float32).T)
    w["WdT"] = np.ascontiguousarray(
        np.asarray(inputs["L0_ffn_w_down"], np.float32).transpose(0, 2, 1))
    w["WrtrT"] = np.ascontiguousarray(np.asarray(inputs["L0_ffn_w_router"], np.float32).T)
    w["wmod"] = np.ascontiguousarray(np.asarray(inputs["L0_wmod"], np.float32))
    tmix = np.empty((128, 32), dtype=np.float32)
    for vi, key in enumerate(["time_mix_r", "time_mix_k", "time_mix_v", "time_mix_g"]):
        v = np.asarray(inputs[f"L0_attn_{key}"], np.float32).reshape(D)
        tmix[:, vi * 8:(vi + 1) * 8] = v.reshape(KC, 128).T
    w["tmix"] = tmix
    w["tdrow"] = np.asarray(inputs["L0_attn_time_decay"], np.float32).reshape(1, 16).copy()
    w["tfrow"] = np.asarray(inputs["L0_attn_time_faaaa"], np.float32).reshape(1, 16).copy()
    w["lnw"] = np.ascontiguousarray(
        np.asarray(inputs["L0_attn_ln_w"], np.float32).reshape(KC, 128).T)
    w["lnb"] = np.ascontiguousarray(
        np.asarray(inputs["L0_attn_ln_b"], np.float32).reshape(KC, 128).T)
    return w


def _prep_core_inputs(cid, inputs, consts, w):
    b, j = cid // 4, cid % 4
    x = np.asarray(inputs["L0_attn_xq"], np.float32)
    xs = np.empty((1 + TOK, D), dtype=np.float32)
    t0 = j * TOK
    xs[0] = 0.0 if j == 0 else x[b, t0 - 1]
    xs[1:] = x[b, t0:t0 + TOK]
    m = dict(w)
    m["xs"] = xs
    m["bnd"] = np.full((128, 1), 0.0 if j == 0 else 1.0, dtype=np.float32)
    pm = np.zeros((8, 128), dtype=np.float32)
    pm[np.arange(8) < cid, :] = 1.0
    m["pmask"] = pm
    spow = np.zeros((16, 8), dtype=np.float32)
    smask = np.zeros((16, 8), dtype=np.float32)
    for r in range(8):
        br, jr = r // 4, r % 4
        if br == b and jr < j:
            spow[:, r] = 4096.0 * (j - 1 - jr)
            smask[:, r] = 1.0
    m["spow"] = spow
    m["smask"] = smask
    m["crow"] = np.ascontiguousarray(np.asarray(inputs["L0_c"], np.float32)[b:b + 1, :])
    m.update(consts)
    return m


class Prog:
    ENGS = ("pe", "dv", "ac", "gp", "sy")

    def __init__(self):
        self.steps = []
        self.cnt = {k: 0 for k in self.ENGS}

    def emit(self, eng, fn, wait=(), dma=0):
        wl = {}
        for tkn in wait:
            if tkn is None:
                continue
            en, v = tkn
            if v > 0 and (en not in wl or wl[en] < v):
                wl[en] = v
        self.steps.append((eng, sorted(wl.items()), dma, fn))
        self.cnt[eng] += 16 * dma if dma else 1
        return (eng, self.cnt[eng])


def build(taps=()):
    taps = set(taps)
    nc = bass.Bass("TRN2", target_bir_lowering=False)

    d = {}
    def din(name, shape, dt=F32):
        d[name] = nc.dram_tensor(name, shape, dt, kind="ExternalInput").ap()
    din("xs", [1 + TOK, D])
    din("bnd", [128, 1]); din("pmask", [8, 128])
    din("spow", [16, 8]); din("smask", [16, 8])
    for w in ("WrT", "WkT", "WvT", "WgT", "WoT"):
        din(w, [D, D])
    din("WgateT", [D, DH]); din("WdT", [EXP, DH, D]); din("WrtrT", [D, EXP])
    din("wmod", [D, 6 * D]); din("crow", [1, D])
    din("tmix", [128, 32]); din("tdrow", [1, 16]); din("tfrow", [1, 16])
    din("lnw", [128, 8]); din("lnb", [128, 8])
    din("IDN", [128, 128], BF16); din("EYE", [128, 128]); din("M1", [128, 128])
    din("TRI", [128, 128], BF16); din("IOTAI", [128, 128]); din("REVJ", [128, 1])
    din("LT", [128, 128], BF16); din("ONESB", [128, 128], BF16)
    din("PERT", [128, 8]); din("SELS", [128, 128], BF16)
    din("SELTS64", [16, 8 * 128], BF16)

    out = nc.dram_tensor("out", [TOK, D], F32, kind="ExternalOutput").ap()
    x2s = nc.dram_tensor("x2s", [TOK, D], BF16, kind="Internal").ap()
    bscr = nc.dram_tensor("bscr", [1, 12288], F32, kind="Internal").ap()
    ccin = nc.dram_tensor("ccin", [1, 128 * 512], F32, kind="Internal").ap()
    ccout = nc.dram_tensor("ccout", [8, 128 * 512], F32, kind="Internal").ap()
    ccin2 = nc.dram_tensor("ccin2", [1, 8], F32, kind="Internal").ap()
    ccout2 = nc.dram_tensor("ccout2", [8, 8], F32, kind="Internal").ap()

    tap_shapes = {
        "h1": ([128, KC * H1W], BF16),
        "dk": ([128, 16 * 128 + 8 * 128 + D], BF16),
        "dk2": ([128, 160], F32),
        "st": ([128, 512], F32),
        "y": ([128, D], BF16),
        "gg": ([128, D], BF16),
        "x2": ([TOK, D], BF16),
        "rt": ([128, 32 * 2], F32),
        "cmb": ([128, 32 * 8], F32),
        "hid": ([128, 8192], BF16),
        "ccg": ([9, 128 * 512], F32),
        "gn": ([128, D], BF16),
        "gs": ([16, 256], F32),
        "p2a0": ([128, 4 * D + 8], BF16),
        "bdg": ([128, 32], F32),
        "bdh": ([128, D + 8], BF16),
    }
    tapd = {}
    for tname in taps:
        shape, dt = tap_shapes[tname]
        tapd[tname] = nc.dram_tensor("tap_" + tname, shape, dt, kind="ExternalOutput").ap()

    s = {}
    def sba(name, shape, dt):
        s[name] = nc.alloc_sbuf_tensor(name, shape, dt).ap()
        return s[name]

    h1 = [sba(f"h1_{kc}", [128, H1W], BF16) for kc in range(KC)]
    WS = [sba(f"ws{i}", [128, 8192], BF16) for i in range(5)]
    sba("yacc", [128, 4 * 1024], BF16)
    sba("spair", [128, 512], F32)
    sba("sbf", [128, 512], BF16)
    sba("wmat", [128, 16 * 128], BF16)
    sba("WBCm", [128, 8 * 128], BF16)
    sba("WKM", [128, D], BF16)
    sba("lndbc", [128, 16], F32)
    sba("LNDCOL", [128, 8], F32)
    sba("WSCOL", [128, 8], F32)
    sba("UCOL", [128, 16], F32)
    sba("lrow", [1, 64], F32)
    sba("lcol", [16, 4], F32)
    sba("C8", [16, 8], F32)
    sba("C8P", [128, 64], F32)
    sba("modfm", [128, 48], F32)
    sba("combine", [128, 32 * 8], F32)
    sba("CAb", [128, 8], F32); sba("SAb", [128, 8], F32)
    sba("modG", [128, D], BF16)
    sba("scb", [128, 8], BF16)
    sba("tmixS", [128, 32], F32)
    sba("lnwS", [128, 8], F32); sba("lnbS", [128, 8], F32)
    sba("tdS", [1, 16], F32); sba("tfS", [1, 16], F32)
    sba("bndS", [128, 1], F32)
    sba("pmaskS", [8, 128], F32)
    sba("spowS", [16, 8], F32); sba("smaskS", [16, 8], F32)
    sba("rtrS", [128, 64], BF16)
    for nm, dt in [("IDNs", BF16),
                   ("LTs", BF16), ("ONESBs", BF16), ("SELSs", BF16)]:
        sba(nm, [128, 128], dt)
    sba("REVJs", [128, 1], F32)
    sba("PERTs", [128, 8], F32)
    sba("SELTS64s", [16, 8 * 128], BF16)
    sba("xb0", [128, D], BF16); sba("xb1", [128, D], BF16)
    sba("h1tm", [128, D], BF16)
    sba("dmix", [128, D], BF16)
    sba("mixA", [128, D], BF16); sba("mixB", [128, D], BF16)
    sba("rfm", [128, D], BF16)
    sba("kfm", [128, D], BF16)
    sba("ktm", [128, D], BF16)
    sba("vtm", [128, D], BF16)
    sba("rrb", [128, D], BF16)
    sba("ych", [128, D], BF16)
    sba("gch", [128, D], BF16)
    sba("zt", [128, 128], F32)
    sba("stats", [16, 128], F32)
    sba("rstd", [16, 128], F32)
    sba("rstdb", [16, 128], BF16); sba("s1rb", [16, 128], BF16)
    sba("x2f", [128, D], F32)
    sba("lncol", [128, 16], F32)
    sba("oh1st", [128, 32 * 8], BF16); sba("oh2st", [128, 32 * 8], BF16)
    sba("pgst", [128, 32 * 2], F32)
    sba("ggst", [128, 32 * 2], F32)
    sba("ohsb", [128, 8], BF16)
    sba("gts", [128, 8], F32); sba("pgt", [128, 8], F32); sba("srt", [128, 8], F32)
    sba("r8a", [128, 8], F32)
    sba("cnt", [128, 8], F32)
    sba("basebc", [128, 8], F32)
    sba("cnts", [8, 8], F32)
    sba("rcol", [128, 16], F32)
    sba("epsc", [128, 2], F32)

    assert nc.sbuf_bytes_remaining >= 0, \
        f"SBUF over budget by {-nc.sbuf_bytes_remaining} B/partition"
    build.sbuf_slack = nc.sbuf_bytes_remaining

    s["IOTAIs"] = s["x2f"][:, 0:128]
    s["SAc"] = s["modfm"][:, 0:8]
    s["CAc"] = s["modfm"][:, 8:16]
    s["SBc"] = s["modfm"][:, 24:32]
    s["CBc"] = s["modfm"][:, 32:40]
    s["TRIs"] = s["x2f"][:, 512:576].bitcast(BF16)
    s["attw"] = s["zt"].bitcast(BF16)
    s["M1s"] = s["x2f"][:, 128:256]
    s["EYEs"] = s["x2f"][:, 256:384]
    s["e1"] = s["x2f"][:, 384:512]
    s["sq1"] = s["zt"][0:16, 0:128]
    s["mixC"] = s["yacc"][:, 0:1024]
    s["mixD"] = s["yacc"][:, 1024:2048]

    PB = [nc.alloc_psum_tensor(f"pb{i}", [128, 512], F32).ap() for i in range(8)]
    pt = [PB[0].bitcast(BF16)[:, 0:128], PB[1].bitcast(BF16)[:, 0:128]]
    ATT = [PB[2][:, 0:128], PB[2][:, 0:128]]
    POUT = PB[3][:, 0:128]
    PAX2 = PB[4][:, 128:256]
    PRT = PB[6][:, 256:264]
    PCS1 = PB[6][:, 264:272]
    PCS2 = PB[6][:, 272:280]
    PBASE = PB[6][:, 280:288]
    PS1 = PB[6][0:16, 0:128]
    PS2 = PB[6][0:16, 128:256]
    PB1 = PB[6][:, 256:384]
    PB0 = PB[6][:, 384:512]
    PRJ = [PB[4], PB[5]]
    MOD = [PB[6][:, 0:48], PB[6][:, 64:112]]
    PVV = PB[7]
    PWO = [PB[7], PB[7]]

    def h1c(mm, off=0, n=T):
        return slice(BTC + T * mm + off, BTC + T * mm + off + n)

    def h2c(mm, off=0, n=T):
        return slice(T * mm + off, T * mm + off + n)

    pr = Prog()
    E = pr.emit

    def dram_ap(off, dims):
        return bass.AP(bscr.tensor, off, dims)

    ld_map = [("bnd", "bndS"), ("pmask", "pmaskS"), ("spow", "spowS"),
              ("smask", "smaskS"), ("tmix", "tmixS"), ("lnw", "lnwS"),
              ("lnb", "lnbS"), ("tdrow", "tdS"), ("tfrow", "tfS"),
              ("IDN", "IDNs"), ("REVJ", "REVJs"), ("LT", "LTs"),
              ("ONESB", "ONESBs"), ("PERT", "PERTs"), ("SELS", "SELSs"),
              ("SELTS64", "SELTS64s")]
    tLD = E("sy", lambda e: [e.dma_start(out=s[dst], in_=d[src])
                             for src, dst in ld_map], dma=len(ld_map))
    tEps = E("dv", lambda e: (e.memset(s["epsc"][:, 0:1], EPS1),
                              e.memset(s["epsc"][:, 1:2], CGN))[-1], wait=[])
    tCrow = E("sy", lambda e: [e.dma_start(
        out=s["x2f"][0:2, 0:512],
        in_=bass.AP(d["crow"].tensor, 0, [[512, 2], [1, 512]]))], dma=1)

    def fwk(slot, col0, w, r0):
        return lambda e: [e.dma_start(out=WS[slot][:, col0:col0 + 1024],
                                      in_=d[w][r0:r0 + 128, :])]
    tw = []
    for kc in range(KC):
        tw.append(E("gp", fwk(0, kc * 1024, "WkT", 128 * kc), dma=1))
    for kc in range(KC):
        tw.append(E("gp", fwk(1, kc * 1024, "WvT", 128 * kc), dma=1))
    tW2A = tw[-1]

    tLnd = E("ac", lambda e: (e.activation(s["lrow"][0:1, 0:16], s["tdS"], AF.Exp),
                              e.mul(s["lrow"][0:1, 0:16], s["lrow"][0:1, 0:16], -1.0))[-1],
             wait=[tLD])
    tWsr = E("ac", lambda e: e.activation(s["lrow"][0:1, 32:48], s["lrow"][0:1, 0:16],
                                          AF.Exp, scale=128.0), wait=[tLnd])
    tSilc = E("ac", lambda e: e.activation(s["x2f"][0:2, 0:512], s["x2f"][0:2, 0:512],
                                           AF.Silu), wait=[tCrow])
    tBs1 = E("sy", lambda e: [
        e.dma_start(out=bscr[0:1, 0:16], in_=s["lrow"][0:1, 0:16]),
        e.dma_start(out=bscr[0:1, 16:32], in_=s["tfS"]),
        e.dma_start(out=bscr[0:1, 32:48], in_=s["lrow"][0:1, 32:48]),
        e.dma_start(out=bass.AP(bscr.tensor, 48, [[512, 2], [1, 512]]),
                    in_=s["x2f"][0:2, 0:512]),
    ], wait=[tLnd, tWsr, tSilc], dma=4)
    def fbr(e):
        res = [
            e.dma_start(out=s["lcol"][:, 0:1], in_=dram_ap(0, [[1, 16], [1, 1]]),
                        allow_slow_non_contiguous=True),
            e.dma_start(out=s["lndbc"], in_=dram_ap(0, [[0, 128], [1, 16]])),
            e.dma_start(out=s["UCOL"], in_=dram_ap(16, [[0, 128], [1, 16]])),
        ]
        for cc in range(8):
            for hb in range(2):
                rows = slice(64 * hb, 64 * (hb + 1))
                res.append(e.dma_start(
                    out=s["LNDCOL"][rows, cc:cc + 1],
                    in_=dram_ap(2 * cc + hb, [[0, 64], [1, 1]]),
                    allow_slow_non_contiguous=True))
                res.append(e.dma_start(
                    out=s["WSCOL"][rows, cc:cc + 1],
                    in_=dram_ap(32 + 2 * cc + hb, [[0, 64], [1, 1]]),
                    allow_slow_non_contiguous=True))
        return res
    tBr = E("sy", fbr, wait=[tBs1], dma=3 + 32)
    tScb = E("gp", lambda e: [e.dma_start(
        out=s["scb"][:, kc:kc + 1],
        in_=dram_ap(48 + 128 * kc, [[1, 128], [1, 1]])) for kc in range(KC)],
        wait=[tBs1], dma=KC)
    tWKa = E("ac", lambda e: [e.activation(s["zt"][:, hh:hh + 1], s["REVJs"],
                                           AF.Exp, scale=s["lndbc"][:, hh:hh + 1])
                              for hh in range(16)][-1], wait=[tBr, tLD])
    tWKM = E("dv", lambda e: [e.tensor_scalar(
        s["WKM"][:, 64 * hh:64 * (hh + 1)], s["ONESBs"][:, 0:64],
        s["zt"][:, hh:hh + 1], None, ALU.mult) for hh in range(16)][-1],
        wait=[tWKa, tLD])
    tCload = E("sy", lambda e: [
        e.dma_start(out=s["IOTAIs"], in_=d["IOTAI"]),
        e.dma_start(out=s["M1s"], in_=d["M1"]),
        e.dma_start(out=s["EYEs"], in_=d["EYE"]),
        e.dma_start(out=s["TRIs"], in_=d["TRI"]),
    ], wait=[tWKM], dma=4)
    tWBC = E("ac", lambda e: [e.activation(s["WBCm"][:, 128 * cc:128 * (cc + 1)],
                                           s["IOTAIs"], AF.Exp,
                                           scale=s["LNDCOL"][:, cc:cc + 1])
                              for cc in range(PAIRS)][-1], wait=[tBr, tLD, tCload])
    tWm = []
    for hh in range(16):
        wl = [tBr, tLD, tCload] + ([tWm[-1]] if tWm else [])
        tWmA = E("ac", (lambda hh=hh: (lambda e: e.activation(
            s["e1"], s["M1s"], AF.Exp, scale=s["lndbc"][:, hh:hh + 1])))(), wait=wl)
        tWm.append(E("dv", (lambda hh=hh: (lambda e: (
            e.tensor_tensor(s["zt"], s["e1"], s["TRIs"], ALU.mult),
            e.scalar_tensor_tensor(s["wmat"][:, 128 * hh:128 * (hh + 1)], s["EYEs"],
                                   s["UCOL"][:, hh:hh + 1], s["zt"],
                                   ALU.mult, ALU.add))[-1]))(), wait=[tWmA]))
    tWmat = tWm[-1]

    tC8a = E("dv", lambda e: e.tensor_scalar(s["C8"], s["spowS"], s["lcol"][:, 0:1],
                                             None, ALU.mult), wait=[tBr, tLD])
    tC8b = E("ac", lambda e: e.activation(s["C8"], s["C8"], AF.Exp), wait=[tC8a])
    tC8c = E("dv", lambda e: e.tensor_tensor(s["C8"], s["C8"], s["smaskS"], ALU.mult),
             wait=[tC8b])
    tC8d = E("sy", lambda e: [e.dma_start(
        out=bass.AP(bscr.tensor, 2048, [[8, 16], [1, 8]]), in_=s["C8"])],
        wait=[tC8c], dma=1)
    def fc8e(e):
        res = []
        for hb in range(2):
            rows = slice(64 * hb, 64 * (hb + 1))
            res.append(e.dma_start(
                out=s["C8P"][rows, :].rearrange("p (c r) -> p c r", c=8),
                in_=dram_ap(2048 + 8 * hb, [[0, 64], [16, 8], [1, 8]]),
                allow_slow_non_contiguous=True))
        return res
    tC8e = E("sy", fc8e, wait=[tC8d], dma=2)

    tMz = E("dv", lambda e: e.memset(s["modfm"], 0.0), wait=[])
    tModMM = [None, None]
    tModAdd = [tMz, tMz]
    for kc in range(KC):
        slot = 2 + (kc % 2)
        bank = kc % 2
        wl = [tLD, tScb] if kc < 2 else [tModMM[0], tModMM[1]]
        tload = E("gp", (lambda slot=slot, kc=kc: (lambda e: [e.dma_start(
            out=WS[slot][:, 0:6144], in_=d["wmod"][128 * kc:128 * (kc + 1), :])]))(),
            wait=wl, dma=1)
        def fmodmm(slot=slot, kc=kc, bank=bank):
            def f(e):
                last = None
                for ob in range(48):
                    last = e.matmul(MOD[bank][:, ob:ob + 1],
                                    WS[slot][:, 128 * ob:128 * (ob + 1)],
                                    s["scb"][:, kc:kc + 1],
                                    start=True, stop=True)
                return last
            return f
        wl = [tload, tScb, tModAdd[0], tModAdd[1]]
        tModMM[bank] = E("pe", fmodmm(), wait=wl)
        tModAdd[bank] = E("dv", (lambda bank=bank: (lambda e: e.tensor_tensor(
            s["modfm"], s["modfm"], MOD[bank], ALU.add)))(),
            wait=[tModMM[bank], tMz])
    tModDone = E("dv", lambda e: (
        e.tensor_scalar(s["modfm"][:, 8:16], s["modfm"][:, 8:16], 1.0, None, ALU.add),
        e.tensor_scalar(s["modfm"][:, 32:40], s["modfm"][:, 32:40], 1.0, None,
                        ALU.add))[-1], wait=[tModAdd[0], tModAdd[1]])

    def emit_clamp(c0, wait):
        sc2 = s["zt"][:, 0:8]
        sgn = s["zt"][:, 8:16]
        mg = s["modfm"][:, c0:c0 + 8]
        tA = E("dv", lambda e: (
            e.tensor_scalar(sgn, mg, 0.0, None, ALU.is_ge),
            e.tensor_scalar(sc2, mg, -1.0, None, ALU.mult))[-1], wait=wait)
        tB = E("dv", lambda e: (
            e.tensor_scalar(sgn, sgn, 2.0, -1.0, ALU.mult, ALU.add),
            e.tensor_tensor(sc2, sc2, mg, ALU.max))[-1], wait=[tA])
        tC = E("dv", lambda e: e.tensor_scalar(sc2, sc2, 0.25, None, ALU.max),
               wait=[tB])
        return E("dv", lambda e: e.tensor_tensor(mg, sgn, sc2, ALU.mult),
                 wait=[tC])
    tCl1 = emit_clamp(16, [tModDone, tWmat])
    tCl2 = emit_clamp(40, [tCl1])
    def fmodb(e):
        res = []
        for i, c0 in enumerate((16, 40)):
            for cc in range(8):
                res.append(e.dma_start(
                    out=bscr[0:1, 4096 + 1024 * i + 128 * cc:
                             4096 + 1024 * i + 128 * (cc + 1)],
                    in_=s["modfm"][:, c0 + cc:c0 + cc + 1]))
        return res
    tModB = E("gp", fmodb, wait=[tCl2], dma=16)

    tCAc = tSAc = tCBc = tSBc = tModDone
    tBndCA = E("dv", lambda e: (
        e.tensor_scalar(s["CAb"], s["CAc"], s["bndS"], None, ALU.mult),
        e.tensor_scalar(s["SAb"], s["SAc"], s["bndS"], None, ALU.mult))[-1],
        wait=[tCAc, tSAc, tLD])
    tGA = E("gp", lambda e: [e.dma_start(
        out=s["modG"], in_=dram_ap(4096, [[0, 128], [1, D]]))],
        wait=[tModB], dma=1)

    tXL = {}
    tH1tm = {}
    tTr = {}
    tEv = {}
    tSqA = {}
    tSt1 = {}
    ptTok = [None, None]

    tXL[-1] = E("gp", lambda e: [e.dma_start(out=s["xb1"][0:1, :],
                                             in_=d["xs"][0:1, :])], dma=1)
    tTapB = None
    for mm in range(-1, M):
        if mm >= 0:
            xb = s["xb0"] if mm % 2 == 0 else s["xb1"]
            wl = [tH1tm[mm - 2]] if mm >= 2 else ([tH1tm[-1]] if mm == 1 else [])
            tXL[mm] = E("gp", (lambda mm=mm, xb=xb: (lambda e: [e.dma_start(
                out=xb, in_=d["xs"][1 + T * mm:1 + T * (mm + 1), :])]))(),
                wait=wl, dma=1)
        src = s["xb1"] if mm == -1 else (s["xb0"] if mm % 2 == 0 else s["xb1"])
        rows = slice(0, 1) if mm == -1 else slice(0, 128)
        wl = [tXL[mm], tLD] + ([tSt1[mm - 1]] if (mm - 1) in tSt1 else [])
        if mm == 0 and tTapB is not None:
            wl.append(tTapB)
        tSqA[mm] = E("ac", (lambda src=src, rows=rows, mm=mm: (lambda e: e.activation(
            s["dmix"][rows, :], src[rows, :], AF.Square,
            accum_out=s["lncol"][rows, 2:3])))(), wait=wl)
        tSt0 = E("dv", (lambda src=src, rows=rows: (lambda e: e.reduce_sum(
            s["lncol"][rows, 0:1], src[rows, :], mybir.AxisListType.X)))(),
            wait=[tXL[mm], tLD] + ([tTapB] if (mm == 0 and tTapB) else []))
        def fst1a(rows=rows):
            def f(e):
                e.tensor_scalar(s["lncol"][rows, 12:13], s["lncol"][rows, 0:1],
                                1.0 / 1024, None, ALU.mult)
                return e.tensor_scalar(s["lncol"][rows, 11:12],
                                       s["lncol"][rows, 2:3],
                                       1.0 / 1024, None, ALU.mult)
            return f
        t1a = E("dv", fst1a(), wait=[tSt0, tSqA[mm]])
        t1b = E("dv", (lambda rows=rows: (lambda e: e.tensor_tensor(
            s["lncol"][rows, 1:2], s["lncol"][rows, 12:13],
            s["lncol"][rows, 12:13], ALU.mult)))(), wait=[t1a])
        tSt1[mm] = E("dv", (lambda rows=rows: (lambda e: e.tensor_tensor(
            s["lncol"][rows, 13:14], s["lncol"][rows, 11:12],
            s["lncol"][rows, 1:2], ALU.subtract)))(), wait=[t1b])
        tQ = E("ac", (lambda rows=rows: (lambda e: e.activation(
            s["lncol"][rows, 14:15], s["lncol"][rows, 13:14], AF.Sqrt,
            bias=s["epsc"][rows, 0:1])))(), wait=[tSt1[mm], tEps])
        tRc = E("dv", (lambda rows=rows: (lambda e: e.reciprocal(
            s["lncol"][rows, 15:16], s["lncol"][rows, 14:15])))(), wait=[tQ])
        def fnx(src=src, rows=rows):
            def f(e):
                return e.tensor_scalar(s["h1tm"][rows, :], src[rows, :],
                                       s["lncol"][rows, 12:13],
                                       s["lncol"][rows, 15:16],
                                       ALU.subtract, ALU.mult)
            return f
        wl = [tRc] + ([tTr[(mm - 1, KC - 1)]] if mm >= 0 else [])
        tH1tm[mm] = E("dv", fnx(), wait=wl)
        for kc in range(KC):
            gi = (mm + 1) * KC + kc
            bank = gi % 2
            wl = [tH1tm[mm], tLD]
            if ptTok[bank] is not None:
                wl.append(ptTok[bank])
            if mm == -1:
                fn = (lambda kc=kc, bank=bank: (lambda e: e.transpose(
                    pt[bank][:, 0:1], s["h1tm"][0:1, 128 * kc:128 * (kc + 1)],
                    s["IDNs"][0:1, 0:1])))()
            else:
                fn = (lambda kc=kc, bank=bank: (lambda e: e.transpose(
                    pt[bank], s["h1tm"][:, 128 * kc:128 * (kc + 1)], s["IDNs"])))()
            tTr[(mm, kc)] = E("pe", fn, wait=wl)
            if mm == -1:
                fe = (lambda kc=kc, bank=bank: (lambda e: e.tensor_scalar(
                    h1[kc][:, BNDC:BNDC + 1], pt[bank][:, 0:1],
                    s["CAb"][:, kc:kc + 1], s["SAb"][:, kc:kc + 1],
                    ALU.mult, ALU.add)))()
                wl2 = [tTr[(mm, kc)], tBndCA]
            else:
                fe = (lambda kc=kc, bank=bank, mm=mm: (lambda e: e.tensor_scalar(
                    h1[kc][:, h1c(mm)], pt[bank],
                    s["CAc"][:, kc:kc + 1], s["SAc"][:, kc:kc + 1],
                    ALU.mult, ALU.add)))()
                wl2 = [tTr[(mm, kc)], tCAc, tSAc]
            ptTok[bank] = tEv[(mm, kc)] = E("dv", fe, wait=wl2)
        if mm == -1 and "bdg" in tapd:
            tTapB = E("sy", lambda e: [
                e.dma_start(out=tapd["bdg"][:, 0:8], in_=s["CAb"]),
                e.dma_start(out=tapd["bdg"][:, 8:16], in_=s["SAb"]),
                e.dma_start(out=tapd["bdg"][:, 16:32], in_=s["lncol"]),
                e.dma_start(out=tapd["bdh"][:, 0:D], in_=s["h1tm"]),
            ] + [e.dma_start(out=tapd["bdh"][:, D + kc:D + kc + 1],
                             in_=h1[kc][:, BNDC:BNDC + 1],
                             allow_slow_non_contiguous=True) for kc in range(KC)],
                wait=[tEv[(-1, KC - 1)], tBndCA], dma=4 + KC)
    tH1done = tEv[(M - 1, KC - 1)]

    tDm = {}; tXk = {}; tXv = {}
    tKt = {}; tKe = {}; tVt = {}; tVe = {}; tKk = {}
    tAx = {}; tSu = {}
    tSinit = E("dv", lambda e: e.memset(s["spair"], 0.0), wait=[])

    def fdmix(mm):
        def f(e):
            last = None
            for kc in range(KC):
                last = e.tensor_tensor(s["dmix"][:, 128 * kc:128 * (kc + 1)],
                                       h1[kc][:, h1c(mm)], h1[kc][:, h1c(mm, -1)],
                                       ALU.subtract)
            return last
        return f

    def fmixv(dst, tmcol0, mm):
        def f(e):
            last = None
            for kc in range(KC):
                last = e.scalar_tensor_tensor(
                    s[dst][:, 128 * kc:128 * (kc + 1)],
                    s["dmix"][:, 128 * kc:128 * (kc + 1)],
                    s["tmixS"][:, tmcol0 + kc:tmcol0 + kc + 1],
                    h1[kc][:, h1c(mm, -1)], ALU.mult, ALU.add)
            return last
        return f

    def emit_mix_gp(dst, tmcol0, mm, wait):
        def fa(e):
            last = None
            for kc in range(KC):
                last = e.tensor_scalar(
                    s[dst][:, 128 * kc:128 * (kc + 1)],
                    s["dmix"][:, 128 * kc:128 * (kc + 1)],
                    s["tmixS"][:, tmcol0 + kc:tmcol0 + kc + 1],
                    None, ALU.mult)
            return last
        tA = E("gp", fa, wait=wait)
        def fb(e):
            last = None
            for kc in range(KC):
                dv = s[dst][:, 128 * kc:128 * (kc + 1)]
                last = e.tensor_tensor(dv, dv, h1[kc][:, h1c(mm, -1)], ALU.add)
            return last
        return E("gp", fb, wait=[tA])

    tTap0 = None
    for mm in range(M):
        wl = [tEv[(mm, KC - 1)]]
        if mm >= 1:
            wl += [tXk[mm - 1], tXv[mm - 1]]
        else:
            wl.append(tSqA[M - 1])
        if mm == 1 and tTap0 is not None:
            wl.append(tTap0)
        tDm[mm] = E("dv", fdmix(mm), wait=wl)
        wl = [tDm[mm]] + ([tKt[(mm - 1, 1)]] if mm >= 1 else [])
        tXk[mm] = E("dv", fmixv("mixA", 8, mm), wait=wl)
        wl = [tDm[mm]] + ([tVt[(mm - 1, 1)]] if mm >= 1 else [])
        tXv[mm] = E("dv", fmixv("mixB", 16, mm), wait=wl)
        for hf in range(2):
            def fk(hf=hf):
                def f(e):
                    last = None
                    for kc in range(KC):
                        last = e.matmul(PRJ[hf][:, 0:512],
                                        s["mixA"][:, 128 * kc:128 * (kc + 1)],
                                        WS[0][:, kc * 1024 + 512 * hf:kc * 1024 + 512 * (hf + 1)],
                                        start=(kc == 0), stop=(kc == KC - 1))
                    return last
                return f
            wl = [tXk[mm], tW2A] + ([tKe[(mm - 1, hf)]] if mm >= 1 else [tModMM[hf]])
            tKt[(mm, hf)] = E("pe", fk(), wait=wl)
            wl = [tKt[(mm, hf)]] + ([tKk[mm - 1]] if mm >= 1 else [])
            tKe[(mm, hf)] = E("ac", (lambda hf=hf: (lambda e: e.copy(
                s["ktm"][:, 512 * hf:512 * (hf + 1)], PRJ[hf][:, 0:512])))(), wait=wl)
        for hf in range(2):
            def fv(hf=hf):
                def f(e):
                    last = None
                    for kc in range(KC):
                        last = e.matmul(PVV[:, 0:512],
                                        s["mixB"][:, 128 * kc:128 * (kc + 1)],
                                        WS[1][:, kc * 1024 + 512 * hf:kc * 1024 + 512 * (hf + 1)],
                                        start=(kc == 0), stop=(kc == KC - 1))
                    return last
                return f
            if hf == 0:
                wl = [tXv[mm], tW2A] + ([tVe[(mm - 1, 1)]] if mm >= 1 else [])
            else:
                wl = [tXv[mm], tVe[(mm, 0)]]
            tVt[(mm, hf)] = E("pe", fv(), wait=wl)
            wl = [tVt[(mm, hf)]]
            if mm >= 1:
                wl.append(tAx[(mm - 1, PAIRS - 1)])
            tVe[(mm, hf)] = E("ac", (lambda hf=hf: (lambda e: e.copy(
                s["vtm"][:, 512 * hf:512 * (hf + 1)], PVV[:, 0:512])))(), wait=wl)
        wl = [tKe[(mm, 0)], tKe[(mm, 1)], tWKM]
        if mm >= 1:
            wl.append(tAx[(mm - 1, PAIRS - 1)])
        tKk[mm] = E("dv", lambda e: e.tensor_tensor(s["ktm"], s["ktm"], s["WKM"],
                                                    ALU.mult), wait=wl)
        for cc in range(PAIRS):
            bank = 4 + cc % 2
            wl = [tKk[mm], tVe[(mm, 0)], tVe[(mm, 1)]]
            prev = (mm, cc - 1) if cc >= 1 else ((mm - 1, PAIRS - 1) if mm >= 1 else None)
            if prev is not None and prev in tSu:
                wl.append(tSu[prev])
            tAx[(mm, cc)] = E("pe", (lambda cc=cc: (lambda e: e.matmul(
                ATT[cc % 2], s["ktm"][:, 128 * cc:128 * (cc + 1)],
                s["vtm"][:, 128 * cc:128 * (cc + 1)], start=True, stop=True)))(),
                wait=wl)
            def fsu(cc=cc, bank=bank):
                def f(e):
                    e.scalar_tensor_tensor(
                        s["spair"][0:64, 64 * cc:64 * (cc + 1)],
                        s["spair"][0:64, 64 * cc:64 * (cc + 1)],
                        s["WSCOL"][0:64, cc:cc + 1], ATT[cc % 2][0:64, 0:64],
                        ALU.mult, ALU.add)
                    return e.scalar_tensor_tensor(
                        s["spair"][64:128, 64 * cc:64 * (cc + 1)],
                        s["spair"][64:128, 64 * cc:64 * (cc + 1)],
                        s["WSCOL"][64:128, cc:cc + 1], ATT[cc % 2][64:128, 64:128],
                        ALU.mult, ALU.add)
                return f
            tSu[(mm, cc)] = E("dv", fsu(), wait=[tAx[(mm, cc)], tSinit, tBr])
        if mm == 0 and "p2a0" in tapd:
            tTap0 = E("sy", lambda e: [
                e.dma_start(out=tapd["p2a0"][:, 0:D], in_=s["ktm"]),
                e.dma_start(out=tapd["p2a0"][:, D:2 * D], in_=s["vtm"]),
                e.dma_start(out=tapd["p2a0"][:, 2 * D:3 * D], in_=s["mixA"]),
                e.dma_start(out=tapd["p2a0"][:, 3 * D:4 * D], in_=s["dmix"]),
            ] + [e.dma_start(out=tapd["p2a0"][:, 4 * D + kc:4 * D + kc + 1],
                             in_=h1[kc][:, BNDC:BNDC + 1],
                             allow_slow_non_contiguous=True) for kc in range(KC)],
                wait=[tKk[0], tVe[(0, 1)], tXk[0], tDm[0]], dma=4 + KC)

    ccr = s["yacc"].bitcast(F32)[:, 1024:1536]
    tCCw = E("gp", lambda e: [e.dma_start(
        out=ccin.rearrange("a (p f) -> (a p) f", p=128), in_=s["spair"])],
        wait=[tSu[(M - 1, PAIRS - 1)]], dma=1)
    tCC = E("gp", lambda e: e.collective_compute(
        "AllGather", ALU.bypass, replica_groups=[[0, 1, 2, 3, 4, 5, 6, 7]],
        ins=[ccin], outs=[ccout]), wait=[tCCw])
    tSz = E("dv", lambda e: e.memset(s["spair"], 0.0), wait=[tCCw])
    tCI = tSz
    for r in range(8):
        tRd = E("gp", (lambda r=r: (lambda e: [e.dma_start(
            out=ccr, in_=ccout[r:r + 1, :].rearrange("a (p f) -> (a p) f", p=128))]))(),
            wait=[tCC, tCI, tAx[(M - 1, PAIRS - 1)]], dma=1)
        def fsin(r=r):
            def f(e):
                last = None
                for cc in range(PAIRS):
                    last = e.scalar_tensor_tensor(
                        s["spair"][:, 64 * cc:64 * (cc + 1)],
                        ccr[:, 64 * cc:64 * (cc + 1)],
                        s["C8P"][:, 8 * cc + r:8 * cc + r + 1],
                        s["spair"][:, 64 * cc:64 * (cc + 1)], ALU.mult, ALU.add)
                return last
            return f
        tCI = E("dv", fsin(), wait=[tRd, tC8e, tSz])
    tSb0 = E("ac", lambda e: e.copy(s["sbf"], s["spair"]), wait=[tCI])

    tw2b = []
    for kc in range(KC):
        tw2b.append(E("gp", fwk(2, kc * 1024, "WrT", 128 * kc),
                      wait=[tModMM[0], tModMM[1]], dma=1))
    for kc in range(KC):
        tw2b.append(E("gp", fwk(3, kc * 1024, "WgT", 128 * kc),
                      wait=[tModMM[0], tModMM[1]], dma=1))
    for kc in range(KC):
        tw2b.append(E("gp", fwk(4, kc * 1024, "WoT", 128 * kc), dma=1))
    tRtrL = E("gp", lambda e: [e.dma_start(out=s["rtrS"][:, 8 * kc:8 * (kc + 1)],
                                           in_=d["WrtrT"][128 * kc:128 * (kc + 1), :])
                               for kc in range(KC)], dma=KC)
    tW2B = tw2b[-1]

    tDm2 = {}; tXr = {}; tXk2 = {}; tXv2 = {}; tXg2 = {}
    tRt = {}; tRe = {}; tK2t = {}; tK2e = {}; tGt = {}; tGe = {}
    tV2t = {}; tV2e = {}; tTt = {}; tTe = {}; tKk2 = {}; tRb = {}
    tAtt = {}; tAw = {}; tOutg = {}; tYe = {}; tAx2 = {}; tSu2 = {}; tSbp = {}
    tSq = {}; tS1 = {}; tS2 = {}; tStE = {}; tSqS = {}; tU = {}; tQg = {}; tRst = {}
    tBc = {}; tGn = {}; tGna = {}; tGng = {}; tBcB = {}; tBcE = {}
    tWo = {}; tX2v = {}; tX2s = {}; tL2s = {}; tL2q = {}; tL2n = {}
    tTt2 = {}; tTe2 = {}; tRtr = {}; tX2A = {}
    tRo1 = {}; tRo2 = {}; tRo3 = {}; tCs1 = {}; tRo4 = {}
    tXL2 = {}
    tL2sqTok = {}

    tCz = E("dv", lambda e: e.memset(s["cnt"], 0.0), wait=[])

    for mm in range(M):
        xb = s["xb0"] if mm % 2 == 0 else s["xb1"]
        wl = [tH1tm[M - 1]] if mm < 2 else [tX2v[mm - 2]]
        tXL2[mm] = E("gp", (lambda mm=mm, xb=xb: (lambda e: [e.dma_start(
            out=xb, in_=d["xs"][1 + T * mm:1 + T * (mm + 1), :])]))(), wait=wl, dma=1)

        wl = [tH1done]
        if mm >= 1:
            wl += [tXr[mm - 1], tXk2[mm - 1], tXv2[mm - 1], tXg2[mm - 1],
                   tS2[mm - 1], tL2sqTok[mm - 1]]
        else:
            wl += [tXk[M - 1], tXv[M - 1]]
        tDm2[mm] = E("dv", fdmix(mm), wait=wl)
        wl = [tDm2[mm]] + ([tRt[(mm - 1, KC - 1)]] if mm >= 1 else [tKt[(M - 1, 1)]])
        tXr[mm] = E("dv", fmixv("mixA", 0, mm), wait=wl)
        for mc in range(KC):
            bank = 4 + mc % 2
            def fr(mc=mc, bank=bank):
                def f(e):
                    last = None
                    for kc in range(KC):
                        last = e.matmul(PRJ[bank - 4][:, 0:128],
                                        WS[2][:, kc * 1024 + 128 * mc:kc * 1024 + 128 * (mc + 1)],
                                        s["mixA"][:, 128 * kc:128 * (kc + 1)],
                                        start=(kc == 0), stop=(kc == KC - 1))
                    return last
                return f
            wl = [tXr[mm], tW2B]
            prev = (mm, mc - 2) if mc >= 2 else ((mm - 1, mc + KC - 2) if mm >= 1 else None)
            if prev is not None and prev in tRe:
                wl.append(tRe[prev])
            if mc < 2 and mm >= 1:
                wl += [tGe[(mm - 1, KC - 2 + mc)], tSu2[(mm - 1, PAIRS - 1)]]
            elif mc < 2:
                wl.append(tKe[(M - 1, mc)])
            tRt[(mm, mc)] = E("pe", fr(), wait=wl)
            wl = [tRt[(mm, mc)]]
            if mm >= 1:
                wl += [tAtt[(mm - 1, PAIRS - 1, 1)], tRb[mm - 1], tWo[(mm - 1, 1)]]
            tRe[(mm, mc)] = E("ac", (lambda mc=mc, bank=bank: (lambda e: e.copy(
                s["rfm"][:, 128 * mc:128 * (mc + 1)], PRJ[bank - 4][:, 0:128])))(), wait=wl)
        wl = [tDm2[mm]] + ([tK2t[(mm - 1, KC - 1)]] if mm >= 1 else [])
        tXk2[mm] = E("dv", fmixv("mixC", 8, mm), wait=wl)
        for mc in range(KC):
            bank = 4 + mc % 2
            def fk2(mc=mc, bank=bank):
                def f(e):
                    last = None
                    for kc in range(KC):
                        last = e.matmul(PRJ[bank - 4][:, 0:128],
                                        WS[0][:, kc * 1024 + 128 * mc:kc * 1024 + 128 * (mc + 1)],
                                        s["mixC"][:, 128 * kc:128 * (kc + 1)],
                                        start=(kc == 0), stop=(kc == KC - 1))
                    return last
                return f
            wl = [tXk2[mm]]
            if mc >= 2:
                wl.append(tK2e[(mm, mc - 2)])
            else:
                wl.append(tRe[(mm, KC - 2 + mc)])
            tK2t[(mm, mc)] = E("pe", fk2(), wait=wl)
            wl = [tK2t[(mm, mc)]]
            if mm >= 1:
                wl += [tAtt[(mm - 1, PAIRS - 1, 1)], tTt[(mm - 1, KC - 1)]]
            tK2e[(mm, mc)] = E("ac", (lambda mc=mc, bank=bank: (lambda e: e.copy(
                s["kfm"][:, 128 * mc:128 * (mc + 1)], PRJ[bank - 4][:, 0:128])))(), wait=wl)
        wl = [tDm2[mm]] + ([tV2t[(mm - 1, 1)]] if mm >= 1 else [tVt[(M - 1, 1)]])
        tXv2[mm] = emit_mix_gp("mixB", 16, mm, wl)
        for hf in range(2):
            bank = 4 + hf
            def fv2(hf=hf, bank=bank):
                def f(e):
                    last = None
                    for kc in range(KC):
                        last = e.matmul(PRJ[bank - 4][:, 0:512],
                                        s["mixB"][:, 128 * kc:128 * (kc + 1)],
                                        WS[1][:, kc * 1024 + 512 * hf:kc * 1024 + 512 * (hf + 1)],
                                        start=(kc == 0), stop=(kc == KC - 1))
                    return last
                return f
            wl = [tXv2[mm], tK2e[(mm, KC - 2 + hf)]]
            tV2t[(mm, hf)] = E("pe", fv2(), wait=wl)
            wl = [tV2t[(mm, hf)]]
            if mm >= 1:
                wl += [tOutg[(mm - 1, PAIRS - 1)], tAx2[(mm - 1, PAIRS - 1)]]
            else:
                wl.append(tAx[(M - 1, PAIRS - 1)])
            tV2e[(mm, hf)] = E("ac", (lambda hf=hf, bank=bank: (lambda e: e.copy(
                s["vtm"][:, 512 * hf:512 * (hf + 1)], PRJ[bank - 4][:, 0:512])))(), wait=wl)
        wl = [tDm2[mm]] + ([tGt[(mm - 1, KC - 1)]] if mm >= 1 else [])
        tXg2[mm] = emit_mix_gp("mixD", 24, mm, wl)
        for mc in range(KC):
            bank = 4 + mc % 2
            def fg(mc=mc, bank=bank):
                def f(e):
                    last = None
                    for kc in range(KC):
                        last = e.matmul(PRJ[bank - 4][:, 0:128],
                                        WS[3][:, kc * 1024 + 128 * mc:kc * 1024 + 128 * (mc + 1)],
                                        s["mixD"][:, 128 * kc:128 * (kc + 1)],
                                        start=(kc == 0), stop=(kc == KC - 1))
                    return last
                return f
            wl = [tXg2[mm], tW2B]
            if mc >= 2:
                wl.append(tGe[(mm, mc - 2)])
            else:
                wl.append(tV2e[(mm, mc)])
            tGt[(mm, mc)] = E("pe", fg(), wait=wl)
            wl = [tGt[(mm, mc)]]
            if mm >= 1:
                wl.append(tGng[(mm - 1, PAIRS - 1)])
            tGe[(mm, mc)] = E("ac", (lambda mc=mc, bank=bank: (lambda e: e.activation(
                s["gch"][:, 128 * mc:128 * (mc + 1)], PRJ[bank - 4][:, 0:128], AF.Silu)))(),
                wait=wl)
        for kc in range(KC):
            bank = kc % 2
            wl = [tK2e[(mm, kc)], ptTok[bank]]
            tTt[(mm, kc)] = E("pe", (lambda kc=kc, bank=bank: (lambda e: e.transpose(
                pt[bank], s["kfm"][:, 128 * kc:128 * (kc + 1)], s["IDNs"])))(), wait=wl)
            wl = [tTt[(mm, kc)]]
            if mm >= 1 and kc < 2:
                wl.append(tKk2[mm - 1])
            ptTok[bank] = tTe[(mm, kc)] = E("ac", (lambda kc=kc, bank=bank: (
                lambda e: e.copy(s["ktm"][:, 128 * kc:128 * (kc + 1)], pt[bank])))(),
                wait=wl)
        wl = [tTe[(mm, KC - 1)], tWKM]
        if mm >= 1:
            wl.append(tAx2[(mm - 1, PAIRS - 1)])
        else:
            wl.append(tAx[(M - 1, PAIRS - 1)])
        tKk2[mm] = E("dv", lambda e: e.tensor_tensor(s["ktm"], s["ktm"], s["WKM"],
                                                     ALU.mult), wait=wl)
        wl = [tRe[(mm, KC - 1)], tWBC]
        if mm >= 1:
            wl.append(tOutg[(mm - 1, PAIRS - 1)])
        tRb[mm] = E("dv", lambda e: e.tensor_tensor(s["rrb"], s["rfm"], s["WBCm"],
                                                    ALU.mult), wait=wl)

        for cc in range(PAIRS):
            for hs in range(2):
                bank = hs
                wl = [tK2e[(mm, cc)], tRe[(mm, cc)]]
                prevaw = (mm, cc, 0) if hs == 1 else (
                    (mm, cc - 1, 1) if cc >= 1 else
                    ((mm - 1, PAIRS - 1, 1) if mm >= 1 else None))
                if prevaw is not None and prevaw in tAw:
                    wl.append(tAw[prevaw])
                elif mm == 0 and cc == 0 and hs == 0:
                    wl.append(tSu[(M - 1, PAIRS - 1)])
                r0 = 64 * hs
                tAtt[(mm, cc, hs)] = E("pe", (lambda cc=cc, r0=r0, hs=hs: (
                    lambda e: e.matmul(ATT[hs],
                                       s["kfm"][r0:r0 + 64, 128 * cc:128 * (cc + 1)],
                                       s["rfm"][r0:r0 + 64, 128 * cc:128 * (cc + 1)],
                                       start=True, stop=True)))(), wait=wl)
                wl = [tAtt[(mm, cc, hs)], tWmat]
                if (mm, cc - 1) in tOutg:
                    wl.append(tOutg[(mm, cc - 1)])
                elif mm >= 1:
                    wl.append(tOutg[(mm - 1, PAIRS - 1)])
                if cc == 0 and mm >= 1:
                    wl.append(tSqS[mm - 1])
                tAw[(mm, cc, hs)] = E("dv", (lambda cc=cc, hs=hs, bank=bank: (
                    lambda e: e.tensor_tensor(
                        s["attw"][:, 128 * hs:128 * (hs + 1)], ATT[hs],
                        s["wmat"][:, 128 * (2 * cc + hs):128 * (2 * cc + hs + 1)],
                        ALU.mult)))(), wait=wl)
            wl = [tAw[(mm, cc, 0)], tAw[(mm, cc, 1)], tV2e[(mm, 0)], tV2e[(mm, 1)],
                  tRb[mm], tSb0]
            if (mm - 1, cc) in tSbp:
                wl.append(tSbp[(mm - 1, cc)])
            if (mm, cc - 1) in tYe:
                wl.append(tYe[(mm, cc - 1)])
            elif mm >= 1:
                wl.append(tRo4[mm - 1])
            def fout(cc=cc):
                def f(e):
                    e.matmul(POUT[0:64, :], s["vtm"][:, 128 * cc:128 * cc + 64],
                             s["attw"][:, 0:128], start=True, stop=False)
                    e.matmul(POUT[0:64, :], s["sbf"][0:64, 64 * cc:64 * (cc + 1)],
                             s["rrb"][0:64, 128 * cc:128 * (cc + 1)],
                             start=False, stop=True)
                    e.matmul(POUT[64:128, :],
                             s["vtm"][:, 128 * cc + 64:128 * (cc + 1)],
                             s["attw"][:, 128:256], start=True, stop=False,
                             tile_position=(0, 64))
                    return e.matmul(POUT[64:128, :],
                                    s["sbf"][64:128, 64 * cc:64 * (cc + 1)],
                                    s["rrb"][64:128, 128 * cc:128 * (cc + 1)],
                                    start=False, stop=True, tile_position=(64, 64))
                return f
            tOutg[(mm, cc)] = E("pe", fout(), wait=wl)
            wl = [tOutg[(mm, cc)]]
            if mm >= 1 and cc == 0:
                wl.append(tGn[(mm - 1, PAIRS - 1)])
            tYe[(mm, cc)] = E("ac", (lambda cc=cc: (lambda e: e.copy(
                s["ych"][:, 128 * cc:128 * (cc + 1)], POUT)))(), wait=wl)
            wl = [tKk2[mm], tV2e[(mm, 0)], tV2e[(mm, 1)]]
            if (mm, cc - 1) in tSu2:
                wl.append(tSu2[(mm, cc - 1)])
            elif mm >= 1:
                wl.append(tSu2[(mm - 1, PAIRS - 1)])
            else:
                wl.append(tSu[(M - 1, PAIRS - 1)])
            tAx2[(mm, cc)] = E("pe", (lambda cc=cc: (lambda e: e.matmul(
                PAX2, s["ktm"][:, 128 * cc:128 * (cc + 1)],
                s["vtm"][:, 128 * cc:128 * (cc + 1)], start=True, stop=True)))(),
                wait=wl)
            def fsu2(cc=cc):
                def f(e):
                    e.scalar_tensor_tensor(
                        s["spair"][0:64, 64 * cc:64 * (cc + 1)],
                        s["spair"][0:64, 64 * cc:64 * (cc + 1)],
                        s["WSCOL"][0:64, cc:cc + 1], PAX2[0:64, 0:64],
                        ALU.mult, ALU.add)
                    return e.scalar_tensor_tensor(
                        s["spair"][64:128, 64 * cc:64 * (cc + 1)],
                        s["spair"][64:128, 64 * cc:64 * (cc + 1)],
                        s["WSCOL"][64:128, cc:cc + 1], PAX2[64:128, 64:128],
                        ALU.mult, ALU.add)
                return f
            tSu2[(mm, cc)] = E("dv", fsu2(), wait=[tAx2[(mm, cc)], tCI])
            tSbp[(mm, cc)] = E("ac", (lambda cc=cc: (lambda e: e.copy(
                s["sbf"][:, 64 * cc:64 * (cc + 1)],
                s["spair"][:, 64 * cc:64 * (cc + 1)])))(),
                wait=[tSu2[(mm, cc)], tOutg[(mm, cc)]])

        wl = [tYe[(mm, PAIRS - 1)], tXr[mm], tXk2[mm], tXv2[mm], tXg2[mm]]
        tSq[mm] = E("ac", lambda e: e.activation(s["dmix"], s["ych"], AF.Square),
                    wait=wl)
        wl = [tYe[(mm, PAIRS - 1)], tLD]
        if mm >= 1:
            wl += [tStE[mm - 1], tBcE[mm - 1], tRo4[mm - 1]]
        def fs1():
            def f(e):
                last = None
                for cc in range(PAIRS):
                    last = e.matmul(PS1,
                                    s["SELSs"][:, 16 * cc:16 * (cc + 1)],
                                    s["ych"][:, 128 * cc:128 * (cc + 1)],
                                    start=(cc == 0), stop=(cc == PAIRS - 1))
                return last
            return f
        tS1[mm] = E("pe", fs1(), wait=wl)
        wl = [tSq[mm]]
        def fs2():
            def f(e):
                last = None
                for cc in range(PAIRS):
                    last = e.matmul(PS2,
                                    s["SELSs"][:, 16 * cc:16 * (cc + 1)],
                                    s["dmix"][:, 128 * cc:128 * (cc + 1)],
                                    start=(cc == 0), stop=(cc == PAIRS - 1))
                return last
            return f
        tS2[mm] = E("pe", fs2(), wait=wl)
        wl = [tS1[mm], tS2[mm]]
        if mm >= 1:
            wl.append(tRst[mm - 1])
        tStE[mm] = E("ac", lambda e: (
            e.copy(s["stats"][:, 0:128], PS1),
            e.copy(s["rstd"], PS2))[-1], wait=wl)
        wl = [tStE[mm]]
        if mm >= 1:
            wl.append(tGna[(mm - 1, PAIRS - 1)])
        tSqS[mm] = E("ac", lambda e: e.activation(s["sq1"], s["stats"][:, 0:128],
                                                  AF.Square), wait=wl)
        wl = [tSqS[mm]]
        if mm >= 1:
            wl.append(tBcB[mm - 1])
        tU[mm] = E("dv", lambda e: e.scalar_tensor_tensor(
            s["rstd"], s["rstd"], 64.0, s["sq1"],
            ALU.mult, ALU.subtract), wait=wl)
        tQg[mm] = E("ac", lambda e: e.activation(s["rstd"], s["rstd"], AF.Sqrt,
                                                 bias=s["epsc"][0:16, 1:2]),
                    wait=[tU[mm], tEps])
        tRs1 = E("dv", lambda e: e.reciprocal(s["rstd"], s["rstd"]),
                 wait=[tQg[mm]])
        tRs2 = E("dv", lambda e: (
            e.tensor_copy(s["rstdb"], s["rstd"]),
            e.tensor_tensor(s["s1rb"], s["stats"][:, 0:128], s["rstd"],
                            ALU.mult))[-1], wait=[tRs1])
        tRst[mm] = E("dv", lambda e: e.tensor_scalar(
            s["s1rb"], s["s1rb"], -0.015625, None, ALU.mult), wait=[tRs2])
        B1s = s["yacc"][:, 3072:4096]
        B0s = s["x2f"].bitcast(BF16)[:, 0:1024]
        def fbcb(srct, half):
            def f(e):
                last = None
                for q4 in range(4):
                    cc = 4 * half + q4
                    last = e.matmul(PB[6][:, 128 * q4:128 * (q4 + 1)],
                                    s["SELTS64s"][:, 128 * cc:128 * (cc + 1)],
                                    s[srct], start=True, stop=True)
                return last
            return f
        tprev = None
        for srct, dst in (("rstdb", B1s), ("s1rb", B0s)):
            for half in range(2):
                wl = [tRst[mm], tLD] + ([tprev] if tprev else [])
                if srct == "s1rb" and half == 0 and mm >= 1:
                    wl += [tL2n[mm - 1], tX2s[mm - 1]]
                tpe = E("pe", fbcb(srct, half), wait=wl)
                tprev = E("ac", (lambda dst=dst, half=half: (lambda e: e.copy(
                    dst[:, 512 * half:512 * (half + 1)],
                    PB[6][:, 0:512])))(), wait=[tpe])
                if srct == "s1rb" and half == 1:
                    tBcB[mm] = tpe
        tBcE[mm] = tprev
        for cc in range(PAIRS):
            wl = [tBcE[mm], tYe[(mm, cc)], tOutg[(mm, PAIRS - 1)]]
            if (mm, cc - 1) in tGna:
                wl.append(tGna[(mm, cc - 1)])
            tGnA = E("dv", (lambda cc=cc: (lambda e: e.scalar_tensor_tensor(
                s["zt"], B1s[:, 128 * cc:128 * (cc + 1)], 1.0,
                s["ych"][:, 128 * cc:128 * (cc + 1)],
                ALU.mult, ALU.mult)))(), wait=wl)
            tGn[(mm, cc)] = E("dv", (lambda cc=cc: (lambda e: e.tensor_tensor(
                s["zt"], s["zt"], B0s[:, 128 * cc:128 * (cc + 1)],
                ALU.add)))(), wait=[tGnA])
            wl = [tGn[(mm, cc)]]
            if (mm, cc - 1) in tGng:
                wl.append(tGng[(mm, cc - 1)])
            elif mm >= 1:
                wl.append(tWo[(mm - 1, 1)])
            tGna[(mm, cc)] = E("dv", (lambda cc=cc: (lambda e: e.tensor_scalar(
                s["rfm"][:, 128 * cc:128 * (cc + 1)], s["zt"],
                s["lnwS"][:, cc:cc + 1], s["lnbS"][:, cc:cc + 1],
                ALU.mult, ALU.add)))(),
                wait=wl)
            tGng[(mm, cc)] = E("gp", (lambda cc=cc: (lambda e: e.tensor_tensor(
                s["rfm"][:, 128 * cc:128 * (cc + 1)],
                s["rfm"][:, 128 * cc:128 * (cc + 1)],
                s["gch"][:, 128 * cc:128 * (cc + 1)], ALU.mult)))(),
                wait=[tGna[(mm, cc)], tGe[(mm, cc)]])
        def fwo(hf):
            def f(e):
                last = None
                for kc in range(KC):
                    last = e.matmul(PWO[hf][:, 0:512],
                                    s["rfm"][:, 128 * kc:128 * (kc + 1)],
                                    WS[4][:, kc * 1024 + 512 * hf:kc * 1024 + 512 * (hf + 1)],
                                    start=(kc == 0), stop=(kc == KC - 1))
                return last
            return f
        def fx2ah(hf):
            def f(e):
                return e.scalar_tensor_tensor(
                    s["x2f"][:, 512 * hf:512 * (hf + 1)], PWO[hf][:, 0:512],
                    1.0, s["modG"][:, 512 * hf:512 * (hf + 1)],
                    ALU.mult, ALU.mult)
            return f
        wl = [tGng[(mm, PAIRS - 1)], tBcE[mm], tGn[(mm, PAIRS - 1)]]
        if mm >= 1:
            wl.append(tX2A[mm - 1])
        tWo[(mm, 0)] = E("pe", fwo(0), wait=wl)
        wl = [tWo[(mm, 0)], tGA]
        if mm >= 1:
            wl.append(tL2n[mm - 1])
        tXa0 = E("dv", fx2ah(0), wait=wl)
        tWo[(mm, 1)] = E("pe", fwo(1), wait=[tXa0])
        tX2A[mm] = tX2a = E("dv", fx2ah(1), wait=[tWo[(mm, 1)]])
        wl = [tX2a, tXL2[mm]]
        def fx2b(xb=xb):
            def f(e):
                last = None
                for hf in range(2):
                    last = e.tensor_tensor(s["x2f"][:, 512 * hf:512 * (hf + 1)],
                                           s["x2f"][:, 512 * hf:512 * (hf + 1)],
                                           xb[:, 512 * hf:512 * (hf + 1)], ALU.add)
                return last
            return f
        tX2v[mm] = E("dv", fx2b(), wait=wl)
        tX2s[mm] = E("gp", (lambda mm=mm: (lambda e: [e.dma_start(
            out=x2s[T * mm:T * (mm + 1), :], in_=s["x2f"])]))(), wait=[tX2v[mm]], dma=1)
        tL2sqTok[mm] = tL2sq = E("ac", lambda e: e.activation(
            s["dmix"], s["x2f"], AF.Square, accum_out=s["lncol"][:, 2:3]),
            wait=[tX2v[mm], tS2[mm]])
        tL2s[mm] = E("dv", lambda e: e.reduce_sum(
            s["lncol"][:, 0:1], s["x2f"], mybir.AxisListType.X), wait=[tX2v[mm]])
        tL2ma = E("dv", lambda e: (
            e.tensor_scalar(s["lncol"][:, 12:13], s["lncol"][:, 0:1],
                            1.0 / 1024, None, ALU.mult),
            e.tensor_scalar(s["lncol"][:, 11:12], s["lncol"][:, 2:3],
                            1.0 / 1024, None, ALU.mult))[-1],
            wait=[tL2s[mm], tL2sq])
        tL2mb = E("dv", lambda e: e.tensor_tensor(
            s["lncol"][:, 1:2], s["lncol"][:, 12:13],
            s["lncol"][:, 12:13], ALU.mult), wait=[tL2ma])
        tL2m = E("dv", lambda e: e.tensor_tensor(
            s["lncol"][:, 13:14], s["lncol"][:, 11:12],
            s["lncol"][:, 1:2], ALU.subtract), wait=[tL2mb])
        tL2q[mm] = E("ac", lambda e: e.activation(s["lncol"][:, 14:15],
                                                  s["lncol"][:, 13:14], AF.Sqrt,
                                                  bias=s["epsc"][:, 0:1]),
                     wait=[tL2m, tEps])
        tL2r = E("dv", lambda e: e.reciprocal(s["lncol"][:, 15:16],
                                              s["lncol"][:, 14:15]),
                 wait=[tL2q[mm]])
        wl = [tL2r, tX2s[mm]]
        if mm >= 1:
            wl.append(tTt2[(mm - 1, KC - 1)])
        tL2n[mm] = E("dv", lambda e: e.tensor_scalar(
            s["h1tm"], s["x2f"], s["lncol"][:, 12:13],
            s["lncol"][:, 15:16], ALU.subtract, ALU.mult), wait=wl)
        for kc in range(KC):
            bank = kc % 2
            wl = [tL2n[mm], ptTok[bank]]
            tTt2[(mm, kc)] = E("pe", (lambda kc=kc, bank=bank: (lambda e: e.transpose(
                pt[bank], s["h1tm"][:, 128 * kc:128 * (kc + 1)], s["IDNs"])))(), wait=wl)
            ptTok[bank] = tTe2[(mm, kc)] = E("dv", (lambda mm=mm, kc=kc, bank=bank: (
                lambda e: e.tensor_scalar(h1[kc][:, h2c(mm)], pt[bank],
                                          s["CBc"][:, kc:kc + 1],
                                          s["SBc"][:, kc:kc + 1],
                                          ALU.mult, ALU.add)))(),
                wait=[tTt2[(mm, kc)], tCBc, tSBc])
        wl = [tTe2[(mm, KC - 1)], tRtrL, tStE[mm]]
        if mm >= 1:
            wl.append(tRo2[mm - 1])
        def frtr(mm=mm):
            def f(e):
                last = None
                for kc in range(KC):
                    last = e.matmul(PRT, h1[kc][:, h2c(mm)],
                                    s["rtrS"][:, 8 * kc:8 * (kc + 1)],
                                    start=(kc == 0), stop=(kc == KC - 1))
                return last
            return f
        tRtr[mm] = E("pe", frtr(), wait=wl)
        wl = [tRtr[mm]]
        if mm >= 1:
            wl.append(tRo2[mm - 1])
        tRo1a = E("dv", lambda e: e.reduce_max(
            s["rcol"][:, 0:1], PRT, mybir.AxisListType.X), wait=wl)
        tRo1[mm] = E("dv", lambda e: e.tensor_scalar(
            s["rcol"][:, 1:2], s["rcol"][:, 0:1], -1.0, None, ALU.mult),
            wait=[tRo1a])
        wl = [tRo1[mm]]
        if mm >= 1:
            wl.append(tRo4[mm - 1])
        tRo2[mm] = E("ac", lambda e: e.activation(
            s["gts"], PRT, AF.Exp, bias=s["rcol"][:, 1:2],
            accum_out=s["rcol"][:, 2:3]), wait=wl)
        tRo3a = E("dv", lambda e: (
            e.reciprocal(s["rcol"][:, 3:4], s["rcol"][:, 2:3]),
            e.tensor_tensor(s["pgt"], s["gts"], s["PERTs"], ALU.mult))[-1],
            wait=[tRo2[mm], tLD])
        tRo3b = E("dv", lambda e: (
            e.max(s["srt"], s["pgt"]),
            e.tensor_scalar(s["gts"], s["gts"], s["rcol"][:, 3:4], None,
                            ALU.mult))[-1], wait=[tRo3a])
        def fro3(mm=mm):
            def f(e):
                e.tensor_scalar(s["oh1st"][:, 8 * mm:8 * (mm + 1)], s["pgt"],
                                s["srt"][:, 0:1], None, ALU.is_equal)
                e.tensor_scalar(s["oh2st"][:, 8 * mm:8 * (mm + 1)], s["pgt"],
                                s["srt"][:, 1:2], None, ALU.is_equal)
                e.scalar_tensor_tensor(s["r8a"], s["pgt"], s["srt"][:, 0:1], s["gts"],
                                       ALU.is_equal, ALU.mult,
                                       accum_out=s["ggst"][:, 2 * mm:2 * mm + 1])
                return e.scalar_tensor_tensor(
                    s["r8a"], s["pgt"], s["srt"][:, 1:2], s["gts"],
                    ALU.is_equal, ALU.mult,
                    accum_out=s["ggst"][:, 2 * mm + 1:2 * mm + 2])
            return f
        tRo3c = E("dv", fro3(), wait=[tRo3b])
        tRo3[mm] = E("dv", (lambda mm=mm: (lambda e: e.tensor_tensor(
            s["ohsb"], s["oh1st"][:, 8 * mm:8 * (mm + 1)],
            s["oh2st"][:, 8 * mm:8 * (mm + 1)], ALU.add)))(), wait=[tRo3c])
        wl = [tRo3[mm], tStE[mm], tYe[(mm, PAIRS - 1)], tGn[(mm, PAIRS - 1)]]
        if mm >= 1:
            wl.append(tRo4[mm - 1])
        def fcs():
            def f(e):
                e.matmul(PCS1, s["LTs"], s["ohsb"], start=True, stop=True)
                return e.matmul(PCS2, s["ONESBs"], s["ohsb"],
                                start=True, stop=True)
            return f
        tCs1[mm] = E("pe", fcs(), wait=wl)
        tRo4a = E("dv", lambda e: e.tensor_tensor(
            s["r8a"], PCS1, s["cnt"], ALU.add), wait=[tCs1[mm], tCz])
        def fro4(mm=mm):
            def f(e):
                e.scalar_tensor_tensor(s["gts"], s["pgt"], s["srt"][:, 0:1],
                                       s["r8a"], ALU.is_equal, ALU.mult,
                                       accum_out=s["pgst"][:, 2 * mm:2 * mm + 1])
                e.scalar_tensor_tensor(s["pgt"], s["pgt"], s["srt"][:, 1:2],
                                       s["r8a"], ALU.is_equal, ALU.mult,
                                       accum_out=s["pgst"][:, 2 * mm + 1:2 * mm + 2])
                return e.tensor_tensor(s["cnt"], s["cnt"], PCS2, ALU.add)
            return f
        tRo4[mm] = E("dv", fro4(), wait=[tRo4a])

    tCC2w = E("gp", lambda e: [e.dma_start(out=ccin2, in_=s["cnt"][0:1, :])],
              wait=[tRo4[M - 1]], dma=1)
    tCC2 = E("gp", lambda e: e.collective_compute(
        "AllGather", ALU.bypass, replica_groups=[[0, 1, 2, 3, 4, 5, 6, 7]],
        ins=[ccin2], outs=[ccout2]), wait=[tCC2w])
    tCntR = E("gp", lambda e: [e.dma_start(out=s["cnts"], in_=ccout2)],
              wait=[tCC2], dma=1)
    tBase = E("pe", lambda e: e.matmul(PBASE, s["pmaskS"], s["cnts"],
                                       start=True, stop=True),
              wait=[tCntR, tSu2[(M - 1, PAIRS - 1)], tLD])
    tBaseE = E("ac", lambda e: e.copy(s["basebc"], PBASE), wait=[tBase])

    tCmb = {}
    for mm in range(M):
        def fca(mm=mm):
            def f(e):
                e.scalar_tensor_tensor(s["r8a"], s["oh1st"][:, 8 * mm:8 * (mm + 1)],
                                       1.0, s["basebc"], ALU.mult, ALU.mult,
                                       accum_out=s["rcol"][:, 4:5])
                return e.scalar_tensor_tensor(
                    s["gts"], s["oh2st"][:, 8 * mm:8 * (mm + 1)],
                    1.0, s["basebc"], ALU.mult, ALU.mult,
                    accum_out=s["rcol"][:, 5:6])
            return f
        ta_ = E("dv", fca(), wait=[tBaseE, tModB])
        ta2_ = E("dv", (lambda mm=mm: (lambda e: (
            e.tensor_scalar(s["rcol"][:, 6:7], s["rcol"][:, 4:5],
                            s["pgst"][:, 2 * mm:2 * mm + 1], CAP, ALU.add,
                            ALU.is_le),
            e.tensor_scalar(s["rcol"][:, 7:8], s["rcol"][:, 5:6],
                            s["pgst"][:, 2 * mm + 1:2 * mm + 2], CAP, ALU.add,
                            ALU.is_le))[-1]))(), wait=[ta_])
        ta3_ = E("dv", (lambda mm=mm: (lambda e: (
            e.tensor_scalar(s["rcol"][:, 9:10], s["rcol"][:, 7:8],
                            s["ggst"][:, 2 * mm + 1:2 * mm + 2], None, ALU.mult),
            e.tensor_scalar(s["rcol"][:, 8:9], s["rcol"][:, 6:7],
                            s["ggst"][:, 2 * mm:2 * mm + 1], None,
                            ALU.mult))[-1]))(), wait=[ta2_])
        tb1_ = E("dv", lambda e: e.tensor_scalar(
            s["rcol"][:, 10:11], s["rcol"][:, 8:9],
            s["rcol"][:, 9:10], 1e-6, ALU.add, ALU.add), wait=[ta3_])
        tb_ = E("dv", lambda e: e.reciprocal(
            s["rcol"][:, 11:12], s["rcol"][:, 10:11]), wait=[tb1_])
        tc_ = E("dv", lambda e: (
            e.tensor_scalar(s["rcol"][:, 12:13], s["rcol"][:, 8:9],
                            s["rcol"][:, 11:12], None, ALU.mult),
            e.tensor_scalar(s["rcol"][:, 13:14], s["rcol"][:, 9:10],
                            s["rcol"][:, 11:12], None, ALU.mult))[-1],
            wait=[tb_])
        tcd1 = E("dv", (lambda mm=mm: (lambda e: e.tensor_scalar(
            s["r8a"], s["oh1st"][:, 8 * mm:8 * (mm + 1)],
            s["rcol"][:, 12:13], None, ALU.mult)))(), wait=[tc_])
        tCmb[mm] = E("dv", (lambda mm=mm: (lambda e: e.scalar_tensor_tensor(
            s["combine"][:, 8 * mm:8 * (mm + 1)],
            s["oh2st"][:, 8 * mm:8 * (mm + 1)], s["rcol"][:, 13:14],
            s["r8a"], ALU.mult, ALU.add)))(), wait=[tcd1])

    tPEend = tRtr[M - 1]
    tg5w = []
    for kc in range(KC):
        slot = 0 if kc < 4 else 1
        col0 = (kc % 4) * 2048
        tg5w.append(E("gp", (lambda slot=slot, col0=col0, kc=kc: (lambda e: [e.dma_start(
            out=WS[slot][:, col0:col0 + 2048],
            in_=d["WgateT"][128 * kc:128 * (kc + 1), :])]))(),
            wait=[tPEend], dma=1))
    tG5W = tg5w[-1]
    tGB = E("gp", lambda e: [e.dma_start(
        out=s["modG"], in_=dram_ap(4096 + 1024, [[0, 128], [1, D]]))],
        wait=[tModB, tX2v[M - 1]], dma=1)

    tG5t = {}; tG5e = {}; tDn = {}; tYa = {}; tX5 = {}; tFo = {}; tOutD = {}
    wdidx = 0
    wdslotTok = {}
    for q in range(NE):
        for hk in range(HKC):
            gi = q * HKC + hk
            bank = 4 + gi % 2
            wl = [tG5W, tTe2[(4 * q + 3, KC - 1)]]
            if gi == 0:
                wl += [tX2v[M - 1], tSu2[(M - 1, PAIRS - 1)]]
            prev = gi - 2
            if prev >= 0:
                wl.append(tG5e[(prev // HKC, prev % HKC)])
            def fg5(q=q, hk=hk, bank=bank):
                def f(e):
                    last = None
                    for kc in range(KC):
                        wsl = WS[0] if kc < 4 else WS[1]
                        last = e.matmul(
                            PRJ[bank - 4][:, 0:512],
                            wsl[:, (kc % 4) * 2048 + 128 * hk:(kc % 4) * 2048 + 128 * (hk + 1)],
                            h1[kc][:, ET * q:ET * (q + 1)],
                            start=(kc == 0), stop=(kc == KC - 1))
                    return last
                return f
            tG5t[(q, hk)] = E("pe", fg5(), wait=wl)
            wl = [tG5t[(q, hk)]]
            if q == 0 and hk < 2:
                wl.append(tWo[(M - 1, 1)])
            if q >= 1:
                wl.append(tDn[(q - 1, EXP - 1, 1, 3)])
            tG5e[(q, hk)] = E("ac", (lambda hk=hk, bank=bank: (
                lambda e: e.activation(WS[4][:, 512 * hk:512 * (hk + 1)],
                                       PRJ[bank - 4][:, 0:512], AF.Silu)))(), wait=wl)
        for ex in range(EXP):
            for oh in range(2):
                slot = WS[2] if wdidx % 2 == 0 else WS[3]
                wl = [tPEend]
                if wdidx >= 2:
                    wl.append(wdslotTok[wdidx - 2])
                def fwd(slot=slot, ex=ex, oh=oh):
                    def f(e):
                        res = []
                        for part in range(4):
                            src = d["WdT"][ex, :, 512 * oh:512 * (oh + 1)] \
                                .rearrange("(o p) c -> p o c", p=128)[:, 4 * part:4 * (part + 1), :]
                            dst = slot[:, 2048 * part:2048 * (part + 1)] \
                                .rearrange("p (o c) -> p o c", c=512)
                            res.append(e.dma_start(out=dst, in_=src))
                        return res
                    return f
                tWdTok = E("gp", fwd(), wait=wl, dma=4)
                for t8 in range(4):
                    bank = (t8 + 4 * (oh + 2 * ex)) % 4
                    key = (q, ex, oh, t8)
                    wl = [tWdTok, tG5e[(q, HKC - 1)]]
                    pidx = (ex * 8 + oh * 4 + t8) - 4
                    if pidx >= 0:
                        pex, prem = divmod(pidx, 8)
                        poh, pt8 = divmod(prem, 4)
                        wl.append(tYa[(q, pex, poh, pt8)])
                    elif q >= 1:
                        wl.append(tYa[(q - 1, EXP - 1, 1, t8)])
                    else:
                        wl += [tRo4[M - 1], tBaseE]
                    def fdn(oh=oh, t8=t8, bank=bank, slot=slot):
                        def f(e):
                            last = None
                            for hk in range(HKC):
                                last = e.matmul(
                                    PB[bank][:, 0:512],
                                    WS[4][:, 512 * hk + 128 * t8:512 * hk + 128 * (t8 + 1)],
                                    slot[:, 512 * hk:512 * (hk + 1)],
                                    start=(hk == 0), stop=(hk == HKC - 1))
                            return last
                        return f
                    tDn[key] = E("pe", fdn(), wait=wl)
                    mm = 4 * q + t8
                    wl = [tDn[key], tCmb[mm]]
                    if q >= 1 and ex == 0 and oh == 0:
                        wl.append(tFo[4 * (q - 1) + t8])
                    def fya(ex=ex, oh=oh, t8=t8, bank=bank, mm=mm):
                        def f(e):
                            dstc = slice(1024 * t8 + 512 * oh,
                                         1024 * t8 + 512 * (oh + 1))
                            if ex == 0:
                                return e.tensor_scalar(
                                    s["yacc"][:, dstc], PB[bank][:, 0:512],
                                    s["combine"][:, 8 * mm + ex:8 * mm + ex + 1],
                                    None, ALU.mult)
                            return e.scalar_tensor_tensor(
                                s["yacc"][:, dstc], PB[bank][:, 0:512],
                                s["combine"][:, 8 * mm + ex:8 * mm + ex + 1],
                                s["yacc"][:, dstc], ALU.mult, ALU.add)
                        return f
                    tYa[key] = E("dv", fya(), wait=wl)
                wdslotTok[wdidx] = tDn[(q, ex, oh, 3)]
                wdidx += 1
        for t8 in range(4):
            mm = 4 * q + t8
            xb = s["xb0"] if mm % 2 == 0 else s["xb1"]
            wl = [tX2s[mm]]
            if mm >= 2:
                wl.append(tFo.get(mm - 2))
            else:
                wl.append(tX2v[M - 1])
            tX5[mm] = E("gp", (lambda mm=mm, xb=xb: (lambda e: [e.dma_start(
                out=xb, in_=x2s[T * mm:T * (mm + 1), :])]))(), wait=wl, dma=1)
            wl = [tYa[(q, EXP - 1, 0, t8)], tYa[(q, EXP - 1, 1, t8)], tX5[mm], tGB,
                  tL2s[M - 1]]
            if mm >= 1:
                wl.append(tOutD.get(mm - 1))
            tFoA = E("dv", (lambda t8=t8: (lambda e: e.tensor_tensor(
                s["x2f"], s["yacc"][:, 1024 * t8:1024 * (t8 + 1)],
                s["modG"], ALU.mult)))(), wait=wl)
            tFo[mm] = E("dv", (lambda xb=xb: (lambda e: e.tensor_tensor(
                s["x2f"], s["x2f"], xb, ALU.add)))(), wait=[tFoA])
            tOutD[mm] = E("sy", (lambda mm=mm: (lambda e: [e.dma_start(
                out=out[T * mm:T * (mm + 1), :], in_=s["x2f"])]))(),
                wait=[tFo[mm]], dma=1)

    if "h1" in tapd:
        E("sy", lambda e: [e.dma_start(out=tapd["h1"][:, kc * H1W:(kc + 1) * H1W],
                                       in_=h1[kc]) for kc in range(KC)],
          wait=[tH1done], dma=KC)
    if "dk" in tapd:
        E("sy", lambda e: [e.dma_start(out=tapd["dk"][:, 0:2048], in_=s["wmat"]),
                           e.dma_start(out=tapd["dk"][:, 2048:3072], in_=s["WBCm"]),
                           e.dma_start(out=tapd["dk"][:, 3072:4096], in_=s["WKM"])],
          wait=[tWmat, tWBC, tWKM], dma=3)
    if "dk2" in tapd:
        E("sy", lambda e: [e.dma_start(out=tapd["dk2"][:, 0:8], in_=s["LNDCOL"]),
                           e.dma_start(out=tapd["dk2"][:, 8:16], in_=s["WSCOL"]),
                           e.dma_start(out=tapd["dk2"][:, 16:80], in_=s["C8P"]),
                           e.dma_start(out=tapd["dk2"][:, 80:96], in_=s["lndbc"]),
                           e.dma_start(out=tapd["dk2"][:, 96:112], in_=s["UCOL"]),
                           e.dma_start(out=tapd["dk2"][:, 112:120], in_=s["CAc"]),
                           e.dma_start(out=tapd["dk2"][:, 120:128], in_=s["SAc"]),
                           e.dma_start(out=tapd["dk2"][:, 128:136], in_=s["CBc"]),
                           e.dma_start(out=tapd["dk2"][:, 136:144], in_=s["SBc"]),
                           e.dma_start(out=tapd["dk2"][:, 144:152], in_=s["basebc"])],
          wait=[tBr, tC8e, tCAc, tSAc, tCBc, tSBc, tBaseE], dma=10)
    if "st" in tapd:
        E("sy", lambda e: [e.dma_start(out=tapd["st"], in_=s["spair"])],
          wait=[tCI], dma=1)
    if "gn" in tapd:
        E("sy", lambda e: [e.dma_start(out=tapd["gn"], in_=s["rfm"])],
          wait=[tGng[(M - 1, PAIRS - 1)]], dma=1)
    if "gs" in tapd:
        E("sy", lambda e: [e.dma_start(out=tapd["gs"][:, 0:128],
                                       in_=s["stats"][:, 0:128]),
                           e.dma_start(out=tapd["gs"][:, 128:256],
                                       in_=s["rstd"])],
          wait=[tRst[M - 1]], dma=2)
    if "ccg" in tapd:
        E("sy", lambda e: [e.dma_start(out=tapd["ccg"][0:8, :], in_=ccout),
                           e.dma_start(out=tapd["ccg"][8:9, :], in_=ccin)],
          wait=[tCC], dma=2)
    if "rt" in tapd:
        E("sy", lambda e: [e.dma_start(out=tapd["rt"], in_=s["pgst"])],
          wait=[tRo4[M - 1]], dma=1)
    if "cmb" in tapd:
        E("sy", lambda e: [e.dma_start(out=tapd["cmb"], in_=s["combine"])],
          wait=[tCmb[M - 1]], dma=1)
    if "x2" in tapd:
        E("sy", lambda e: [e.dma_start(out=tapd["x2"], in_=x2s)],
          wait=[tX2s[M - 1]], dma=1)
    if "hid" in tapd:
        E("sy", lambda e: [e.dma_start(out=tapd["hid"], in_=WS[4])],
          wait=[tG5e[(0, HKC - 1)]], dma=1)
    if "y" in tapd:
        E("sy", lambda e: [e.dma_start(out=tapd["y"], in_=s["ych"])],
          wait=[tYe[(M - 1, PAIRS - 1)]], dma=1)
    if "gg" in tapd:
        E("sy", lambda e: [e.dma_start(out=tapd["gg"], in_=s["gch"])],
          wait=[tGe[(M - 1, KC - 1)]], dma=1)

    sems = {k: nc.semaphore("sem_" + k).__enter__() for k in Prog.ENGS}

    def replay(ename):
        def go(e):
            cnt = 0
            for eng, waits, dma, fn in pr.steps:
                if eng != ename:
                    continue
                for wn, wv in waits:
                    e.wait_ge(sems[wn], wv)
                res = fn(e)
                if dma:
                    res = res if isinstance(res, (list, tuple)) else [res]
                    assert len(res) == dma, (ename, len(res), dma)
                    for r in res:
                        r.then_inc(sems[ename], 16)
                    cnt += 16 * dma
                else:
                    res.then_inc(sems[ename], 1)
                    cnt += 1
            assert cnt == pr.cnt[ename], (ename, cnt, pr.cnt[ename])
            if ename == "sy":
                e.wait_ge(sems["sy"], pr.cnt["sy"])
        return go

    with nc.Block() as block:
        block.sync(replay("sy"))
        block.gpsimd(replay("gp"))
        block.scalar(replay("ac"))
        block.vector(replay("dv"))
        block.tensor(replay("pe"))

    return nc


_NC_CACHE = {}


def kernel(**inputs):
    key = tuple(DEBUG_TAPS)
    if key not in _NC_CACHE:
        _NC_CACHE[key] = build(key)
    nc = _NC_CACHE[key]
    consts = _host_consts()
    w = _shared_weights(inputs)
    in_maps = [_prep_core_inputs(c, inputs, consts, w) for c in range(8)]
    res = run_bass_kernel_spmd(nc, in_maps, core_ids=list(range(8)))
    kernel.last_results = res.results
    full = np.empty((2, 16384, D), dtype=np.float32)
    for c in range(8):
        b, j = c // 4, c % 4
        full[b, j * TOK:(j + 1) * TOK] = res.results[c]["out"]
    return full


# version 04
# speedup vs baseline: 1.0672x; 1.0672x over previous
import numpy as np
import ml_dtypes

import concourse.bass as bass
import concourse.mybir as mybir
from concourse.bass_utils import run_bass_kernel_spmd

F32 = mybir.dt.float32
BF16 = mybir.dt.bfloat16
AF = mybir.ActivationFunctionType
ALU = mybir.AluOpType
BF16NP = ml_dtypes.bfloat16

D = 1024
KC = 8
T = 128
M = 32
TOK = 4096
PAIRS = 8
DH = 2048
HKC = 16
EXP = 8
NE = 8
ET = 512
CAP = 5120.0
EPS1 = 1e-5
CGN = 4096.0 * 64e-5
BTC = 129
BNDC = 128
H1W = 128 + 1 + TOK

DEBUG_TAPS = []


def _host_consts():
    c = {}
    i = np.arange(128, dtype=np.float32)
    c["IDN"] = np.eye(128, dtype=BF16NP)
    c["EYE"] = np.eye(128, dtype=np.float32)
    c["M1"] = np.maximum(i[None, :] - i[:, None] - 1.0, 0.0).astype(np.float32)
    c["TRI"] = (i[None, :] > i[:, None]).astype(BF16NP)
    c["LT"] = (i[:, None] <= i[None, :]).astype(BF16NP)
    c["ONESB"] = np.ones((128, 128), dtype=BF16NP)
    c["REVJ"] = np.ascontiguousarray((127.0 - i)[:, None])
    c["IOTAI"] = np.tile(i[None, :], (128, 1)).astype(np.float32)
    c["PERT"] = np.ascontiguousarray(
        (1.0 - 1e-6 * np.arange(8, dtype=np.float32))[None, :].repeat(128, 0))
    sels = np.zeros((128, 128), dtype=np.float32)
    for cc in range(8):
        sels[0:64, 16 * cc + 2 * cc] = 1.0
        sels[64:128, 16 * cc + 2 * cc + 1] = 1.0
    c["SELS"] = sels.astype(BF16NP)
    selts = np.zeros((16, 8 * 128), dtype=np.float32)
    for cc in range(8):
        selts[2 * cc, 128 * cc:128 * cc + 64] = 1.0
        selts[2 * cc + 1, 128 * cc + 64:128 * cc + 128] = 1.0
    c["SELTS64"] = (64.0 * selts).astype(BF16NP)
    return c


def _shared_weights(inputs):
    w = {}
    w["WrT"] = np.ascontiguousarray(np.asarray(inputs["L0_attn_W_r"], np.float32).T)
    w["WkT"] = np.ascontiguousarray(np.asarray(inputs["L0_attn_W_k"], np.float32).T)
    w["WvT"] = np.ascontiguousarray(np.asarray(inputs["L0_attn_W_v"], np.float32).T)
    w["WgT"] = np.ascontiguousarray(np.asarray(inputs["L0_attn_W_g"], np.float32).T)
    w["WoT"] = np.ascontiguousarray(np.asarray(inputs["L0_attn_W_o"], np.float32).T)
    w["WgateT"] = np.ascontiguousarray(np.asarray(inputs["L0_ffn_w_gate"], np.float32).T)
    w["WdT"] = np.ascontiguousarray(
        np.asarray(inputs["L0_ffn_w_down"], np.float32).transpose(0, 2, 1))
    w["WrtrT"] = np.ascontiguousarray(np.asarray(inputs["L0_ffn_w_router"], np.float32).T)
    w["wmod"] = np.ascontiguousarray(np.asarray(inputs["L0_wmod"], np.float32))
    tmix = np.empty((128, 32), dtype=np.float32)
    for vi, key in enumerate(["time_mix_r", "time_mix_k", "time_mix_v", "time_mix_g"]):
        v = np.asarray(inputs[f"L0_attn_{key}"], np.float32).reshape(D)
        tmix[:, vi * 8:(vi + 1) * 8] = v.reshape(KC, 128).T
    w["tmix"] = tmix
    w["tdrow"] = np.asarray(inputs["L0_attn_time_decay"], np.float32).reshape(1, 16).copy()
    w["tfrow"] = np.asarray(inputs["L0_attn_time_faaaa"], np.float32).reshape(1, 16).copy()
    w["lnw"] = np.ascontiguousarray(
        np.asarray(inputs["L0_attn_ln_w"], np.float32).reshape(KC, 128).T)
    w["lnb"] = np.ascontiguousarray(
        np.asarray(inputs["L0_attn_ln_b"], np.float32).reshape(KC, 128).T)
    return w


def _prep_core_inputs(cid, inputs, consts, w):
    b, j = cid // 4, cid % 4
    x = np.asarray(inputs["L0_attn_xq"], np.float32)
    xs = np.empty((1 + TOK, D), dtype=np.float32)
    t0 = j * TOK
    xs[0] = 0.0 if j == 0 else x[b, t0 - 1]
    xs[1:] = x[b, t0:t0 + TOK]
    m = dict(w)
    m["xs"] = xs
    m["bnd"] = np.full((128, 1), 0.0 if j == 0 else 1.0, dtype=np.float32)
    pm = np.zeros((8, 128), dtype=np.float32)
    pm[np.arange(8) < cid, :] = 1.0
    m["pmask"] = pm
    spow = np.zeros((16, 8), dtype=np.float32)
    smask = np.zeros((16, 8), dtype=np.float32)
    for r in range(8):
        br, jr = r // 4, r % 4
        if br == b and jr < j:
            spow[:, r] = 4096.0 * (j - 1 - jr)
            smask[:, r] = 1.0
    m["spow"] = spow
    m["smask"] = smask
    m["crow"] = np.ascontiguousarray(np.asarray(inputs["L0_c"], np.float32)[b:b + 1, :])
    m.update(consts)
    return m


class Prog:
    ENGS = ("pe", "dv", "ac", "gp", "sy")

    def __init__(self):
        self.steps = []
        self.cnt = {k: 0 for k in self.ENGS}

    def emit(self, eng, fn, wait=(), dma=0):
        wl = {}
        for tkn in wait:
            if tkn is None:
                continue
            en, v = tkn
            if v > 0 and (en not in wl or wl[en] < v):
                wl[en] = v
        self.steps.append((eng, sorted(wl.items()), dma, fn))
        self.cnt[eng] += 16 * dma if dma else 1
        return (eng, self.cnt[eng])


def build(taps=()):
    taps = set(taps)
    nc = bass.Bass("TRN2", target_bir_lowering=False)

    d = {}
    def din(name, shape, dt=F32):
        d[name] = nc.dram_tensor(name, shape, dt, kind="ExternalInput").ap()
    din("xs", [1 + TOK, D])
    din("bnd", [128, 1]); din("pmask", [8, 128])
    din("spow", [16, 8]); din("smask", [16, 8])
    for w in ("WrT", "WkT", "WvT", "WgT", "WoT"):
        din(w, [D, D])
    din("WgateT", [D, DH]); din("WdT", [EXP, DH, D]); din("WrtrT", [D, EXP])
    din("wmod", [D, 6 * D]); din("crow", [1, D])
    din("tmix", [128, 32]); din("tdrow", [1, 16]); din("tfrow", [1, 16])
    din("lnw", [128, 8]); din("lnb", [128, 8])
    din("IDN", [128, 128], BF16); din("EYE", [128, 128]); din("M1", [128, 128])
    din("TRI", [128, 128], BF16); din("IOTAI", [128, 128]); din("REVJ", [128, 1])
    din("LT", [128, 128], BF16); din("ONESB", [128, 128], BF16)
    din("PERT", [128, 8]); din("SELS", [128, 128], BF16)
    din("SELTS64", [16, 8 * 128], BF16)

    out = nc.dram_tensor("out", [TOK, D], F32, kind="ExternalOutput").ap()
    x2s = nc.dram_tensor("x2s", [TOK, D], BF16, kind="Internal").ap()
    bscr = nc.dram_tensor("bscr", [1, 12288], F32, kind="Internal").ap()
    ccin = nc.dram_tensor("ccin", [1, 128 * 512], F32, kind="Internal").ap()
    ccout = nc.dram_tensor("ccout", [8, 128 * 512], F32, kind="Internal").ap()
    ccin2 = nc.dram_tensor("ccin2", [1, 8], F32, kind="Internal").ap()
    ccout2 = nc.dram_tensor("ccout2", [8, 8], F32, kind="Internal").ap()

    tap_shapes = {
        "h1": ([128, KC * H1W], BF16),
        "dk": ([128, 16 * 128 + 8 * 128 + D], BF16),
        "dk2": ([128, 160], F32),
        "st": ([128, 512], F32),
        "y": ([128, D], BF16),
        "gg": ([128, D], BF16),
        "x2": ([TOK, D], BF16),
        "rt": ([128, 32 * 2], F32),
        "cmb": ([128, 32 * 8], F32),
        "hid": ([128, 8192], BF16),
        "ccg": ([9, 128 * 512], F32),
        "gn": ([128, D], BF16),
        "gs": ([16, 256], F32),
        "p2a0": ([128, 4 * D + 8], BF16),
        "bdg": ([128, 32], F32),
        "bdh": ([128, D + 8], BF16),
    }
    tapd = {}
    for tname in taps:
        shape, dt = tap_shapes[tname]
        tapd[tname] = nc.dram_tensor("tap_" + tname, shape, dt, kind="ExternalOutput").ap()

    s = {}
    def sba(name, shape, dt):
        s[name] = nc.alloc_sbuf_tensor(name, shape, dt).ap()
        return s[name]

    h1 = [sba(f"h1_{kc}", [128, H1W], BF16) for kc in range(KC)]
    WS = [sba(f"ws{i}", [128, 8192], BF16) for i in range(5)]
    sba("yacc", [128, 4 * 1024], BF16)
    sba("spair", [128, 512], F32)
    sba("sbf", [128, 512], BF16)
    sba("wmat", [128, 16 * 128], BF16)
    sba("WBCm", [128, 8 * 128], BF16)
    sba("WKM", [128, D], BF16)
    sba("lndbc", [128, 16], F32)
    sba("LNDCOL", [128, 8], F32)
    sba("WSCOL", [128, 8], F32)
    sba("UCOL", [128, 16], F32)
    sba("lrow", [1, 64], F32)
    sba("lcol", [16, 4], F32)
    sba("C8", [16, 8], F32)
    sba("C8P", [128, 64], F32)
    sba("modfm", [128, 48], F32)
    sba("combine", [128, 32 * 8], F32)
    sba("CAb", [128, 8], F32); sba("SAb", [128, 8], F32)
    sba("modG", [128, D], BF16)
    sba("scb", [128, 8], BF16)
    sba("tmixS", [128, 32], F32)
    sba("lnwS", [128, 8], F32); sba("lnbS", [128, 8], F32)
    sba("tdS", [1, 16], F32); sba("tfS", [1, 16], F32)
    sba("bndS", [128, 1], F32)
    sba("pmaskS", [8, 128], F32)
    sba("spowS", [16, 8], F32); sba("smaskS", [16, 8], F32)
    sba("rtrS", [128, 64], BF16)
    for nm, dt in [("IDNs", BF16),
                   ("LTs", BF16), ("ONESBs", BF16), ("SELSs", BF16)]:
        sba(nm, [128, 128], dt)
    sba("REVJs", [128, 1], F32)
    sba("PERTs", [128, 8], F32)
    sba("SELTS64s", [16, 8 * 128], BF16)
    sba("xb0", [128, D], BF16); sba("xb1", [128, D], BF16)
    sba("h1tm", [128, D], BF16)
    sba("dmix", [128, D], BF16)
    sba("mixA", [128, D], BF16); sba("mixB", [128, D], BF16)
    sba("rfm", [128, D], BF16)
    sba("kfm", [128, D], BF16)
    sba("ktm", [128, D], BF16)
    sba("vtm", [128, D], BF16)
    sba("rrb", [128, D], BF16)
    sba("ych", [128, D], BF16)
    sba("gch", [128, D], BF16)
    sba("zt", [128, 128], F32)
    sba("stats", [16, 128], F32)
    sba("rstd", [16, 128], F32)
    sba("rstdb", [16, 128], BF16); sba("s1rb", [16, 128], BF16)
    sba("x2f", [128, D], F32)
    sba("lncol", [128, 16], F32)
    sba("oh1st", [128, 32 * 8], BF16); sba("oh2st", [128, 32 * 8], BF16)
    sba("pgst", [128, 32 * 2], F32)
    sba("ggst", [128, 32 * 2], F32)
    sba("ohsb", [128, 8], BF16)
    sba("gts", [128, 8], F32); sba("pgt", [128, 8], F32); sba("srt", [128, 8], F32)
    sba("r8a", [128, 8], F32)
    sba("cnt", [128, 8], F32)
    sba("basebc", [128, 8], F32)
    sba("cnts", [8, 8], F32)
    sba("rcol", [128, 16], F32)
    sba("epsc", [128, 2], F32)

    assert nc.sbuf_bytes_remaining >= 0, \
        f"SBUF over budget by {-nc.sbuf_bytes_remaining} B/partition"
    build.sbuf_slack = nc.sbuf_bytes_remaining

    s["IOTAIs"] = s["x2f"][:, 0:128]
    s["SAc"] = s["modfm"][:, 0:8]
    s["CAc"] = s["modfm"][:, 8:16]
    s["SBc"] = s["modfm"][:, 24:32]
    s["CBc"] = s["modfm"][:, 32:40]
    s["TRIs"] = s["x2f"][:, 512:576].bitcast(BF16)
    s["attw"] = s["zt"].bitcast(BF16)
    s["M1s"] = s["x2f"][:, 128:256]
    s["EYEs"] = s["x2f"][:, 256:384]
    s["e1"] = s["x2f"][:, 384:512]
    s["sq1"] = s["zt"][0:16, 0:128]
    s["mixC"] = s["yacc"][:, 0:1024]
    s["mixD"] = s["yacc"][:, 1024:2048]

    PB = [nc.alloc_psum_tensor(f"pb{i}", [128, 512], F32).ap() for i in range(8)]
    pt = [PB[0].bitcast(BF16)[:, 0:128], PB[1].bitcast(BF16)[:, 0:128]]
    ATT = [PB[2][:, 0:128], PB[3][:, 0:128]]
    POUT = PB[3][:, 128:256]
    PVV = [PB[6][:, 0:512], PB[7][:, 0:512]]
    PAX2 = PB[4][:, 128:256]
    PRT = PB[6][:, 256:264]
    PCS1 = PB[6][:, 264:272]
    PCS2 = PB[6][:, 272:280]
    PBASE = PB[6][:, 280:288]
    PS1 = PB[6][0:16, 0:128]
    PS2 = PB[6][0:16, 128:256]
    PB1 = PB[6][:, 256:384]
    PB0 = PB[6][:, 384:512]
    PRJ = [PB[4], PB[5]]
    MOD = [PB[6][:, 0:48], PB[6][:, 64:112]]
    PWO = [PB[7], PB[7]]

    def h1c(mm, off=0, n=T):
        return slice(BTC + T * mm + off, BTC + T * mm + off + n)

    def h2c(mm, off=0, n=T):
        return slice(T * mm + off, T * mm + off + n)

    pr = Prog()
    E = pr.emit

    def dram_ap(off, dims):
        return bass.AP(bscr.tensor, off, dims)

    ld_map = [("bnd", "bndS"), ("pmask", "pmaskS"), ("spow", "spowS"),
              ("smask", "smaskS"), ("tmix", "tmixS"), ("lnw", "lnwS"),
              ("lnb", "lnbS"), ("tdrow", "tdS"), ("tfrow", "tfS"),
              ("IDN", "IDNs"), ("REVJ", "REVJs"), ("LT", "LTs"),
              ("ONESB", "ONESBs"), ("PERT", "PERTs"), ("SELS", "SELSs"),
              ("SELTS64", "SELTS64s")]
    tLD = E("sy", lambda e: [e.dma_start(out=s[dst], in_=d[src])
                             for src, dst in ld_map], dma=len(ld_map))
    tEps = E("dv", lambda e: (e.memset(s["epsc"][:, 0:1], EPS1),
                              e.memset(s["epsc"][:, 1:2], CGN))[-1], wait=[])
    tCrow = E("sy", lambda e: [e.dma_start(
        out=s["x2f"][0:2, 0:512],
        in_=bass.AP(d["crow"].tensor, 0, [[512, 2], [1, 512]]))], dma=1)

    def fwk(slot, col0, w, r0):
        return lambda e: [e.dma_start(out=WS[slot][:, col0:col0 + 1024],
                                      in_=d[w][r0:r0 + 128, :])]
    tw = []
    for kc in range(KC):
        tw.append(E("gp", fwk(0, kc * 1024, "WkT", 128 * kc), dma=1))
    for kc in range(KC):
        tw.append(E("gp", fwk(1, kc * 1024, "WvT", 128 * kc), dma=1))
    tW2A = tw[-1]

    tLnd = E("ac", lambda e: (e.activation(s["lrow"][0:1, 0:16], s["tdS"], AF.Exp),
                              e.mul(s["lrow"][0:1, 0:16], s["lrow"][0:1, 0:16], -1.0))[-1],
             wait=[tLD])
    tWsr = E("ac", lambda e: e.activation(s["lrow"][0:1, 32:48], s["lrow"][0:1, 0:16],
                                          AF.Exp, scale=128.0), wait=[tLnd])
    tSilc = E("ac", lambda e: e.activation(s["x2f"][0:2, 0:512], s["x2f"][0:2, 0:512],
                                           AF.Silu), wait=[tCrow])
    tBs1 = E("sy", lambda e: [
        e.dma_start(out=bscr[0:1, 0:16], in_=s["lrow"][0:1, 0:16]),
        e.dma_start(out=bscr[0:1, 16:32], in_=s["tfS"]),
        e.dma_start(out=bscr[0:1, 32:48], in_=s["lrow"][0:1, 32:48]),
        e.dma_start(out=bass.AP(bscr.tensor, 48, [[512, 2], [1, 512]]),
                    in_=s["x2f"][0:2, 0:512]),
    ], wait=[tLnd, tWsr, tSilc], dma=4)
    def fbr(e):
        res = [
            e.dma_start(out=s["lcol"][:, 0:1], in_=dram_ap(0, [[1, 16], [1, 1]]),
                        allow_slow_non_contiguous=True),
            e.dma_start(out=s["lndbc"], in_=dram_ap(0, [[0, 128], [1, 16]])),
            e.dma_start(out=s["UCOL"], in_=dram_ap(16, [[0, 128], [1, 16]])),
        ]
        for cc in range(8):
            for hb in range(2):
                rows = slice(64 * hb, 64 * (hb + 1))
                res.append(e.dma_start(
                    out=s["LNDCOL"][rows, cc:cc + 1],
                    in_=dram_ap(2 * cc + hb, [[0, 64], [1, 1]]),
                    allow_slow_non_contiguous=True))
                res.append(e.dma_start(
                    out=s["WSCOL"][rows, cc:cc + 1],
                    in_=dram_ap(32 + 2 * cc + hb, [[0, 64], [1, 1]]),
                    allow_slow_non_contiguous=True))
        return res
    tBr = E("sy", fbr, wait=[tBs1], dma=3 + 32)
    tScb = E("gp", lambda e: [e.dma_start(
        out=s["scb"][:, kc:kc + 1],
        in_=dram_ap(48 + 128 * kc, [[1, 128], [1, 1]])) for kc in range(KC)],
        wait=[tBs1], dma=KC)
    tWKa = E("ac", lambda e: [e.activation(s["zt"][:, hh:hh + 1], s["REVJs"],
                                           AF.Exp, scale=s["lndbc"][:, hh:hh + 1])
                              for hh in range(16)][-1], wait=[tBr, tLD])
    tWKM = E("dv", lambda e: [e.tensor_scalar(
        s["WKM"][:, 64 * hh:64 * (hh + 1)], s["ONESBs"][:, 0:64],
        s["zt"][:, hh:hh + 1], None, ALU.mult) for hh in range(16)][-1],
        wait=[tWKa, tLD])
    tCload = E("sy", lambda e: [
        e.dma_start(out=s["IOTAIs"], in_=d["IOTAI"]),
        e.dma_start(out=s["M1s"], in_=d["M1"]),
        e.dma_start(out=s["EYEs"], in_=d["EYE"]),
        e.dma_start(out=s["TRIs"], in_=d["TRI"]),
    ], wait=[tWKM], dma=4)
    tWBC = E("ac", lambda e: [e.activation(s["WBCm"][:, 128 * cc:128 * (cc + 1)],
                                           s["IOTAIs"], AF.Exp,
                                           scale=s["LNDCOL"][:, cc:cc + 1])
                              for cc in range(PAIRS)][-1], wait=[tBr, tLD, tCload])
    tWm = []
    for hh in range(16):
        wl = [tBr, tLD, tCload] + ([tWm[-1]] if tWm else [])
        tWmA = E("ac", (lambda hh=hh: (lambda e: e.activation(
            s["e1"], s["M1s"], AF.Exp, scale=s["lndbc"][:, hh:hh + 1])))(), wait=wl)
        tWm.append(E("dv", (lambda hh=hh: (lambda e: (
            e.tensor_tensor(s["zt"], s["e1"], s["TRIs"], ALU.mult),
            e.scalar_tensor_tensor(s["wmat"][:, 128 * hh:128 * (hh + 1)], s["EYEs"],
                                   s["UCOL"][:, hh:hh + 1], s["zt"],
                                   ALU.mult, ALU.add))[-1]))(), wait=[tWmA]))
    tWmat = tWm[-1]

    tC8a = E("dv", lambda e: e.tensor_scalar(s["C8"], s["spowS"], s["lcol"][:, 0:1],
                                             None, ALU.mult), wait=[tBr, tLD])
    tC8b = E("ac", lambda e: e.activation(s["C8"], s["C8"], AF.Exp), wait=[tC8a])
    tC8c = E("dv", lambda e: e.tensor_tensor(s["C8"], s["C8"], s["smaskS"], ALU.mult),
             wait=[tC8b])
    tC8d = E("sy", lambda e: [e.dma_start(
        out=bass.AP(bscr.tensor, 2048, [[8, 16], [1, 8]]), in_=s["C8"])],
        wait=[tC8c], dma=1)
    def fc8e(e):
        res = []
        for hb in range(2):
            rows = slice(64 * hb, 64 * (hb + 1))
            res.append(e.dma_start(
                out=s["C8P"][rows, :].rearrange("p (c r) -> p c r", c=8),
                in_=dram_ap(2048 + 8 * hb, [[0, 64], [16, 8], [1, 8]]),
                allow_slow_non_contiguous=True))
        return res
    tC8e = E("sy", fc8e, wait=[tC8d], dma=2)

    tMz = E("dv", lambda e: e.memset(s["modfm"], 0.0), wait=[])
    tModMM = [None, None]
    tModAdd = [tMz, tMz]
    for kc in range(KC):
        slot = 2 + (kc % 2)
        bank = kc % 2
        wl = [tLD, tScb] if kc < 2 else [tModMM[0], tModMM[1]]
        tload = E("gp", (lambda slot=slot, kc=kc: (lambda e: [e.dma_start(
            out=WS[slot][:, 0:6144], in_=d["wmod"][128 * kc:128 * (kc + 1), :])]))(),
            wait=wl, dma=1)
        def fmodmm(slot=slot, kc=kc, bank=bank):
            def f(e):
                last = None
                for ob in range(48):
                    last = e.matmul(MOD[bank][:, ob:ob + 1],
                                    WS[slot][:, 128 * ob:128 * (ob + 1)],
                                    s["scb"][:, kc:kc + 1],
                                    start=True, stop=True)
                return last
            return f
        wl = [tload, tScb, tModAdd[0], tModAdd[1]]
        tModMM[bank] = E("pe", fmodmm(), wait=wl)
        tModAdd[bank] = E("dv", (lambda bank=bank: (lambda e: e.tensor_tensor(
            s["modfm"], s["modfm"], MOD[bank], ALU.add)))(),
            wait=[tModMM[bank], tMz])
    tModDone = E("dv", lambda e: (
        e.tensor_scalar(s["modfm"][:, 8:16], s["modfm"][:, 8:16], 1.0, None, ALU.add),
        e.tensor_scalar(s["modfm"][:, 32:40], s["modfm"][:, 32:40], 1.0, None,
                        ALU.add))[-1], wait=[tModAdd[0], tModAdd[1]])

    def emit_clamp(c0, wait):
        sc2 = s["zt"][:, 0:8]
        sgn = s["zt"][:, 8:16]
        mg = s["modfm"][:, c0:c0 + 8]
        tA = E("dv", lambda e: (
            e.tensor_scalar(sgn, mg, 0.0, None, ALU.is_ge),
            e.tensor_scalar(sc2, mg, -1.0, None, ALU.mult))[-1], wait=wait)
        tB = E("dv", lambda e: (
            e.tensor_scalar(sgn, sgn, 2.0, -1.0, ALU.mult, ALU.add),
            e.tensor_tensor(sc2, sc2, mg, ALU.max))[-1], wait=[tA])
        tC = E("dv", lambda e: e.tensor_scalar(sc2, sc2, 0.25, None, ALU.max),
               wait=[tB])
        return E("dv", lambda e: e.tensor_tensor(mg, sgn, sc2, ALU.mult),
                 wait=[tC])
    tCl1 = emit_clamp(16, [tModDone, tWmat])
    tCl2 = emit_clamp(40, [tCl1])
    def fmodb(e):
        res = []
        for i, c0 in enumerate((16, 40)):
            for cc in range(8):
                res.append(e.dma_start(
                    out=bscr[0:1, 4096 + 1024 * i + 128 * cc:
                             4096 + 1024 * i + 128 * (cc + 1)],
                    in_=s["modfm"][:, c0 + cc:c0 + cc + 1]))
        return res
    tModB = E("gp", fmodb, wait=[tCl2], dma=16)

    tCAc = tSAc = tCBc = tSBc = tModDone
    tBndCA = E("dv", lambda e: (
        e.tensor_scalar(s["CAb"], s["CAc"], s["bndS"], None, ALU.mult),
        e.tensor_scalar(s["SAb"], s["SAc"], s["bndS"], None, ALU.mult))[-1],
        wait=[tCAc, tSAc, tLD])
    tGA = E("gp", lambda e: [e.dma_start(
        out=s["modG"], in_=dram_ap(4096, [[0, 128], [1, D]]))],
        wait=[tModB], dma=1)

    tXL = {}
    tH1tm = {}
    tTr = {}
    tEv = {}
    tSqA = {}
    tSt1 = {}
    ptTok = [None, None]

    tXL[-1] = E("gp", lambda e: [e.dma_start(out=s["xb1"][0:1, :],
                                             in_=d["xs"][0:1, :])], dma=1)
    tTapB = None
    for mm in range(-1, M):
        if mm >= 0:
            xb = s["xb0"] if mm % 2 == 0 else s["xb1"]
            wl = [tH1tm[mm - 2]] if mm >= 2 else ([tH1tm[-1]] if mm == 1 else [])
            tXL[mm] = E("gp", (lambda mm=mm, xb=xb: (lambda e: [e.dma_start(
                out=xb, in_=d["xs"][1 + T * mm:1 + T * (mm + 1), :])]))(),
                wait=wl, dma=1)
        src = s["xb1"] if mm == -1 else (s["xb0"] if mm % 2 == 0 else s["xb1"])
        rows = slice(0, 1) if mm == -1 else slice(0, 128)
        wl = [tXL[mm], tLD] + ([tSt1[mm - 1]] if (mm - 1) in tSt1 else [])
        if mm == 0 and tTapB is not None:
            wl.append(tTapB)
        tSqA[mm] = E("ac", (lambda src=src, rows=rows, mm=mm: (lambda e: e.activation(
            s["dmix"][rows, :], src[rows, :], AF.Square,
            accum_out=s["lncol"][rows, 2:3])))(), wait=wl)
        tSt0 = E("dv", (lambda src=src, rows=rows: (lambda e: e.reduce_sum(
            s["lncol"][rows, 0:1], src[rows, :], mybir.AxisListType.X)))(),
            wait=[tXL[mm], tLD] + ([tTapB] if (mm == 0 and tTapB) else []))
        def fst1a(rows=rows):
            def f(e):
                e.tensor_scalar(s["lncol"][rows, 12:13], s["lncol"][rows, 0:1],
                                1.0 / 1024, None, ALU.mult)
                return e.tensor_scalar(s["lncol"][rows, 11:12],
                                       s["lncol"][rows, 2:3],
                                       1.0 / 1024, None, ALU.mult)
            return f
        t1a = E("dv", fst1a(), wait=[tSt0, tSqA[mm]])
        t1b = E("dv", (lambda rows=rows: (lambda e: e.tensor_tensor(
            s["lncol"][rows, 1:2], s["lncol"][rows, 12:13],
            s["lncol"][rows, 12:13], ALU.mult)))(), wait=[t1a])
        tSt1[mm] = E("dv", (lambda rows=rows: (lambda e: e.tensor_tensor(
            s["lncol"][rows, 13:14], s["lncol"][rows, 11:12],
            s["lncol"][rows, 1:2], ALU.subtract)))(), wait=[t1b])
        tQ = E("ac", (lambda rows=rows: (lambda e: e.activation(
            s["lncol"][rows, 14:15], s["lncol"][rows, 13:14], AF.Sqrt,
            bias=s["epsc"][rows, 0:1])))(), wait=[tSt1[mm], tEps])
        tRc = E("dv", (lambda rows=rows: (lambda e: e.reciprocal(
            s["lncol"][rows, 15:16], s["lncol"][rows, 14:15])))(), wait=[tQ])
        def fnx(src=src, rows=rows):
            def f(e):
                return e.tensor_scalar(s["h1tm"][rows, :], src[rows, :],
                                       s["lncol"][rows, 12:13],
                                       s["lncol"][rows, 15:16],
                                       ALU.subtract, ALU.mult)
            return f
        wl = [tRc] + ([tTr[(mm - 1, KC - 1)]] if mm >= 0 else [])
        tH1tm[mm] = E("dv", fnx(), wait=wl)
        for kc in range(KC):
            gi = (mm + 1) * KC + kc
            bank = gi % 2
            wl = [tH1tm[mm], tLD]
            if ptTok[bank] is not None:
                wl.append(ptTok[bank])
            if mm == -1:
                fn = (lambda kc=kc, bank=bank: (lambda e: e.transpose(
                    pt[bank][:, 0:1], s["h1tm"][0:1, 128 * kc:128 * (kc + 1)],
                    s["IDNs"][0:1, 0:1])))()
            else:
                fn = (lambda kc=kc, bank=bank: (lambda e: e.transpose(
                    pt[bank], s["h1tm"][:, 128 * kc:128 * (kc + 1)], s["IDNs"])))()
            tTr[(mm, kc)] = E("pe", fn, wait=wl)
            if mm == -1:
                fe = (lambda kc=kc, bank=bank: (lambda e: e.tensor_scalar(
                    h1[kc][:, BNDC:BNDC + 1], pt[bank][:, 0:1],
                    s["CAb"][:, kc:kc + 1], s["SAb"][:, kc:kc + 1],
                    ALU.mult, ALU.add)))()
                wl2 = [tTr[(mm, kc)], tBndCA]
            else:
                fe = (lambda kc=kc, bank=bank, mm=mm: (lambda e: e.tensor_scalar(
                    h1[kc][:, h1c(mm)], pt[bank],
                    s["CAc"][:, kc:kc + 1], s["SAc"][:, kc:kc + 1],
                    ALU.mult, ALU.add)))()
                wl2 = [tTr[(mm, kc)], tCAc, tSAc]
            ptTok[bank] = tEv[(mm, kc)] = E("dv", fe, wait=wl2)
        if mm == -1 and "bdg" in tapd:
            tTapB = E("sy", lambda e: [
                e.dma_start(out=tapd["bdg"][:, 0:8], in_=s["CAb"]),
                e.dma_start(out=tapd["bdg"][:, 8:16], in_=s["SAb"]),
                e.dma_start(out=tapd["bdg"][:, 16:32], in_=s["lncol"]),
                e.dma_start(out=tapd["bdh"][:, 0:D], in_=s["h1tm"]),
            ] + [e.dma_start(out=tapd["bdh"][:, D + kc:D + kc + 1],
                             in_=h1[kc][:, BNDC:BNDC + 1],
                             allow_slow_non_contiguous=True) for kc in range(KC)],
                wait=[tEv[(-1, KC - 1)], tBndCA], dma=4 + KC)
    tH1done = tEv[(M - 1, KC - 1)]

    tDm = {}; tXk = {}; tXv = {}
    tKt = {}; tKe = {}; tVt = {}; tVe = {}; tKk = {}
    tAx = {}; tSu = {}
    tSinit = E("dv", lambda e: e.memset(s["spair"], 0.0), wait=[])

    def fdmix(mm):
        def f(e):
            last = None
            for kc in range(KC):
                last = e.tensor_tensor(s["dmix"][:, 128 * kc:128 * (kc + 1)],
                                       h1[kc][:, h1c(mm)], h1[kc][:, h1c(mm, -1)],
                                       ALU.subtract)
            return last
        return f

    def fmixv(dst, tmcol0, mm):
        def f(e):
            last = None
            for kc in range(KC):
                last = e.scalar_tensor_tensor(
                    s[dst][:, 128 * kc:128 * (kc + 1)],
                    s["dmix"][:, 128 * kc:128 * (kc + 1)],
                    s["tmixS"][:, tmcol0 + kc:tmcol0 + kc + 1],
                    h1[kc][:, h1c(mm, -1)], ALU.mult, ALU.add)
            return last
        return f

    def emit_mix_gp(dst, tmcol0, mm, wait):
        def fa(e):
            last = None
            for kc in range(KC):
                last = e.tensor_scalar(
                    s[dst][:, 128 * kc:128 * (kc + 1)],
                    s["dmix"][:, 128 * kc:128 * (kc + 1)],
                    s["tmixS"][:, tmcol0 + kc:tmcol0 + kc + 1],
                    None, ALU.mult)
            return last
        tA = E("gp", fa, wait=wait)
        def fb(e):
            last = None
            for kc in range(KC):
                dv = s[dst][:, 128 * kc:128 * (kc + 1)]
                last = e.tensor_tensor(dv, dv, h1[kc][:, h1c(mm, -1)], ALU.add)
            return last
        return E("gp", fb, wait=[tA])

    tTap0 = None
    for mm in range(M):
        wl = [tEv[(mm, KC - 1)]]
        if mm >= 1:
            wl += [tXk[mm - 1], tXv[mm - 1]]
        else:
            wl.append(tSqA[M - 1])
        if mm == 1 and tTap0 is not None:
            wl.append(tTap0)
        tDm[mm] = E("dv", fdmix(mm), wait=wl)
        wl = [tDm[mm]] + ([tKt[(mm - 1, 1)]] if mm >= 1 else [])
        tXk[mm] = E("dv", fmixv("mixA", 8, mm), wait=wl)
        wl = [tDm[mm]] + ([tVt[(mm - 1, 1)]] if mm >= 1 else [])
        tXv[mm] = E("dv", fmixv("mixB", 16, mm), wait=wl)
        for hf in range(2):
            def fk(hf=hf):
                def f(e):
                    last = None
                    for kc in range(KC):
                        last = e.matmul(PRJ[hf][:, 0:512],
                                        s["mixA"][:, 128 * kc:128 * (kc + 1)],
                                        WS[0][:, kc * 1024 + 512 * hf:kc * 1024 + 512 * (hf + 1)],
                                        start=(kc == 0), stop=(kc == KC - 1))
                    return last
                return f
            wl = [tXk[mm], tW2A] + ([tKe[(mm - 1, hf)]] if mm >= 1 else [tModMM[hf]])
            tKt[(mm, hf)] = E("pe", fk(), wait=wl)
            wl = [tKt[(mm, hf)]] + ([tKk[mm - 1]] if mm >= 1 else [])
            tKe[(mm, hf)] = E("ac", (lambda hf=hf: (lambda e: e.copy(
                s["ktm"][:, 512 * hf:512 * (hf + 1)], PRJ[hf][:, 0:512])))(), wait=wl)
        for hf in range(2):
            def fv(hf=hf):
                def f(e):
                    last = None
                    for kc in range(KC):
                        last = e.matmul(PVV[hf],
                                        s["mixB"][:, 128 * kc:128 * (kc + 1)],
                                        WS[1][:, kc * 1024 + 512 * hf:kc * 1024 + 512 * (hf + 1)],
                                        start=(kc == 0), stop=(kc == KC - 1))
                    return last
                return f
            wl = [tXv[mm], tW2A] + ([tVe[(mm - 1, hf)]] if mm >= 1
                                     else [tModAdd[0], tModAdd[1]])
            tVt[(mm, hf)] = E("pe", fv(), wait=wl)
            wl = [tVt[(mm, hf)]]
            if mm >= 1:
                wl.append(tAx[(mm - 1, PAIRS - 1)])
            tVe[(mm, hf)] = E("ac", (lambda hf=hf: (lambda e: e.copy(
                s["vtm"][:, 512 * hf:512 * (hf + 1)], PVV[hf])))(), wait=wl)
        wl = [tKe[(mm, 0)], tKe[(mm, 1)], tWKM]
        if mm >= 1:
            wl.append(tAx[(mm - 1, PAIRS - 1)])
        tKk[mm] = E("dv", lambda e: e.tensor_tensor(s["ktm"], s["ktm"], s["WKM"],
                                                    ALU.mult), wait=wl)
        for cc in range(PAIRS):
            bank = 4 + cc % 2
            wl = [tKk[mm], tVe[(mm, 0)], tVe[(mm, 1)]]
            prev = (mm, cc - 2) if cc >= 2 else ((mm - 1, cc + PAIRS - 2) if mm >= 1 else None)
            if prev is not None and prev in tSu:
                wl.append(tSu[prev])
            tAx[(mm, cc)] = E("pe", (lambda cc=cc: (lambda e: e.matmul(
                ATT[cc % 2], s["ktm"][:, 128 * cc:128 * (cc + 1)],
                s["vtm"][:, 128 * cc:128 * (cc + 1)], start=True, stop=True)))(),
                wait=wl)
            def fsu(cc=cc, bank=bank):
                def f(e):
                    e.scalar_tensor_tensor(
                        s["spair"][0:64, 64 * cc:64 * (cc + 1)],
                        s["spair"][0:64, 64 * cc:64 * (cc + 1)],
                        s["WSCOL"][0:64, cc:cc + 1], ATT[cc % 2][0:64, 0:64],
                        ALU.mult, ALU.add)
                    return e.scalar_tensor_tensor(
                        s["spair"][64:128, 64 * cc:64 * (cc + 1)],
                        s["spair"][64:128, 64 * cc:64 * (cc + 1)],
                        s["WSCOL"][64:128, cc:cc + 1], ATT[cc % 2][64:128, 64:128],
                        ALU.mult, ALU.add)
                return f
            tSu[(mm, cc)] = E("dv", fsu(), wait=[tAx[(mm, cc)], tSinit, tBr])
        if mm == 0 and "p2a0" in tapd:
            tTap0 = E("sy", lambda e: [
                e.dma_start(out=tapd["p2a0"][:, 0:D], in_=s["ktm"]),
                e.dma_start(out=tapd["p2a0"][:, D:2 * D], in_=s["vtm"]),
                e.dma_start(out=tapd["p2a0"][:, 2 * D:3 * D], in_=s["mixA"]),
                e.dma_start(out=tapd["p2a0"][:, 3 * D:4 * D], in_=s["dmix"]),
            ] + [e.dma_start(out=tapd["p2a0"][:, 4 * D + kc:4 * D + kc + 1],
                             in_=h1[kc][:, BNDC:BNDC + 1],
                             allow_slow_non_contiguous=True) for kc in range(KC)],
                wait=[tKk[0], tVe[(0, 1)], tXk[0], tDm[0]], dma=4 + KC)

    ccr = s["yacc"].bitcast(F32)[:, 1024:1536]
    tCCw = E("gp", lambda e: [e.dma_start(
        out=ccin.rearrange("a (p f) -> (a p) f", p=128), in_=s["spair"])],
        wait=[tSu[(M - 1, PAIRS - 1)]], dma=1)
    tCC = E("gp", lambda e: e.collective_compute(
        "AllGather", ALU.bypass, replica_groups=[[0, 1, 2, 3, 4, 5, 6, 7]],
        ins=[ccin], outs=[ccout]), wait=[tCCw])
    tSz = E("dv", lambda e: e.memset(s["spair"], 0.0), wait=[tCCw])
    tCI = tSz
    for r in range(8):
        tRd = E("gp", (lambda r=r: (lambda e: [e.dma_start(
            out=ccr, in_=ccout[r:r + 1, :].rearrange("a (p f) -> (a p) f", p=128))]))(),
            wait=[tCC, tCI, tAx[(M - 1, PAIRS - 1)]], dma=1)
        def fsin(r=r):
            def f(e):
                last = None
                for cc in range(PAIRS):
                    last = e.scalar_tensor_tensor(
                        s["spair"][:, 64 * cc:64 * (cc + 1)],
                        ccr[:, 64 * cc:64 * (cc + 1)],
                        s["C8P"][:, 8 * cc + r:8 * cc + r + 1],
                        s["spair"][:, 64 * cc:64 * (cc + 1)], ALU.mult, ALU.add)
                return last
            return f
        tCI = E("dv", fsin(), wait=[tRd, tC8e, tSz])
    tSb0 = E("ac", lambda e: e.copy(s["sbf"], s["spair"]), wait=[tCI])

    tw2b = []
    for kc in range(KC):
        tw2b.append(E("gp", fwk(2, kc * 1024, "WrT", 128 * kc),
                      wait=[tModMM[0], tModMM[1]], dma=1))
    for kc in range(KC):
        tw2b.append(E("gp", fwk(3, kc * 1024, "WgT", 128 * kc),
                      wait=[tModMM[0], tModMM[1]], dma=1))
    for kc in range(KC):
        tw2b.append(E("gp", fwk(4, kc * 1024, "WoT", 128 * kc), dma=1))
    tRtrL = E("gp", lambda e: [e.dma_start(out=s["rtrS"][:, 8 * kc:8 * (kc + 1)],
                                           in_=d["WrtrT"][128 * kc:128 * (kc + 1), :])
                               for kc in range(KC)], dma=KC)
    tW2B = tw2b[-1]

    tDm2 = {}; tXr = {}; tXk2 = {}; tXv2 = {}; tXg2 = {}
    tRt = {}; tRe = {}; tK2t = {}; tK2e = {}; tGt = {}; tGe = {}
    tV2t = {}; tV2e = {}; tTt = {}; tTe = {}; tKk2 = {}; tRb = {}
    tAtt = {}; tAw = {}; tOutg = {}; tYe = {}; tAx2 = {}; tSu2 = {}; tSbp = {}
    tSq = {}; tS1 = {}; tS2 = {}; tStE = {}; tSqS = {}; tU = {}; tQg = {}; tRst = {}
    tBc = {}; tGn = {}; tGna = {}; tGng = {}
    tWo = {}; tX2v = {}; tX2s = {}; tL2s = {}; tL2q = {}; tL2n = {}
    tTt2 = {}; tTe2 = {}; tRtr = {}; tX2A = {}
    tRo1 = {}; tRo2 = {}; tRo3 = {}; tCs1 = {}; tRo4 = {}
    tXL2 = {}
    tL2sqTok = {}

    tCz = E("dv", lambda e: e.memset(s["cnt"], 0.0), wait=[])

    for mm in range(M):
        xb = s["xb0"] if mm % 2 == 0 else s["xb1"]
        wl = [tH1tm[M - 1]] if mm < 2 else [tX2v[mm - 2]]
        tXL2[mm] = E("gp", (lambda mm=mm, xb=xb: (lambda e: [e.dma_start(
            out=xb, in_=d["xs"][1 + T * mm:1 + T * (mm + 1), :])]))(), wait=wl, dma=1)

        wl = [tH1done]
        if mm >= 1:
            wl += [tXr[mm - 1], tXk2[mm - 1], tXv2[mm - 1], tXg2[mm - 1],
                   tS2[mm - 1], tL2sqTok[mm - 1]]
        else:
            wl += [tXk[M - 1], tXv[M - 1]]
        tDm2[mm] = E("dv", fdmix(mm), wait=wl)
        wl = [tDm2[mm]] + ([tRt[(mm - 1, KC - 1)]] if mm >= 1 else [tKt[(M - 1, 1)]])
        tXr[mm] = E("dv", fmixv("mixA", 0, mm), wait=wl)
        for mc in range(KC):
            bank = 4 + mc % 2
            def fr(mc=mc, bank=bank):
                def f(e):
                    last = None
                    for kc in range(KC):
                        last = e.matmul(PRJ[bank - 4][:, 0:128],
                                        WS[2][:, kc * 1024 + 128 * mc:kc * 1024 + 128 * (mc + 1)],
                                        s["mixA"][:, 128 * kc:128 * (kc + 1)],
                                        start=(kc == 0), stop=(kc == KC - 1))
                    return last
                return f
            wl = [tXr[mm], tW2B]
            prev = (mm, mc - 2) if mc >= 2 else ((mm - 1, mc + KC - 2) if mm >= 1 else None)
            if prev is not None and prev in tRe:
                wl.append(tRe[prev])
            if mc < 2 and mm >= 1:
                wl += [tGe[(mm - 1, KC - 2 + mc)], tSu2[(mm - 1, PAIRS - 1)]]
            elif mc < 2:
                wl.append(tKe[(M - 1, mc)])
            tRt[(mm, mc)] = E("pe", fr(), wait=wl)
            wl = [tRt[(mm, mc)]]
            if mm >= 1:
                wl += [tAtt[(mm - 1, PAIRS - 1, 1)], tRb[mm - 1], tWo[(mm - 1, 1)]]
            tRe[(mm, mc)] = E("ac", (lambda mc=mc, bank=bank: (lambda e: e.copy(
                s["rfm"][:, 128 * mc:128 * (mc + 1)], PRJ[bank - 4][:, 0:128])))(), wait=wl)
        wl = [tDm2[mm]] + ([tK2t[(mm - 1, KC - 1)]] if mm >= 1 else [])
        tXk2[mm] = E("dv", fmixv("mixC", 8, mm), wait=wl)
        for mc in range(KC):
            bank = 4 + mc % 2
            def fk2(mc=mc, bank=bank):
                def f(e):
                    last = None
                    for kc in range(KC):
                        last = e.matmul(PRJ[bank - 4][:, 0:128],
                                        WS[0][:, kc * 1024 + 128 * mc:kc * 1024 + 128 * (mc + 1)],
                                        s["mixC"][:, 128 * kc:128 * (kc + 1)],
                                        start=(kc == 0), stop=(kc == KC - 1))
                    return last
                return f
            wl = [tXk2[mm]]
            if mc >= 2:
                wl.append(tK2e[(mm, mc - 2)])
            else:
                wl.append(tRe[(mm, KC - 2 + mc)])
            tK2t[(mm, mc)] = E("pe", fk2(), wait=wl)
            wl = [tK2t[(mm, mc)]]
            if mm >= 1:
                wl += [tAtt[(mm - 1, PAIRS - 1, 1)], tTt[(mm - 1, KC - 1)]]
            tK2e[(mm, mc)] = E("ac", (lambda mc=mc, bank=bank: (lambda e: e.copy(
                s["kfm"][:, 128 * mc:128 * (mc + 1)], PRJ[bank - 4][:, 0:128])))(), wait=wl)
        wl = [tDm2[mm]] + ([tV2t[(mm - 1, 1)]] if mm >= 1 else [tVt[(M - 1, 1)]])
        tXv2[mm] = emit_mix_gp("mixB", 16, mm, wl)
        for hf in range(2):
            bank = 4 + hf
            def fv2(hf=hf, bank=bank):
                def f(e):
                    last = None
                    for kc in range(KC):
                        last = e.matmul(PRJ[bank - 4][:, 0:512],
                                        s["mixB"][:, 128 * kc:128 * (kc + 1)],
                                        WS[1][:, kc * 1024 + 512 * hf:kc * 1024 + 512 * (hf + 1)],
                                        start=(kc == 0), stop=(kc == KC - 1))
                    return last
                return f
            wl = [tXv2[mm], tK2e[(mm, KC - 2 + hf)]]
            tV2t[(mm, hf)] = E("pe", fv2(), wait=wl)
            wl = [tV2t[(mm, hf)]]
            if mm >= 1:
                wl += [tOutg[(mm - 1, PAIRS - 1)], tAx2[(mm - 1, PAIRS - 1)]]
            else:
                wl.append(tAx[(M - 1, PAIRS - 1)])
            tV2e[(mm, hf)] = E("ac", (lambda hf=hf, bank=bank: (lambda e: e.copy(
                s["vtm"][:, 512 * hf:512 * (hf + 1)], PRJ[bank - 4][:, 0:512])))(), wait=wl)
        wl = [tDm2[mm]] + ([tGt[(mm - 1, KC - 1)]] if mm >= 1 else [])
        tXg2[mm] = emit_mix_gp("mixD", 24, mm, wl)
        for mc in range(KC):
            bank = 4 + mc % 2
            def fg(mc=mc, bank=bank):
                def f(e):
                    last = None
                    for kc in range(KC):
                        last = e.matmul(PRJ[bank - 4][:, 0:128],
                                        WS[3][:, kc * 1024 + 128 * mc:kc * 1024 + 128 * (mc + 1)],
                                        s["mixD"][:, 128 * kc:128 * (kc + 1)],
                                        start=(kc == 0), stop=(kc == KC - 1))
                    return last
                return f
            wl = [tXg2[mm], tW2B]
            if mc >= 2:
                wl.append(tGe[(mm, mc - 2)])
            else:
                wl.append(tV2e[(mm, mc)])
            tGt[(mm, mc)] = E("pe", fg(), wait=wl)
            wl = [tGt[(mm, mc)]]
            if mm >= 1:
                wl.append(tGng[(mm - 1, PAIRS - 1)])
            tGe[(mm, mc)] = E("ac", (lambda mc=mc, bank=bank: (lambda e: e.activation(
                s["gch"][:, 128 * mc:128 * (mc + 1)], PRJ[bank - 4][:, 0:128], AF.Silu)))(),
                wait=wl)
        for kc in range(KC):
            bank = kc % 2
            wl = [tK2e[(mm, kc)], ptTok[bank]]
            tTt[(mm, kc)] = E("pe", (lambda kc=kc, bank=bank: (lambda e: e.transpose(
                pt[bank], s["kfm"][:, 128 * kc:128 * (kc + 1)], s["IDNs"])))(), wait=wl)
            wl = [tTt[(mm, kc)]]
            if mm >= 1 and kc < 2:
                wl.append(tKk2[mm - 1])
            ptTok[bank] = tTe[(mm, kc)] = E("ac", (lambda kc=kc, bank=bank: (
                lambda e: e.copy(s["ktm"][:, 128 * kc:128 * (kc + 1)], pt[bank])))(),
                wait=wl)
        wl = [tTe[(mm, KC - 1)], tWKM]
        if mm >= 1:
            wl.append(tAx2[(mm - 1, PAIRS - 1)])
        else:
            wl.append(tAx[(M - 1, PAIRS - 1)])
        tKk2[mm] = E("dv", lambda e: e.tensor_tensor(s["ktm"], s["ktm"], s["WKM"],
                                                     ALU.mult), wait=wl)
        wl = [tRe[(mm, KC - 1)], tWBC]
        if mm >= 1:
            wl.append(tOutg[(mm - 1, PAIRS - 1)])
        tRb[mm] = E("dv", lambda e: e.tensor_tensor(s["rrb"], s["rfm"], s["WBCm"],
                                                    ALU.mult), wait=wl)

        for cc in range(PAIRS):
            for hs in range(2):
                bank = hs
                wl = [tK2e[(mm, cc)], tRe[(mm, cc)]]
                if (mm, cc - 1, hs) in tAw:
                    wl.append(tAw[(mm, cc - 1, hs)])
                elif mm >= 1:
                    wl.append(tAw[(mm - 1, PAIRS - 1, hs)])
                else:
                    wl.append(tSu[(M - 1, PAIRS - 1)])
                if hs == 1 and (mm, cc - 1) in tYe:
                    wl.append(tYe[(mm, cc - 1)])
                elif hs == 1 and mm >= 1:
                    wl.append(tYe[(mm - 1, PAIRS - 1)])
                r0 = 64 * hs
                tAtt[(mm, cc, hs)] = E("pe", (lambda cc=cc, r0=r0, hs=hs: (
                    lambda e: e.matmul(ATT[hs],
                                       s["kfm"][r0:r0 + 64, 128 * cc:128 * (cc + 1)],
                                       s["rfm"][r0:r0 + 64, 128 * cc:128 * (cc + 1)],
                                       start=True, stop=True)))(), wait=wl)
                wl = [tAtt[(mm, cc, hs)], tWmat]
                if (mm, cc - 1) in tOutg:
                    wl.append(tOutg[(mm, cc - 1)])
                elif mm >= 1:
                    wl.append(tOutg[(mm - 1, PAIRS - 1)])
                if cc == 0 and mm >= 1:
                    wl.append(tSqS[mm - 1])
                tAw[(mm, cc, hs)] = E("dv", (lambda cc=cc, hs=hs, bank=bank: (
                    lambda e: e.tensor_tensor(
                        s["attw"][:, 128 * hs:128 * (hs + 1)], ATT[hs],
                        s["wmat"][:, 128 * (2 * cc + hs):128 * (2 * cc + hs + 1)],
                        ALU.mult)))(), wait=wl)
            wl = [tAw[(mm, cc, 0)], tAw[(mm, cc, 1)], tV2e[(mm, 0)], tV2e[(mm, 1)],
                  tRb[mm], tSb0]
            if (mm - 1, cc) in tSbp:
                wl.append(tSbp[(mm - 1, cc)])
            if (mm, cc - 1) in tYe:
                wl.append(tYe[(mm, cc - 1)])
            elif mm >= 1:
                wl.append(tRo4[mm - 1])
            def fout(cc=cc):
                def f(e):
                    e.matmul(POUT[0:64, :], s["vtm"][:, 128 * cc:128 * cc + 64],
                             s["attw"][:, 0:128], start=True, stop=False)
                    e.matmul(POUT[0:64, :], s["sbf"][0:64, 64 * cc:64 * (cc + 1)],
                             s["rrb"][0:64, 128 * cc:128 * (cc + 1)],
                             start=False, stop=True)
                    e.matmul(POUT[64:128, :],
                             s["vtm"][:, 128 * cc + 64:128 * (cc + 1)],
                             s["attw"][:, 128:256], start=True, stop=False,
                             tile_position=(0, 64))
                    return e.matmul(POUT[64:128, :],
                                    s["sbf"][64:128, 64 * cc:64 * (cc + 1)],
                                    s["rrb"][64:128, 128 * cc:128 * (cc + 1)],
                                    start=False, stop=True, tile_position=(64, 64))
                return f
            tOutg[(mm, cc)] = E("pe", fout(), wait=wl)
            wl = [tOutg[(mm, cc)]]
            if mm >= 1 and cc == 0:
                wl.append(tGn[(mm - 1, PAIRS - 1)])
            tYe[(mm, cc)] = E("ac", (lambda cc=cc: (lambda e: e.copy(
                s["ych"][:, 128 * cc:128 * (cc + 1)], POUT)))(), wait=wl)
            wl = [tKk2[mm], tV2e[(mm, 0)], tV2e[(mm, 1)]]
            if (mm, cc - 1) in tSu2:
                wl.append(tSu2[(mm, cc - 1)])
            elif mm >= 1:
                wl.append(tSu2[(mm - 1, PAIRS - 1)])
            else:
                wl.append(tSu[(M - 1, PAIRS - 1)])
            tAx2[(mm, cc)] = E("pe", (lambda cc=cc: (lambda e: e.matmul(
                PAX2, s["ktm"][:, 128 * cc:128 * (cc + 1)],
                s["vtm"][:, 128 * cc:128 * (cc + 1)], start=True, stop=True)))(),
                wait=wl)
            def fsu2(cc=cc):
                def f(e):
                    e.scalar_tensor_tensor(
                        s["spair"][0:64, 64 * cc:64 * (cc + 1)],
                        s["spair"][0:64, 64 * cc:64 * (cc + 1)],
                        s["WSCOL"][0:64, cc:cc + 1], PAX2[0:64, 0:64],
                        ALU.mult, ALU.add)
                    return e.scalar_tensor_tensor(
                        s["spair"][64:128, 64 * cc:64 * (cc + 1)],
                        s["spair"][64:128, 64 * cc:64 * (cc + 1)],
                        s["WSCOL"][64:128, cc:cc + 1], PAX2[64:128, 64:128],
                        ALU.mult, ALU.add)
                return f
            tSu2[(mm, cc)] = E("dv", fsu2(), wait=[tAx2[(mm, cc)], tCI])
            tSbp[(mm, cc)] = E("ac", (lambda cc=cc: (lambda e: e.copy(
                s["sbf"][:, 64 * cc:64 * (cc + 1)],
                s["spair"][:, 64 * cc:64 * (cc + 1)])))(),
                wait=[tSu2[(mm, cc)], tOutg[(mm, cc)]])

        wl = [tYe[(mm, PAIRS - 1)], tXr[mm], tXk2[mm], tXv2[mm], tXg2[mm]]
        tSq[mm] = E("ac", lambda e: e.activation(s["dmix"], s["ych"], AF.Square),
                    wait=wl)
        wl = [tYe[(mm, PAIRS - 1)], tLD]
        if mm >= 1:
            wl += [tStE[mm - 1], tGn[(mm - 1, PAIRS - 1)], tRo4[mm - 1]]
        def fs1():
            def f(e):
                last = None
                for cc in range(PAIRS):
                    last = e.matmul(PS1,
                                    s["SELSs"][:, 16 * cc:16 * (cc + 1)],
                                    s["ych"][:, 128 * cc:128 * (cc + 1)],
                                    start=(cc == 0), stop=(cc == PAIRS - 1))
                return last
            return f
        tS1[mm] = E("pe", fs1(), wait=wl)
        wl = [tSq[mm]]
        def fs2():
            def f(e):
                last = None
                for cc in range(PAIRS):
                    last = e.matmul(PS2,
                                    s["SELSs"][:, 16 * cc:16 * (cc + 1)],
                                    s["dmix"][:, 128 * cc:128 * (cc + 1)],
                                    start=(cc == 0), stop=(cc == PAIRS - 1))
                return last
            return f
        tS2[mm] = E("pe", fs2(), wait=wl)
        wl = [tS1[mm], tS2[mm]]
        if mm >= 1:
            wl.append(tRst[mm - 1])
        tStE[mm] = E("ac", lambda e: (
            e.copy(s["stats"][:, 0:128], PS1),
            e.copy(s["rstd"], PS2))[-1], wait=wl)
        wl = [tStE[mm]]
        if mm >= 1:
            wl.append(tGna[(mm - 1, PAIRS - 1)])
        tSqS[mm] = E("ac", lambda e: e.activation(s["sq1"], s["stats"][:, 0:128],
                                                  AF.Square), wait=wl)
        wl = [tSqS[mm]]
        if mm >= 1:
            wl.append(tBc[(mm - 1, PAIRS - 1)])
        tU[mm] = E("dv", lambda e: e.scalar_tensor_tensor(
            s["rstd"], s["rstd"], 64.0, s["sq1"],
            ALU.mult, ALU.subtract), wait=wl)
        tQg[mm] = E("ac", lambda e: e.activation(s["rstd"], s["rstd"], AF.Sqrt,
                                                 bias=s["epsc"][0:16, 1:2]),
                    wait=[tU[mm], tEps])
        tRs1 = E("dv", lambda e: e.reciprocal(s["rstd"], s["rstd"]),
                 wait=[tQg[mm]])
        tRs2 = E("dv", lambda e: (
            e.tensor_copy(s["rstdb"], s["rstd"]),
            e.tensor_tensor(s["s1rb"], s["stats"][:, 0:128], s["rstd"],
                            ALU.mult))[-1], wait=[tRs1])
        tRst[mm] = E("dv", lambda e: e.tensor_scalar(
            s["s1rb"], s["s1rb"], -0.015625, None, ALU.mult), wait=[tRs2])
        for cc in range(PAIRS):
            wl = [tRst[mm], tLD]
            if (mm, cc - 1) in tGn:
                wl.append(tGn[(mm, cc - 1)])
            else:
                wl += [tGe[(mm, KC - 1)], tV2e[(mm, 1)]]
            def fbc(cc=cc):
                def f(e):
                    e.matmul(PB1, s["SELTS64s"][:, 128 * cc:128 * (cc + 1)],
                             s["rstdb"], start=True, stop=True)
                    return e.matmul(PB0,
                                    s["SELTS64s"][:, 128 * cc:128 * (cc + 1)],
                                    s["s1rb"], start=True, stop=True)
                return f
            tBc[(mm, cc)] = E("pe", fbc(), wait=wl)
            wl = [tBc[(mm, cc)], tYe[(mm, cc)], tOutg[(mm, PAIRS - 1)]]
            if (mm, cc - 1) in tGna:
                wl.append(tGna[(mm, cc - 1)])
            tGnA = E("dv", (lambda cc=cc: (lambda e: e.scalar_tensor_tensor(
                s["zt"], PB1, 1.0,
                s["ych"][:, 128 * cc:128 * (cc + 1)],
                ALU.mult, ALU.mult)))(), wait=wl)
            tGn[(mm, cc)] = E("dv", lambda e: e.tensor_tensor(
                s["zt"], s["zt"], PB0, ALU.add), wait=[tGnA])
            wl = [tGn[(mm, cc)]]
            if (mm, cc - 1) in tGng:
                wl.append(tGng[(mm, cc - 1)])
            elif mm >= 1:
                wl.append(tWo[(mm - 1, 1)])
            tGna[(mm, cc)] = E("dv", (lambda cc=cc: (lambda e: e.tensor_scalar(
                s["rfm"][:, 128 * cc:128 * (cc + 1)], s["zt"],
                s["lnwS"][:, cc:cc + 1], s["lnbS"][:, cc:cc + 1],
                ALU.mult, ALU.add)))(),
                wait=wl)
            tGng[(mm, cc)] = E("gp", (lambda cc=cc: (lambda e: e.tensor_tensor(
                s["rfm"][:, 128 * cc:128 * (cc + 1)],
                s["rfm"][:, 128 * cc:128 * (cc + 1)],
                s["gch"][:, 128 * cc:128 * (cc + 1)], ALU.mult)))(),
                wait=[tGna[(mm, cc)], tGe[(mm, cc)]])
        def fwo(hf):
            def f(e):
                last = None
                for kc in range(KC):
                    last = e.matmul(PWO[hf][:, 0:512],
                                    s["rfm"][:, 128 * kc:128 * (kc + 1)],
                                    WS[4][:, kc * 1024 + 512 * hf:kc * 1024 + 512 * (hf + 1)],
                                    start=(kc == 0), stop=(kc == KC - 1))
                return last
            return f
        def fx2ah(hf):
            def f(e):
                return e.scalar_tensor_tensor(
                    s["x2f"][:, 512 * hf:512 * (hf + 1)], PWO[hf][:, 0:512],
                    1.0, s["modG"][:, 512 * hf:512 * (hf + 1)],
                    ALU.mult, ALU.mult)
            return f
        wl = [tGng[(mm, PAIRS - 1)], tBc[(mm, PAIRS - 1)], tGn[(mm, PAIRS - 1)]]
        if mm >= 1:
            wl.append(tX2A[mm - 1])
        else:
            wl += [tVe[(M - 1, 0)], tVe[(M - 1, 1)]]
        tWo[(mm, 0)] = E("pe", fwo(0), wait=wl)
        wl = [tWo[(mm, 0)], tGA]
        if mm >= 1:
            wl.append(tL2n[mm - 1])
        tXa0 = E("dv", fx2ah(0), wait=wl)
        tWo[(mm, 1)] = E("pe", fwo(1), wait=[tXa0])
        tX2A[mm] = tX2a = E("dv", fx2ah(1), wait=[tWo[(mm, 1)]])
        wl = [tX2a, tXL2[mm]]
        def fx2b(xb=xb):
            def f(e):
                last = None
                for hf in range(2):
                    last = e.tensor_tensor(s["x2f"][:, 512 * hf:512 * (hf + 1)],
                                           s["x2f"][:, 512 * hf:512 * (hf + 1)],
                                           xb[:, 512 * hf:512 * (hf + 1)], ALU.add)
                return last
            return f
        tX2v[mm] = E("dv", fx2b(), wait=wl)
        tX2s[mm] = E("gp", (lambda mm=mm: (lambda e: [e.dma_start(
            out=x2s[T * mm:T * (mm + 1), :], in_=s["x2f"])]))(), wait=[tX2v[mm]], dma=1)
        tL2sqTok[mm] = tL2sq = E("ac", lambda e: e.activation(
            s["dmix"], s["x2f"], AF.Square, accum_out=s["lncol"][:, 2:3]),
            wait=[tX2v[mm], tS2[mm]])
        tL2s[mm] = E("dv", lambda e: e.reduce_sum(
            s["lncol"][:, 0:1], s["x2f"], mybir.AxisListType.X), wait=[tX2v[mm]])
        tL2ma = E("dv", lambda e: (
            e.tensor_scalar(s["lncol"][:, 12:13], s["lncol"][:, 0:1],
                            1.0 / 1024, None, ALU.mult),
            e.tensor_scalar(s["lncol"][:, 11:12], s["lncol"][:, 2:3],
                            1.0 / 1024, None, ALU.mult))[-1],
            wait=[tL2s[mm], tL2sq])
        tL2mb = E("dv", lambda e: e.tensor_tensor(
            s["lncol"][:, 1:2], s["lncol"][:, 12:13],
            s["lncol"][:, 12:13], ALU.mult), wait=[tL2ma])
        tL2m = E("dv", lambda e: e.tensor_tensor(
            s["lncol"][:, 13:14], s["lncol"][:, 11:12],
            s["lncol"][:, 1:2], ALU.subtract), wait=[tL2mb])
        tL2q[mm] = E("ac", lambda e: e.activation(s["lncol"][:, 14:15],
                                                  s["lncol"][:, 13:14], AF.Sqrt,
                                                  bias=s["epsc"][:, 0:1]),
                     wait=[tL2m, tEps])
        tL2r = E("dv", lambda e: e.reciprocal(s["lncol"][:, 15:16],
                                              s["lncol"][:, 14:15]),
                 wait=[tL2q[mm]])
        wl = [tL2r, tX2s[mm]]
        if mm >= 1:
            wl.append(tTt2[(mm - 1, KC - 1)])
        tL2n[mm] = E("dv", lambda e: e.tensor_scalar(
            s["h1tm"], s["x2f"], s["lncol"][:, 12:13],
            s["lncol"][:, 15:16], ALU.subtract, ALU.mult), wait=wl)
        for kc in range(KC):
            bank = kc % 2
            wl = [tL2n[mm], ptTok[bank]]
            tTt2[(mm, kc)] = E("pe", (lambda kc=kc, bank=bank: (lambda e: e.transpose(
                pt[bank], s["h1tm"][:, 128 * kc:128 * (kc + 1)], s["IDNs"])))(), wait=wl)
            ptTok[bank] = tTe2[(mm, kc)] = E("dv", (lambda mm=mm, kc=kc, bank=bank: (
                lambda e: e.tensor_scalar(h1[kc][:, h2c(mm)], pt[bank],
                                          s["CBc"][:, kc:kc + 1],
                                          s["SBc"][:, kc:kc + 1],
                                          ALU.mult, ALU.add)))(),
                wait=[tTt2[(mm, kc)], tCBc, tSBc])
        wl = [tTe2[(mm, KC - 1)], tRtrL, tStE[mm]]
        if mm >= 1:
            wl.append(tRo2[mm - 1])
        def frtr(mm=mm):
            def f(e):
                last = None
                for kc in range(KC):
                    last = e.matmul(PRT, h1[kc][:, h2c(mm)],
                                    s["rtrS"][:, 8 * kc:8 * (kc + 1)],
                                    start=(kc == 0), stop=(kc == KC - 1))
                return last
            return f
        tRtr[mm] = E("pe", frtr(), wait=wl)
        wl = [tRtr[mm]]
        if mm >= 1:
            wl.append(tRo2[mm - 1])
        tRo1a = E("dv", lambda e: e.reduce_max(
            s["rcol"][:, 0:1], PRT, mybir.AxisListType.X), wait=wl)
        tRo1[mm] = E("dv", lambda e: e.tensor_scalar(
            s["rcol"][:, 1:2], s["rcol"][:, 0:1], -1.0, None, ALU.mult),
            wait=[tRo1a])
        wl = [tRo1[mm]]
        if mm >= 1:
            wl.append(tRo4[mm - 1])
        tRo2[mm] = E("ac", lambda e: e.activation(
            s["gts"], PRT, AF.Exp, bias=s["rcol"][:, 1:2],
            accum_out=s["rcol"][:, 2:3]), wait=wl)
        tRo3a = E("dv", lambda e: (
            e.reciprocal(s["rcol"][:, 3:4], s["rcol"][:, 2:3]),
            e.tensor_tensor(s["pgt"], s["gts"], s["PERTs"], ALU.mult))[-1],
            wait=[tRo2[mm], tLD])
        tRo3b = E("dv", lambda e: (
            e.max(s["srt"], s["pgt"]),
            e.tensor_scalar(s["gts"], s["gts"], s["rcol"][:, 3:4], None,
                            ALU.mult))[-1], wait=[tRo3a])
        def fro3(mm=mm):
            def f(e):
                e.tensor_scalar(s["oh1st"][:, 8 * mm:8 * (mm + 1)], s["pgt"],
                                s["srt"][:, 0:1], None, ALU.is_equal)
                e.tensor_scalar(s["oh2st"][:, 8 * mm:8 * (mm + 1)], s["pgt"],
                                s["srt"][:, 1:2], None, ALU.is_equal)
                e.scalar_tensor_tensor(s["r8a"], s["pgt"], s["srt"][:, 0:1], s["gts"],
                                       ALU.is_equal, ALU.mult,
                                       accum_out=s["ggst"][:, 2 * mm:2 * mm + 1])
                return e.scalar_tensor_tensor(
                    s["r8a"], s["pgt"], s["srt"][:, 1:2], s["gts"],
                    ALU.is_equal, ALU.mult,
                    accum_out=s["ggst"][:, 2 * mm + 1:2 * mm + 2])
            return f
        tRo3c = E("dv", fro3(), wait=[tRo3b])
        tRo3[mm] = E("dv", (lambda mm=mm: (lambda e: e.tensor_tensor(
            s["ohsb"], s["oh1st"][:, 8 * mm:8 * (mm + 1)],
            s["oh2st"][:, 8 * mm:8 * (mm + 1)], ALU.add)))(), wait=[tRo3c])
        wl = [tRo3[mm], tStE[mm], tYe[(mm, PAIRS - 1)], tGn[(mm, PAIRS - 1)]]
        if mm >= 1:
            wl.append(tRo4[mm - 1])
        def fcs():
            def f(e):
                e.matmul(PCS1, s["LTs"], s["ohsb"], start=True, stop=True)
                return e.matmul(PCS2, s["ONESBs"], s["ohsb"],
                                start=True, stop=True)
            return f
        tCs1[mm] = E("pe", fcs(), wait=wl)
        tRo4a = E("dv", lambda e: e.tensor_tensor(
            s["r8a"], PCS1, s["cnt"], ALU.add), wait=[tCs1[mm], tCz])
        def fro4(mm=mm):
            def f(e):
                e.scalar_tensor_tensor(s["gts"], s["pgt"], s["srt"][:, 0:1],
                                       s["r8a"], ALU.is_equal, ALU.mult,
                                       accum_out=s["pgst"][:, 2 * mm:2 * mm + 1])
                e.scalar_tensor_tensor(s["pgt"], s["pgt"], s["srt"][:, 1:2],
                                       s["r8a"], ALU.is_equal, ALU.mult,
                                       accum_out=s["pgst"][:, 2 * mm + 1:2 * mm + 2])
                return e.tensor_tensor(s["cnt"], s["cnt"], PCS2, ALU.add)
            return f
        tRo4[mm] = E("dv", fro4(), wait=[tRo4a])

    tCC2w = E("gp", lambda e: [e.dma_start(out=ccin2, in_=s["cnt"][0:1, :])],
              wait=[tRo4[M - 1]], dma=1)
    tCC2 = E("gp", lambda e: e.collective_compute(
        "AllGather", ALU.bypass, replica_groups=[[0, 1, 2, 3, 4, 5, 6, 7]],
        ins=[ccin2], outs=[ccout2]), wait=[tCC2w])
    tCntR = E("gp", lambda e: [e.dma_start(out=s["cnts"], in_=ccout2)],
              wait=[tCC2], dma=1)
    tBase = E("pe", lambda e: e.matmul(PBASE, s["pmaskS"], s["cnts"],
                                       start=True, stop=True),
              wait=[tCntR, tSu2[(M - 1, PAIRS - 1)], tLD])
    tBaseE = E("ac", lambda e: e.copy(s["basebc"], PBASE), wait=[tBase])

    tCmb = {}
    for mm in range(M):
        def fca(mm=mm):
            def f(e):
                e.scalar_tensor_tensor(s["r8a"], s["oh1st"][:, 8 * mm:8 * (mm + 1)],
                                       1.0, s["basebc"], ALU.mult, ALU.mult,
                                       accum_out=s["rcol"][:, 4:5])
                return e.scalar_tensor_tensor(
                    s["gts"], s["oh2st"][:, 8 * mm:8 * (mm + 1)],
                    1.0, s["basebc"], ALU.mult, ALU.mult,
                    accum_out=s["rcol"][:, 5:6])
            return f
        ta_ = E("dv", fca(), wait=[tBaseE, tModB])
        ta2_ = E("dv", (lambda mm=mm: (lambda e: (
            e.tensor_scalar(s["rcol"][:, 6:7], s["rcol"][:, 4:5],
                            s["pgst"][:, 2 * mm:2 * mm + 1], CAP, ALU.add,
                            ALU.is_le),
            e.tensor_scalar(s["rcol"][:, 7:8], s["rcol"][:, 5:6],
                            s["pgst"][:, 2 * mm + 1:2 * mm + 2], CAP, ALU.add,
                            ALU.is_le))[-1]))(), wait=[ta_])
        ta3_ = E("dv", (lambda mm=mm: (lambda e: (
            e.tensor_scalar(s["rcol"][:, 9:10], s["rcol"][:, 7:8],
                            s["ggst"][:, 2 * mm + 1:2 * mm + 2], None, ALU.mult),
            e.tensor_scalar(s["rcol"][:, 8:9], s["rcol"][:, 6:7],
                            s["ggst"][:, 2 * mm:2 * mm + 1], None,
                            ALU.mult))[-1]))(), wait=[ta2_])
        tb1_ = E("dv", lambda e: e.tensor_scalar(
            s["rcol"][:, 10:11], s["rcol"][:, 8:9],
            s["rcol"][:, 9:10], 1e-6, ALU.add, ALU.add), wait=[ta3_])
        tb_ = E("dv", lambda e: e.reciprocal(
            s["rcol"][:, 11:12], s["rcol"][:, 10:11]), wait=[tb1_])
        tc_ = E("dv", lambda e: (
            e.tensor_scalar(s["rcol"][:, 12:13], s["rcol"][:, 8:9],
                            s["rcol"][:, 11:12], None, ALU.mult),
            e.tensor_scalar(s["rcol"][:, 13:14], s["rcol"][:, 9:10],
                            s["rcol"][:, 11:12], None, ALU.mult))[-1],
            wait=[tb_])
        tcd1 = E("dv", (lambda mm=mm: (lambda e: e.tensor_scalar(
            s["r8a"], s["oh1st"][:, 8 * mm:8 * (mm + 1)],
            s["rcol"][:, 12:13], None, ALU.mult)))(), wait=[tc_])
        tCmb[mm] = E("dv", (lambda mm=mm: (lambda e: e.scalar_tensor_tensor(
            s["combine"][:, 8 * mm:8 * (mm + 1)],
            s["oh2st"][:, 8 * mm:8 * (mm + 1)], s["rcol"][:, 13:14],
            s["r8a"], ALU.mult, ALU.add)))(), wait=[tcd1])

    tPEend = tRtr[M - 1]
    tg5w = []
    for kc in range(KC):
        slot = 0 if kc < 4 else 1
        col0 = (kc % 4) * 2048
        tg5w.append(E("gp", (lambda slot=slot, col0=col0, kc=kc: (lambda e: [e.dma_start(
            out=WS[slot][:, col0:col0 + 2048],
            in_=d["WgateT"][128 * kc:128 * (kc + 1), :])]))(),
            wait=[tPEend], dma=1))
    tG5W = tg5w[-1]
    tGB = E("gp", lambda e: [e.dma_start(
        out=s["modG"], in_=dram_ap(4096 + 1024, [[0, 128], [1, D]]))],
        wait=[tModB, tX2v[M - 1]], dma=1)

    tG5t = {}; tG5e = {}; tDn = {}; tYa = {}; tX5 = {}; tFo = {}; tOutD = {}
    wdidx = 0
    wdslotTok = {}
    for q in range(NE):
        for hk in range(HKC):
            gi = q * HKC + hk
            bank = 4 + gi % 2
            wl = [tG5W, tTe2[(4 * q + 3, KC - 1)]]
            if gi == 0:
                wl += [tX2v[M - 1], tSu2[(M - 1, PAIRS - 1)]]
            prev = gi - 2
            if prev >= 0:
                wl.append(tG5e[(prev // HKC, prev % HKC)])
            def fg5(q=q, hk=hk, bank=bank):
                def f(e):
                    last = None
                    for kc in range(KC):
                        wsl = WS[0] if kc < 4 else WS[1]
                        last = e.matmul(
                            PRJ[bank - 4][:, 0:512],
                            wsl[:, (kc % 4) * 2048 + 128 * hk:(kc % 4) * 2048 + 128 * (hk + 1)],
                            h1[kc][:, ET * q:ET * (q + 1)],
                            start=(kc == 0), stop=(kc == KC - 1))
                    return last
                return f
            tG5t[(q, hk)] = E("pe", fg5(), wait=wl)
            wl = [tG5t[(q, hk)]]
            if q == 0 and hk < 2:
                wl.append(tWo[(M - 1, 1)])
            if q >= 1:
                wl.append(tDn[(q - 1, EXP - 1, 1, 3)])
            tG5e[(q, hk)] = E("ac", (lambda hk=hk, bank=bank: (
                lambda e: e.activation(WS[4][:, 512 * hk:512 * (hk + 1)],
                                       PRJ[bank - 4][:, 0:512], AF.Silu)))(), wait=wl)
        for ex in range(EXP):
            for oh in range(2):
                slot = WS[2] if wdidx % 2 == 0 else WS[3]
                wl = [tPEend]
                if wdidx >= 2:
                    wl.append(wdslotTok[wdidx - 2])
                def fwd(slot=slot, ex=ex, oh=oh):
                    def f(e):
                        res = []
                        for part in range(4):
                            src = d["WdT"][ex, :, 512 * oh:512 * (oh + 1)] \
                                .rearrange("(o p) c -> p o c", p=128)[:, 4 * part:4 * (part + 1), :]
                            dst = slot[:, 2048 * part:2048 * (part + 1)] \
                                .rearrange("p (o c) -> p o c", c=512)
                            res.append(e.dma_start(out=dst, in_=src))
                        return res
                    return f
                tWdTok = E("gp", fwd(), wait=wl, dma=4)
                for t8 in range(4):
                    bank = (t8 + 4 * (oh + 2 * ex)) % 4
                    key = (q, ex, oh, t8)
                    wl = [tWdTok, tG5e[(q, HKC - 1)]]
                    pidx = (ex * 8 + oh * 4 + t8) - 4
                    if pidx >= 0:
                        pex, prem = divmod(pidx, 8)
                        poh, pt8 = divmod(prem, 4)
                        wl.append(tYa[(q, pex, poh, pt8)])
                    elif q >= 1:
                        wl.append(tYa[(q - 1, EXP - 1, 1, t8)])
                    else:
                        wl += [tRo4[M - 1], tBaseE]
                    def fdn(oh=oh, t8=t8, bank=bank, slot=slot):
                        def f(e):
                            last = None
                            for hk in range(HKC):
                                last = e.matmul(
                                    PB[bank][:, 0:512],
                                    WS[4][:, 512 * hk + 128 * t8:512 * hk + 128 * (t8 + 1)],
                                    slot[:, 512 * hk:512 * (hk + 1)],
                                    start=(hk == 0), stop=(hk == HKC - 1))
                            return last
                        return f
                    tDn[key] = E("pe", fdn(), wait=wl)
                    mm = 4 * q + t8
                    wl = [tDn[key], tCmb[mm]]
                    if q >= 1 and ex == 0 and oh == 0:
                        wl.append(tFo[4 * (q - 1) + t8])
                    def fya(ex=ex, oh=oh, t8=t8, bank=bank, mm=mm):
                        def f(e):
                            dstc = slice(1024 * t8 + 512 * oh,
                                         1024 * t8 + 512 * (oh + 1))
                            if ex == 0:
                                return e.tensor_scalar(
                                    s["yacc"][:, dstc], PB[bank][:, 0:512],
                                    s["combine"][:, 8 * mm + ex:8 * mm + ex + 1],
                                    None, ALU.mult)
                            return e.scalar_tensor_tensor(
                                s["yacc"][:, dstc], PB[bank][:, 0:512],
                                s["combine"][:, 8 * mm + ex:8 * mm + ex + 1],
                                s["yacc"][:, dstc], ALU.mult, ALU.add)
                        return f
                    tYa[key] = E("dv", fya(), wait=wl)
                wdslotTok[wdidx] = tDn[(q, ex, oh, 3)]
                wdidx += 1
        for t8 in range(4):
            mm = 4 * q + t8
            xb = s["xb0"] if mm % 2 == 0 else s["xb1"]
            wl = [tX2s[mm]]
            if mm >= 2:
                wl.append(tFo.get(mm - 2))
            else:
                wl.append(tX2v[M - 1])
            tX5[mm] = E("gp", (lambda mm=mm, xb=xb: (lambda e: [e.dma_start(
                out=xb, in_=x2s[T * mm:T * (mm + 1), :])]))(), wait=wl, dma=1)
            wl = [tYa[(q, EXP - 1, 0, t8)], tYa[(q, EXP - 1, 1, t8)], tX5[mm], tGB,
                  tL2s[M - 1]]
            if mm >= 1:
                wl.append(tOutD.get(mm - 1))
            tFoA = E("dv", (lambda t8=t8: (lambda e: e.tensor_tensor(
                s["x2f"], s["yacc"][:, 1024 * t8:1024 * (t8 + 1)],
                s["modG"], ALU.mult)))(), wait=wl)
            tFo[mm] = E("dv", (lambda xb=xb: (lambda e: e.tensor_tensor(
                s["x2f"], s["x2f"], xb, ALU.add)))(), wait=[tFoA])
            tOutD[mm] = E("sy", (lambda mm=mm: (lambda e: [e.dma_start(
                out=out[T * mm:T * (mm + 1), :], in_=s["x2f"])]))(),
                wait=[tFo[mm]], dma=1)

    if "h1" in tapd:
        E("sy", lambda e: [e.dma_start(out=tapd["h1"][:, kc * H1W:(kc + 1) * H1W],
                                       in_=h1[kc]) for kc in range(KC)],
          wait=[tH1done], dma=KC)
    if "dk" in tapd:
        E("sy", lambda e: [e.dma_start(out=tapd["dk"][:, 0:2048], in_=s["wmat"]),
                           e.dma_start(out=tapd["dk"][:, 2048:3072], in_=s["WBCm"]),
                           e.dma_start(out=tapd["dk"][:, 3072:4096], in_=s["WKM"])],
          wait=[tWmat, tWBC, tWKM], dma=3)
    if "dk2" in tapd:
        E("sy", lambda e: [e.dma_start(out=tapd["dk2"][:, 0:8], in_=s["LNDCOL"]),
                           e.dma_start(out=tapd["dk2"][:, 8:16], in_=s["WSCOL"]),
                           e.dma_start(out=tapd["dk2"][:, 16:80], in_=s["C8P"]),
                           e.dma_start(out=tapd["dk2"][:, 80:96], in_=s["lndbc"]),
                           e.dma_start(out=tapd["dk2"][:, 96:112], in_=s["UCOL"]),
                           e.dma_start(out=tapd["dk2"][:, 112:120], in_=s["CAc"]),
                           e.dma_start(out=tapd["dk2"][:, 120:128], in_=s["SAc"]),
                           e.dma_start(out=tapd["dk2"][:, 128:136], in_=s["CBc"]),
                           e.dma_start(out=tapd["dk2"][:, 136:144], in_=s["SBc"]),
                           e.dma_start(out=tapd["dk2"][:, 144:152], in_=s["basebc"])],
          wait=[tBr, tC8e, tCAc, tSAc, tCBc, tSBc, tBaseE], dma=10)
    if "st" in tapd:
        E("sy", lambda e: [e.dma_start(out=tapd["st"], in_=s["spair"])],
          wait=[tCI], dma=1)
    if "gn" in tapd:
        E("sy", lambda e: [e.dma_start(out=tapd["gn"], in_=s["rfm"])],
          wait=[tGng[(M - 1, PAIRS - 1)]], dma=1)
    if "gs" in tapd:
        E("sy", lambda e: [e.dma_start(out=tapd["gs"][:, 0:128],
                                       in_=s["stats"][:, 0:128]),
                           e.dma_start(out=tapd["gs"][:, 128:256],
                                       in_=s["rstd"])],
          wait=[tRst[M - 1]], dma=2)
    if "ccg" in tapd:
        E("sy", lambda e: [e.dma_start(out=tapd["ccg"][0:8, :], in_=ccout),
                           e.dma_start(out=tapd["ccg"][8:9, :], in_=ccin)],
          wait=[tCC], dma=2)
    if "rt" in tapd:
        E("sy", lambda e: [e.dma_start(out=tapd["rt"], in_=s["pgst"])],
          wait=[tRo4[M - 1]], dma=1)
    if "cmb" in tapd:
        E("sy", lambda e: [e.dma_start(out=tapd["cmb"], in_=s["combine"])],
          wait=[tCmb[M - 1]], dma=1)
    if "x2" in tapd:
        E("sy", lambda e: [e.dma_start(out=tapd["x2"], in_=x2s)],
          wait=[tX2s[M - 1]], dma=1)
    if "hid" in tapd:
        E("sy", lambda e: [e.dma_start(out=tapd["hid"], in_=WS[4])],
          wait=[tG5e[(0, HKC - 1)]], dma=1)
    if "y" in tapd:
        E("sy", lambda e: [e.dma_start(out=tapd["y"], in_=s["ych"])],
          wait=[tYe[(M - 1, PAIRS - 1)]], dma=1)
    if "gg" in tapd:
        E("sy", lambda e: [e.dma_start(out=tapd["gg"], in_=s["gch"])],
          wait=[tGe[(M - 1, KC - 1)]], dma=1)

    sems = {k: nc.semaphore("sem_" + k).__enter__() for k in Prog.ENGS}

    def replay(ename):
        def go(e):
            cnt = 0
            for eng, waits, dma, fn in pr.steps:
                if eng != ename:
                    continue
                for wn, wv in waits:
                    e.wait_ge(sems[wn], wv)
                res = fn(e)
                if dma:
                    res = res if isinstance(res, (list, tuple)) else [res]
                    assert len(res) == dma, (ename, len(res), dma)
                    for r in res:
                        r.then_inc(sems[ename], 16)
                    cnt += 16 * dma
                else:
                    res.then_inc(sems[ename], 1)
                    cnt += 1
            assert cnt == pr.cnt[ename], (ename, cnt, pr.cnt[ename])
            if ename == "sy":
                e.wait_ge(sems["sy"], pr.cnt["sy"])
        return go

    with nc.Block() as block:
        block.sync(replay("sy"))
        block.gpsimd(replay("gp"))
        block.scalar(replay("ac"))
        block.vector(replay("dv"))
        block.tensor(replay("pe"))

    return nc


_NC_CACHE = {}


def kernel(**inputs):
    key = tuple(DEBUG_TAPS)
    if key not in _NC_CACHE:
        _NC_CACHE[key] = build(key)
    nc = _NC_CACHE[key]
    consts = _host_consts()
    w = _shared_weights(inputs)
    in_maps = [_prep_core_inputs(c, inputs, consts, w) for c in range(8)]
    res = run_bass_kernel_spmd(nc, in_maps, core_ids=list(range(8)))
    kernel.last_results = res.results
    full = np.empty((2, 16384, D), dtype=np.float32)
    for c in range(8):
        b, j = c // 4, c % 4
        full[b, j * TOK:(j + 1) * TOK] = res.results[c]["out"]
    return full


# version 05 (final)
# speedup vs baseline: 1.1829x; 1.1084x over previous
import numpy as np
import ml_dtypes

import concourse.bass as bass
import concourse.mybir as mybir
from concourse.bass_utils import run_bass_kernel_spmd

F32 = mybir.dt.float32
BF16 = mybir.dt.bfloat16
AF = mybir.ActivationFunctionType
ALU = mybir.AluOpType
BF16NP = ml_dtypes.bfloat16

D = 1024
KC = 8
T = 128
M = 32
TOK = 4096
PAIRS = 8
DH = 2048
HKC = 16
EXP = 8
NE = 8
ET = 512
CAP = 5120.0
EPS1 = 1e-5
CGN = 4096.0 * 64e-5
BTC = 129
BNDC = 128
H1W = 128 + 1 + TOK

DEBUG_TAPS = []


def _host_consts():
    c = {}
    i = np.arange(128, dtype=np.float32)
    c["IDN"] = np.eye(128, dtype=BF16NP)
    c["EYE"] = np.eye(128, dtype=np.float32)
    c["M1"] = np.maximum(i[None, :] - i[:, None] - 1.0, 0.0).astype(np.float32)
    c["TRI"] = (i[None, :] > i[:, None]).astype(BF16NP)
    c["LT"] = (i[:, None] <= i[None, :]).astype(BF16NP)
    c["ONESB"] = np.ones((128, 128), dtype=BF16NP)
    c["REVJ"] = np.ascontiguousarray((127.0 - i)[:, None])
    c["IOTAI"] = np.tile(i[None, :], (128, 1)).astype(np.float32)
    c["PERT"] = np.ascontiguousarray(
        (1.0 - 1e-6 * np.arange(8, dtype=np.float32))[None, :].repeat(128, 0))
    sels = np.zeros((128, 128), dtype=np.float32)
    for cc in range(8):
        sels[0:64, 16 * cc + 2 * cc] = 1.0
        sels[64:128, 16 * cc + 2 * cc + 1] = 1.0
    c["SELS"] = sels.astype(BF16NP)
    selts = np.zeros((16, 8 * 128), dtype=np.float32)
    for cc in range(8):
        selts[2 * cc, 128 * cc:128 * cc + 64] = 1.0
        selts[2 * cc + 1, 128 * cc + 64:128 * cc + 128] = 1.0
    c["SELTS64"] = (64.0 * selts).astype(BF16NP)
    return c


def _shared_weights(inputs):
    w = {}
    w["WrT"] = np.ascontiguousarray(np.asarray(inputs["L0_attn_W_r"], np.float32).T)
    w["WkT"] = np.ascontiguousarray(np.asarray(inputs["L0_attn_W_k"], np.float32).T)
    w["WvT"] = np.ascontiguousarray(np.asarray(inputs["L0_attn_W_v"], np.float32).T)
    w["WgT"] = np.ascontiguousarray(np.asarray(inputs["L0_attn_W_g"], np.float32).T)
    w["WoT"] = np.ascontiguousarray(np.asarray(inputs["L0_attn_W_o"], np.float32).T)
    w["WgateT"] = np.ascontiguousarray(np.asarray(inputs["L0_ffn_w_gate"], np.float32).T)
    w["WdT"] = np.ascontiguousarray(
        np.asarray(inputs["L0_ffn_w_down"], np.float32).transpose(0, 2, 1))
    w["WrtrT"] = np.ascontiguousarray(np.asarray(inputs["L0_ffn_w_router"], np.float32).T)
    w["wmod"] = np.ascontiguousarray(np.asarray(inputs["L0_wmod"], np.float32))
    tmix = np.empty((128, 32), dtype=np.float32)
    for vi, key in enumerate(["time_mix_r", "time_mix_k", "time_mix_v", "time_mix_g"]):
        v = np.asarray(inputs[f"L0_attn_{key}"], np.float32).reshape(D)
        tmix[:, vi * 8:(vi + 1) * 8] = v.reshape(KC, 128).T
    w["tmix"] = tmix
    w["tdrow"] = np.asarray(inputs["L0_attn_time_decay"], np.float32).reshape(1, 16).copy()
    w["tfrow"] = np.asarray(inputs["L0_attn_time_faaaa"], np.float32).reshape(1, 16).copy()
    w["lnw"] = np.ascontiguousarray(
        np.asarray(inputs["L0_attn_ln_w"], np.float32).reshape(KC, 128).T)
    w["lnb"] = np.ascontiguousarray(
        np.asarray(inputs["L0_attn_ln_b"], np.float32).reshape(KC, 128).T)
    return w


def _prep_core_inputs(cid, inputs, consts, w):
    b, j = cid // 4, cid % 4
    x = np.asarray(inputs["L0_attn_xq"], np.float32)
    xs = np.empty((1 + TOK, D), dtype=np.float32)
    t0 = j * TOK
    xs[0] = 0.0 if j == 0 else x[b, t0 - 1]
    xs[1:] = x[b, t0:t0 + TOK]
    m = dict(w)
    m["xs"] = xs
    m["bnd"] = np.full((128, 1), 0.0 if j == 0 else 1.0, dtype=np.float32)
    pm = np.zeros((8, 128), dtype=np.float32)
    pm[np.arange(8) < cid, :] = 1.0
    m["pmask"] = pm
    spow = np.zeros((16, 8), dtype=np.float32)
    smask = np.zeros((16, 8), dtype=np.float32)
    for r in range(8):
        br, jr = r // 4, r % 4
        if br == b and jr < j:
            spow[:, r] = 4096.0 * (j - 1 - jr)
            smask[:, r] = 1.0
    m["spow"] = spow
    m["smask"] = smask
    m["crow"] = np.ascontiguousarray(np.asarray(inputs["L0_c"], np.float32)[b:b + 1, :])
    m.update(consts)
    return m


class Prog:
    ENGS = ("pe", "dv", "ac", "gp", "sy")

    def __init__(self):
        self.steps = []
        self.cnt = {k: 0 for k in self.ENGS}

    def emit(self, eng, fn, wait=(), dma=0):
        wl = {}
        for tkn in wait:
            if tkn is None:
                continue
            en, v = tkn
            if v > 0 and (en not in wl or wl[en] < v):
                wl[en] = v
        self.steps.append((eng, sorted(wl.items()), dma, fn))
        self.cnt[eng] += 16 * dma if dma else 1
        return (eng, self.cnt[eng])


def build(taps=()):
    taps = set(taps)
    nc = bass.Bass("TRN2", target_bir_lowering=False)

    d = {}
    def din(name, shape, dt=F32):
        d[name] = nc.dram_tensor(name, shape, dt, kind="ExternalInput").ap()
    din("xs", [1 + TOK, D])
    din("bnd", [128, 1]); din("pmask", [8, 128])
    din("spow", [16, 8]); din("smask", [16, 8])
    for w in ("WrT", "WkT", "WvT", "WgT", "WoT"):
        din(w, [D, D])
    din("WgateT", [D, DH]); din("WdT", [EXP, DH, D]); din("WrtrT", [D, EXP])
    din("wmod", [D, 6 * D]); din("crow", [1, D])
    din("tmix", [128, 32]); din("tdrow", [1, 16]); din("tfrow", [1, 16])
    din("lnw", [128, 8]); din("lnb", [128, 8])
    din("IDN", [128, 128], BF16); din("EYE", [128, 128]); din("M1", [128, 128])
    din("TRI", [128, 128], BF16); din("IOTAI", [128, 128]); din("REVJ", [128, 1])
    din("LT", [128, 128], BF16); din("ONESB", [128, 128], BF16)
    din("PERT", [128, 8]); din("SELS", [128, 128], BF16)
    din("SELTS64", [16, 8 * 128], BF16)

    out = nc.dram_tensor("out", [TOK, D], F32, kind="ExternalOutput").ap()
    x2s = nc.dram_tensor("x2s", [TOK, D], BF16, kind="Internal").ap()
    bscr = nc.dram_tensor("bscr", [1, 12288], F32, kind="Internal").ap()
    ccin = nc.dram_tensor("ccin", [1, 128 * 512], F32, kind="Internal").ap()
    ccout = nc.dram_tensor("ccout", [8, 128 * 512], F32, kind="Internal").ap()
    ccin2 = nc.dram_tensor("ccin2", [1, 8], F32, kind="Internal").ap()
    ccout2 = nc.dram_tensor("ccout2", [8, 8], F32, kind="Internal").ap()

    tap_shapes = {
        "h1": ([128, KC * H1W], BF16),
        "dk": ([128, 16 * 128 + 8 * 128 + D], BF16),
        "dk2": ([128, 160], F32),
        "st": ([128, 512], F32),
        "y": ([128, D], BF16),
        "gg": ([128, D], BF16),
        "x2": ([TOK, D], BF16),
        "rt": ([128, 32 * 2], F32),
        "cmb": ([128, 32 * 8], F32),
        "hid": ([128, 8192], BF16),
        "ccg": ([9, 128 * 512], F32),
        "gn": ([128, D], BF16),
        "gs": ([16, 256], F32),
        "p2a0": ([128, 4 * D + 8], BF16),
        "bdg": ([128, 32], F32),
        "bdh": ([128, D + 8], BF16),
    }
    tapd = {}
    for tname in taps:
        shape, dt = tap_shapes[tname]
        tapd[tname] = nc.dram_tensor("tap_" + tname, shape, dt, kind="ExternalOutput").ap()

    s = {}
    def sba(name, shape, dt):
        s[name] = nc.alloc_sbuf_tensor(name, shape, dt).ap()
        return s[name]

    h1 = [sba(f"h1_{kc}", [128, H1W], BF16) for kc in range(KC)]
    WS = [sba(f"ws{i}", [128, 8192], BF16) for i in range(5)]
    sba("yacc", [128, 4 * 1024], BF16)
    sba("spair", [128, 512], F32)
    sba("sbf", [128, 512], BF16)
    sba("wmat", [128, 16 * 128], BF16)
    sba("WBCm", [128, 8 * 128], BF16)
    sba("WKM", [128, D], BF16)
    sba("lndbc", [128, 16], F32)
    sba("LNDCOL", [128, 8], F32)
    sba("WSCOL", [128, 8], F32)
    sba("UCOL", [128, 16], F32)
    sba("lrow", [1, 64], F32)
    sba("lcol", [16, 4], F32)
    sba("C8", [16, 8], F32)
    sba("C8P", [128, 64], F32)
    sba("modfm", [128, 48], F32)
    sba("combine", [128, 32 * 8], F32)
    sba("CAb", [128, 8], F32); sba("SAb", [128, 8], F32)
    sba("modG", [128, D], BF16)
    sba("scb", [128, 8], BF16)
    sba("tmixS", [128, 32], F32)
    sba("lnwS", [128, 8], F32); sba("lnbS", [128, 8], F32)
    sba("tdS", [1, 16], F32); sba("tfS", [1, 16], F32)
    sba("bndS", [128, 1], F32)
    sba("pmaskS", [8, 128], F32)
    sba("spowS", [16, 8], F32); sba("smaskS", [16, 8], F32)
    sba("rtrS", [128, 64], BF16)
    for nm, dt in [("IDNs", BF16),
                   ("LTs", BF16), ("ONESBs", BF16), ("SELSs", BF16)]:
        sba(nm, [128, 128], dt)
    sba("REVJs", [128, 1], F32)
    sba("PERTs", [128, 8], F32)
    sba("SELTS64s", [16, 8 * 128], BF16)
    sba("xb0", [128, D], BF16); sba("xb1", [128, D], BF16)
    sba("h1tm", [128, D], BF16)
    sba("dmix", [128, D], BF16)
    sba("mixA", [128, D], BF16); sba("mixB", [128, D], BF16)
    sba("rfm", [128, D], BF16)
    sba("kfm", [128, D], BF16)
    sba("ktm", [128, D], BF16)
    sba("vtm", [128, D], BF16)
    sba("rrb", [128, D], BF16)
    sba("ych", [128, D], BF16)
    sba("gch", [128, D], BF16)
    sba("zt", [128, 128], F32)
    sba("stats", [16, 128], F32)
    sba("rstd", [16, 128], F32)
    sba("rstdb", [16, 128], BF16); sba("s1rb", [16, 128], BF16)
    sba("x2f", [128, D], F32)
    sba("lncol", [128, 16], F32)
    sba("oh1st", [128, 32 * 8], BF16); sba("oh2st", [128, 32 * 8], BF16)
    sba("pgst", [128, 32 * 2], F32)
    sba("ggst", [128, 32 * 2], F32)
    sba("ohsb", [128, 8], BF16)
    sba("gts", [128, 8], F32); sba("pgt", [128, 8], F32); sba("srt", [128, 8], F32)
    sba("r8a", [128, 8], F32)
    sba("cnt", [128, 8], F32)
    sba("basebc", [128, 8], F32)
    sba("cnts", [8, 8], F32)
    sba("rcol", [128, 16], F32)
    sba("epsc", [128, 2], F32)

    assert nc.sbuf_bytes_remaining >= 0, \
        f"SBUF over budget by {-nc.sbuf_bytes_remaining} B/partition"
    build.sbuf_slack = nc.sbuf_bytes_remaining

    s["IOTAIs"] = s["x2f"][:, 0:128]
    s["SAc"] = s["modfm"][:, 0:8]
    s["CAc"] = s["modfm"][:, 8:16]
    s["SBc"] = s["modfm"][:, 24:32]
    s["CBc"] = s["modfm"][:, 32:40]
    s["TRIs"] = s["x2f"][:, 512:576].bitcast(BF16)
    s["attw"] = s["zt"].bitcast(BF16)
    s["M1s"] = s["x2f"][:, 128:256]
    s["EYEs"] = s["x2f"][:, 256:384]
    s["e1"] = s["x2f"][:, 384:512]
    s["sq1"] = s["zt"][0:16, 0:128]
    s["mixC"] = s["yacc"][:, 0:1024]
    s["mixD"] = s["yacc"][:, 1024:2048]

    PB = [nc.alloc_psum_tensor(f"pb{i}", [128, 512], F32).ap() for i in range(8)]
    pt = [PB[0].bitcast(BF16)[:, 0:128], PB[1].bitcast(BF16)[:, 0:128]]
    ATT = [PB[2][:, 0:128], PB[3][:, 0:128]]
    POUT = PB[3][:, 128:256]
    PVV = [PB[6][:, 0:512], PB[7][:, 0:512]]
    PAX2 = PB[4][:, 128:256]
    PRT = PB[6][:, 256:264]
    PCS1 = PB[6][:, 264:272]
    PCS2 = PB[6][:, 272:280]
    PBASE = PB[6][:, 280:288]
    PS1 = PB[6][0:16, 0:128]
    PS2 = PB[6][0:16, 128:256]
    PB1 = PB[6][:, 256:384]
    PB0 = PB[6][:, 384:512]
    PRJ = [PB[4], PB[5]]
    MOD = [PB[6][:, 0:48], PB[6][:, 64:112]]
    PWO = [PB[7], PB[7]]

    def h1c(mm, off=0, n=T):
        return slice(BTC + T * mm + off, BTC + T * mm + off + n)

    def h2c(mm, off=0, n=T):
        return slice(T * mm + off, T * mm + off + n)

    pr = Prog()
    E = pr.emit

    def dram_ap(off, dims):
        return bass.AP(bscr.tensor, off, dims)

    ld_map = [("bnd", "bndS"), ("pmask", "pmaskS"), ("spow", "spowS"),
              ("smask", "smaskS"), ("tmix", "tmixS"), ("lnw", "lnwS"),
              ("lnb", "lnbS"), ("tdrow", "tdS"), ("tfrow", "tfS"),
              ("IDN", "IDNs"), ("REVJ", "REVJs"), ("LT", "LTs"),
              ("ONESB", "ONESBs"), ("PERT", "PERTs"), ("SELS", "SELSs"),
              ("SELTS64", "SELTS64s")]
    tLD = E("sy", lambda e: [e.dma_start(out=s[dst], in_=d[src])
                             for src, dst in ld_map], dma=len(ld_map))
    tEps = E("dv", lambda e: (e.memset(s["epsc"][:, 0:1], EPS1),
                              e.memset(s["epsc"][:, 1:2], CGN))[-1], wait=[])
    tCrow = E("sy", lambda e: [e.dma_start(
        out=s["x2f"][0:2, 0:512],
        in_=bass.AP(d["crow"].tensor, 0, [[512, 2], [1, 512]]))], dma=1)

    def fwk(slot, col0, w, r0):
        return lambda e: [e.dma_start(out=WS[slot][:, col0:col0 + 1024],
                                      in_=d[w][r0:r0 + 128, :])]
    tw = []
    for kc in range(KC):
        tw.append(E("gp", fwk(0, kc * 1024, "WkT", 128 * kc), dma=1))
    for kc in range(KC):
        tw.append(E("gp", fwk(1, kc * 1024, "WvT", 128 * kc), dma=1))
    tW2A = tw[-1]

    tLnd = E("ac", lambda e: (e.activation(s["lrow"][0:1, 0:16], s["tdS"], AF.Exp),
                              e.mul(s["lrow"][0:1, 0:16], s["lrow"][0:1, 0:16], -1.0))[-1],
             wait=[tLD])
    tWsr = E("ac", lambda e: e.activation(s["lrow"][0:1, 32:48], s["lrow"][0:1, 0:16],
                                          AF.Exp, scale=128.0), wait=[tLnd])
    tSilc = E("ac", lambda e: e.activation(s["x2f"][0:2, 0:512], s["x2f"][0:2, 0:512],
                                           AF.Silu), wait=[tCrow])
    tBs1 = E("sy", lambda e: [
        e.dma_start(out=bscr[0:1, 0:16], in_=s["lrow"][0:1, 0:16]),
        e.dma_start(out=bscr[0:1, 16:32], in_=s["tfS"]),
        e.dma_start(out=bscr[0:1, 32:48], in_=s["lrow"][0:1, 32:48]),
        e.dma_start(out=bass.AP(bscr.tensor, 48, [[512, 2], [1, 512]]),
                    in_=s["x2f"][0:2, 0:512]),
    ], wait=[tLnd, tWsr, tSilc], dma=4)
    def fbr(e):
        res = [
            e.dma_start(out=s["lcol"][:, 0:1], in_=dram_ap(0, [[1, 16], [1, 1]]),
                        allow_slow_non_contiguous=True),
            e.dma_start(out=s["lndbc"], in_=dram_ap(0, [[0, 128], [1, 16]])),
            e.dma_start(out=s["UCOL"], in_=dram_ap(16, [[0, 128], [1, 16]])),
        ]
        for cc in range(8):
            for hb in range(2):
                rows = slice(64 * hb, 64 * (hb + 1))
                res.append(e.dma_start(
                    out=s["LNDCOL"][rows, cc:cc + 1],
                    in_=dram_ap(2 * cc + hb, [[0, 64], [1, 1]]),
                    allow_slow_non_contiguous=True))
                res.append(e.dma_start(
                    out=s["WSCOL"][rows, cc:cc + 1],
                    in_=dram_ap(32 + 2 * cc + hb, [[0, 64], [1, 1]]),
                    allow_slow_non_contiguous=True))
        return res
    tBr = E("sy", fbr, wait=[tBs1], dma=3 + 32)
    tScb = E("gp", lambda e: [e.dma_start(
        out=s["scb"][:, kc:kc + 1],
        in_=dram_ap(48 + 128 * kc, [[1, 128], [1, 1]])) for kc in range(KC)],
        wait=[tBs1], dma=KC)
    tWKa = E("ac", lambda e: [e.activation(s["zt"][:, hh:hh + 1], s["REVJs"],
                                           AF.Exp, scale=s["lndbc"][:, hh:hh + 1])
                              for hh in range(16)][-1], wait=[tBr, tLD])
    tWKM = E("dv", lambda e: [e.tensor_scalar(
        s["WKM"][:, 64 * hh:64 * (hh + 1)], s["ONESBs"][:, 0:64],
        s["zt"][:, hh:hh + 1], None, ALU.mult) for hh in range(16)][-1],
        wait=[tWKa, tLD])
    tCload = E("sy", lambda e: [
        e.dma_start(out=s["IOTAIs"], in_=d["IOTAI"]),
        e.dma_start(out=s["M1s"], in_=d["M1"]),
        e.dma_start(out=s["EYEs"], in_=d["EYE"]),
        e.dma_start(out=s["TRIs"], in_=d["TRI"]),
    ], wait=[tWKM], dma=4)
    tWBC = E("ac", lambda e: [e.activation(s["WBCm"][:, 128 * cc:128 * (cc + 1)],
                                           s["IOTAIs"], AF.Exp,
                                           scale=s["LNDCOL"][:, cc:cc + 1])
                              for cc in range(PAIRS)][-1], wait=[tBr, tLD, tCload])
    tWm = []
    for hh in range(16):
        wl = [tBr, tLD, tCload] + ([tWm[-1]] if tWm else [])
        tWmA = E("ac", (lambda hh=hh: (lambda e: e.activation(
            s["e1"], s["M1s"], AF.Exp, scale=s["lndbc"][:, hh:hh + 1])))(), wait=wl)
        tWm.append(E("dv", (lambda hh=hh: (lambda e: (
            e.tensor_tensor(s["zt"], s["e1"], s["TRIs"], ALU.mult),
            e.scalar_tensor_tensor(s["wmat"][:, 128 * hh:128 * (hh + 1)], s["EYEs"],
                                   s["UCOL"][:, hh:hh + 1], s["zt"],
                                   ALU.mult, ALU.add))[-1]))(), wait=[tWmA]))
    tWmat = tWm[-1]

    tC8a = E("dv", lambda e: e.tensor_scalar(s["C8"], s["spowS"], s["lcol"][:, 0:1],
                                             None, ALU.mult), wait=[tBr, tLD])
    tC8b = E("ac", lambda e: e.activation(s["C8"], s["C8"], AF.Exp), wait=[tC8a])
    tC8c = E("dv", lambda e: e.tensor_tensor(s["C8"], s["C8"], s["smaskS"], ALU.mult),
             wait=[tC8b])
    tC8d = E("sy", lambda e: [e.dma_start(
        out=bass.AP(bscr.tensor, 2048, [[8, 16], [1, 8]]), in_=s["C8"])],
        wait=[tC8c], dma=1)
    def fc8e(e):
        res = []
        for hb in range(2):
            rows = slice(64 * hb, 64 * (hb + 1))
            res.append(e.dma_start(
                out=s["C8P"][rows, :].rearrange("p (c r) -> p c r", c=8),
                in_=dram_ap(2048 + 8 * hb, [[0, 64], [16, 8], [1, 8]]),
                allow_slow_non_contiguous=True))
        return res
    tC8e = E("sy", fc8e, wait=[tC8d], dma=2)

    tMz = E("dv", lambda e: e.memset(s["modfm"], 0.0), wait=[])
    tModMM = [None, None]
    tModAdd = [tMz, tMz]
    for kc in range(KC):
        slot = 2 + (kc % 2)
        bank = kc % 2
        wl = [tLD, tScb] if kc < 2 else [tModMM[0], tModMM[1]]
        tload = E("gp", (lambda slot=slot, kc=kc: (lambda e: [e.dma_start(
            out=WS[slot][:, 0:6144], in_=d["wmod"][128 * kc:128 * (kc + 1), :])]))(),
            wait=wl, dma=1)
        def fmodmm(slot=slot, kc=kc, bank=bank):
            def f(e):
                last = None
                for ob in range(48):
                    last = e.matmul(MOD[bank][:, ob:ob + 1],
                                    WS[slot][:, 128 * ob:128 * (ob + 1)],
                                    s["scb"][:, kc:kc + 1],
                                    start=True, stop=True)
                return last
            return f
        wl = [tload, tScb, tModAdd[0], tModAdd[1]]
        tModMM[bank] = E("pe", fmodmm(), wait=wl)
        tModAdd[bank] = E("dv", (lambda bank=bank: (lambda e: e.tensor_tensor(
            s["modfm"], s["modfm"], MOD[bank], ALU.add)))(),
            wait=[tModMM[bank], tMz])
    tModDone = E("dv", lambda e: (
        e.tensor_scalar(s["modfm"][:, 8:16], s["modfm"][:, 8:16], 1.0, None, ALU.add),
        e.tensor_scalar(s["modfm"][:, 32:40], s["modfm"][:, 32:40], 1.0, None,
                        ALU.add))[-1], wait=[tModAdd[0], tModAdd[1]])

    def emit_clamp(c0, wait):
        sc2 = s["zt"][:, 0:8]
        sgn = s["zt"][:, 8:16]
        mg = s["modfm"][:, c0:c0 + 8]
        tA = E("dv", lambda e: (
            e.tensor_scalar(sgn, mg, 0.0, None, ALU.is_ge),
            e.tensor_scalar(sc2, mg, -1.0, None, ALU.mult))[-1], wait=wait)
        tB = E("dv", lambda e: (
            e.tensor_scalar(sgn, sgn, 2.0, -1.0, ALU.mult, ALU.add),
            e.tensor_tensor(sc2, sc2, mg, ALU.max))[-1], wait=[tA])
        tC = E("dv", lambda e: e.tensor_scalar(sc2, sc2, 0.25, None, ALU.max),
               wait=[tB])
        return E("dv", lambda e: e.tensor_tensor(mg, sgn, sc2, ALU.mult),
                 wait=[tC])
    tCl1 = emit_clamp(16, [tModDone, tWmat])
    tCl2 = emit_clamp(40, [tCl1])
    def fmodb(e):
        res = []
        for i, c0 in enumerate((16, 40)):
            for cc in range(8):
                res.append(e.dma_start(
                    out=bscr[0:1, 4096 + 1024 * i + 128 * cc:
                             4096 + 1024 * i + 128 * (cc + 1)],
                    in_=s["modfm"][:, c0 + cc:c0 + cc + 1]))
        return res
    tModB = E("gp", fmodb, wait=[tCl2], dma=16)

    tCAc = tSAc = tCBc = tSBc = tModDone
    tBndCA = E("dv", lambda e: (
        e.tensor_scalar(s["CAb"], s["CAc"], s["bndS"], None, ALU.mult),
        e.tensor_scalar(s["SAb"], s["SAc"], s["bndS"], None, ALU.mult))[-1],
        wait=[tCAc, tSAc, tLD])
    tGA = E("gp", lambda e: [e.dma_start(
        out=s["modG"], in_=dram_ap(4096, [[0, 128], [1, D]]))],
        wait=[tModB], dma=1)

    tXL = {}
    tH1tm = {}
    tTr = {}
    tEv = {}
    tSqA = {}
    tSt1 = {}
    ptTok = [None, None]

    tXL[-1] = E("gp", lambda e: [e.dma_start(out=s["xb1"][0:1, :],
                                             in_=d["xs"][0:1, :])], dma=1)
    tTapB = None
    for mm in range(-1, M):
        if mm >= 0:
            xb = s["xb0"] if mm % 2 == 0 else s["xb1"]
            wl = [tH1tm[mm - 2]] if mm >= 2 else ([tH1tm[-1]] if mm == 1 else [])
            tXL[mm] = E("gp", (lambda mm=mm, xb=xb: (lambda e: [e.dma_start(
                out=xb, in_=d["xs"][1 + T * mm:1 + T * (mm + 1), :])]))(),
                wait=wl, dma=1)
        src = s["xb1"] if mm == -1 else (s["xb0"] if mm % 2 == 0 else s["xb1"])
        rows = slice(0, 1) if mm == -1 else slice(0, 128)
        wl = [tXL[mm], tLD] + ([tSt1[mm - 1]] if (mm - 1) in tSt1 else [])
        if mm == 0 and tTapB is not None:
            wl.append(tTapB)
        tSqA[mm] = E("ac", (lambda src=src, rows=rows, mm=mm: (lambda e: e.activation(
            s["dmix"][rows, :], src[rows, :], AF.Square,
            accum_out=s["lncol"][rows, 2:3])))(), wait=wl)
        tSt0 = E("dv", (lambda src=src, rows=rows: (lambda e: e.reduce_sum(
            s["lncol"][rows, 0:1], src[rows, :], mybir.AxisListType.X)))(),
            wait=[tXL[mm], tLD] + ([tTapB] if (mm == 0 and tTapB) else []))
        def fst1a(rows=rows):
            def f(e):
                e.tensor_scalar(s["lncol"][rows, 12:13], s["lncol"][rows, 0:1],
                                1.0 / 1024, None, ALU.mult)
                return e.tensor_scalar(s["lncol"][rows, 11:12],
                                       s["lncol"][rows, 2:3],
                                       1.0 / 1024, None, ALU.mult)
            return f
        t1a = E("dv", fst1a(), wait=[tSt0, tSqA[mm]])
        t1b = E("dv", (lambda rows=rows: (lambda e: e.tensor_tensor(
            s["lncol"][rows, 1:2], s["lncol"][rows, 12:13],
            s["lncol"][rows, 12:13], ALU.mult)))(), wait=[t1a])
        tSt1[mm] = E("dv", (lambda rows=rows: (lambda e: e.tensor_tensor(
            s["lncol"][rows, 13:14], s["lncol"][rows, 11:12],
            s["lncol"][rows, 1:2], ALU.subtract)))(), wait=[t1b])
        tQ = E("ac", (lambda rows=rows: (lambda e: e.activation(
            s["lncol"][rows, 14:15], s["lncol"][rows, 13:14], AF.Sqrt,
            bias=s["epsc"][rows, 0:1])))(), wait=[tSt1[mm], tEps])
        tRc = E("dv", (lambda rows=rows: (lambda e: e.reciprocal(
            s["lncol"][rows, 15:16], s["lncol"][rows, 14:15])))(), wait=[tQ])
        def fnx(src=src, rows=rows):
            def f(e):
                return e.tensor_scalar(s["h1tm"][rows, :], src[rows, :],
                                       s["lncol"][rows, 12:13],
                                       s["lncol"][rows, 15:16],
                                       ALU.subtract, ALU.mult)
            return f
        wl = [tRc] + ([tTr[(mm - 1, KC - 1)]] if mm >= 0 else [])
        tH1tm[mm] = E("dv", fnx(), wait=wl)
        for kc in range(KC):
            gi = (mm + 1) * KC + kc
            bank = gi % 2
            wl = [tH1tm[mm], tLD]
            if ptTok[bank] is not None:
                wl.append(ptTok[bank])
            if mm == -1:
                fn = (lambda kc=kc, bank=bank: (lambda e: e.transpose(
                    pt[bank][:, 0:1], s["h1tm"][0:1, 128 * kc:128 * (kc + 1)],
                    s["IDNs"][0:1, 0:1])))()
            else:
                fn = (lambda kc=kc, bank=bank: (lambda e: e.transpose(
                    pt[bank], s["h1tm"][:, 128 * kc:128 * (kc + 1)], s["IDNs"])))()
            tTr[(mm, kc)] = E("pe", fn, wait=wl)
            if mm == -1:
                fe = (lambda kc=kc, bank=bank: (lambda e: e.tensor_scalar(
                    h1[kc][:, BNDC:BNDC + 1], pt[bank][:, 0:1],
                    s["CAb"][:, kc:kc + 1], s["SAb"][:, kc:kc + 1],
                    ALU.mult, ALU.add)))()
                wl2 = [tTr[(mm, kc)], tBndCA]
            else:
                fe = (lambda kc=kc, bank=bank, mm=mm: (lambda e: e.tensor_scalar(
                    h1[kc][:, h1c(mm)], pt[bank],
                    s["CAc"][:, kc:kc + 1], s["SAc"][:, kc:kc + 1],
                    ALU.mult, ALU.add)))()
                wl2 = [tTr[(mm, kc)], tCAc, tSAc]
            ptTok[bank] = tEv[(mm, kc)] = E("dv", fe, wait=wl2)
        if mm == -1 and "bdg" in tapd:
            tTapB = E("sy", lambda e: [
                e.dma_start(out=tapd["bdg"][:, 0:8], in_=s["CAb"]),
                e.dma_start(out=tapd["bdg"][:, 8:16], in_=s["SAb"]),
                e.dma_start(out=tapd["bdg"][:, 16:32], in_=s["lncol"]),
                e.dma_start(out=tapd["bdh"][:, 0:D], in_=s["h1tm"]),
            ] + [e.dma_start(out=tapd["bdh"][:, D + kc:D + kc + 1],
                             in_=h1[kc][:, BNDC:BNDC + 1],
                             allow_slow_non_contiguous=True) for kc in range(KC)],
                wait=[tEv[(-1, KC - 1)], tBndCA], dma=4 + KC)
    tH1done = tEv[(M - 1, KC - 1)]

    tDm = {}; tXk = {}; tXv = {}
    tKt = {}; tKe = {}; tVt = {}; tVe = {}; tKk = {}
    tAx = {}; tSu = {}
    tSinit = E("dv", lambda e: e.memset(s["spair"], 0.0), wait=[])

    def fdmix(mm):
        def f(e):
            last = None
            for kc in range(KC):
                last = e.tensor_tensor(s["dmix"][:, 128 * kc:128 * (kc + 1)],
                                       h1[kc][:, h1c(mm)], h1[kc][:, h1c(mm, -1)],
                                       ALU.subtract)
            return last
        return f

    def fmixv(dst, tmcol0, mm):
        def f(e):
            last = None
            for kc in range(KC):
                last = e.scalar_tensor_tensor(
                    s[dst][:, 128 * kc:128 * (kc + 1)],
                    s["dmix"][:, 128 * kc:128 * (kc + 1)],
                    s["tmixS"][:, tmcol0 + kc:tmcol0 + kc + 1],
                    h1[kc][:, h1c(mm, -1)], ALU.mult, ALU.add)
            return last
        return f

    def emit_mix_gp(dst, tmcol0, mm, wait):
        def fa(e):
            last = None
            for kc in range(KC):
                last = e.tensor_scalar(
                    s[dst][:, 128 * kc:128 * (kc + 1)],
                    s["dmix"][:, 128 * kc:128 * (kc + 1)],
                    s["tmixS"][:, tmcol0 + kc:tmcol0 + kc + 1],
                    None, ALU.mult)
            return last
        tA = E("gp", fa, wait=wait)
        def fb(e):
            last = None
            for kc in range(KC):
                dv = s[dst][:, 128 * kc:128 * (kc + 1)]
                last = e.tensor_tensor(dv, dv, h1[kc][:, h1c(mm, -1)], ALU.add)
            return last
        return E("gp", fb, wait=[tA])

    tTap0 = None
    for mm in range(M):
        wl = [tEv[(mm, KC - 1)]]
        if mm >= 1:
            wl += [tXk[mm - 1], tXv[mm - 1]]
        else:
            wl.append(tSqA[M - 1])
        if mm == 1 and tTap0 is not None:
            wl.append(tTap0)
        tDm[mm] = E("dv", fdmix(mm), wait=wl)
        wl = [tDm[mm]] + ([tKt[(mm - 1, 1)]] if mm >= 1 else [])
        tXk[mm] = E("dv", fmixv("mixA", 8, mm), wait=wl)
        wl = [tDm[mm]] + ([tVt[(mm - 1, 1)]] if mm >= 1 else [])
        tXv[mm] = E("dv", fmixv("mixB", 16, mm), wait=wl)
        for hf in range(2):
            def fk(hf=hf):
                def f(e):
                    last = None
                    for kc in range(KC):
                        last = e.matmul(PRJ[hf][:, 0:512],
                                        s["mixA"][:, 128 * kc:128 * (kc + 1)],
                                        WS[0][:, kc * 1024 + 512 * hf:kc * 1024 + 512 * (hf + 1)],
                                        start=(kc == 0), stop=(kc == KC - 1))
                    return last
                return f
            wl = [tXk[mm], tW2A] + ([tKe[(mm - 1, hf)]] if mm >= 1 else [tModMM[hf]])
            tKt[(mm, hf)] = E("pe", fk(), wait=wl)
            wl = [tKt[(mm, hf)]] + ([tKk[mm - 1]] if mm >= 1 else [])
            tKe[(mm, hf)] = E("ac", (lambda hf=hf: (lambda e: e.copy(
                s["ktm"][:, 512 * hf:512 * (hf + 1)], PRJ[hf][:, 0:512])))(), wait=wl)
        for hf in range(2):
            def fv(hf=hf):
                def f(e):
                    last = None
                    for kc in range(KC):
                        last = e.matmul(PVV[hf],
                                        s["mixB"][:, 128 * kc:128 * (kc + 1)],
                                        WS[1][:, kc * 1024 + 512 * hf:kc * 1024 + 512 * (hf + 1)],
                                        start=(kc == 0), stop=(kc == KC - 1))
                    return last
                return f
            wl = [tXv[mm], tW2A] + ([tVe[(mm - 1, hf)]] if mm >= 1
                                     else [tModAdd[0], tModAdd[1]])
            tVt[(mm, hf)] = E("pe", fv(), wait=wl)
            wl = [tVt[(mm, hf)]]
            if mm >= 1:
                wl.append(tAx[(mm - 1, PAIRS - 1)])
            tVe[(mm, hf)] = E("ac", (lambda hf=hf: (lambda e: e.copy(
                s["vtm"][:, 512 * hf:512 * (hf + 1)], PVV[hf])))(), wait=wl)
        wl = [tKe[(mm, 0)], tKe[(mm, 1)], tWKM]
        if mm >= 1:
            wl.append(tAx[(mm - 1, PAIRS - 1)])
        tKk[mm] = E("dv", lambda e: e.tensor_tensor(s["ktm"], s["ktm"], s["WKM"],
                                                    ALU.mult), wait=wl)
        for cc in range(PAIRS):
            bank = 4 + cc % 2
            wl = [tKk[mm], tVe[(mm, 0)], tVe[(mm, 1)]]
            prev = (mm, cc - 2) if cc >= 2 else ((mm - 1, cc + PAIRS - 2) if mm >= 1 else None)
            if prev is not None and prev in tSu:
                wl.append(tSu[prev])
            tAx[(mm, cc)] = E("pe", (lambda cc=cc: (lambda e: e.matmul(
                ATT[cc % 2], s["ktm"][:, 128 * cc:128 * (cc + 1)],
                s["vtm"][:, 128 * cc:128 * (cc + 1)], start=True, stop=True)))(),
                wait=wl)
            def fsu(cc=cc, bank=bank):
                def f(e):
                    e.scalar_tensor_tensor(
                        s["spair"][0:64, 64 * cc:64 * (cc + 1)],
                        s["spair"][0:64, 64 * cc:64 * (cc + 1)],
                        s["WSCOL"][0:64, cc:cc + 1], ATT[cc % 2][0:64, 0:64],
                        ALU.mult, ALU.add)
                    return e.scalar_tensor_tensor(
                        s["spair"][64:128, 64 * cc:64 * (cc + 1)],
                        s["spair"][64:128, 64 * cc:64 * (cc + 1)],
                        s["WSCOL"][64:128, cc:cc + 1], ATT[cc % 2][64:128, 64:128],
                        ALU.mult, ALU.add)
                return f
            tSu[(mm, cc)] = E("dv", fsu(), wait=[tAx[(mm, cc)], tSinit, tBr])
        if mm == 0 and "p2a0" in tapd:
            tTap0 = E("sy", lambda e: [
                e.dma_start(out=tapd["p2a0"][:, 0:D], in_=s["ktm"]),
                e.dma_start(out=tapd["p2a0"][:, D:2 * D], in_=s["vtm"]),
                e.dma_start(out=tapd["p2a0"][:, 2 * D:3 * D], in_=s["mixA"]),
                e.dma_start(out=tapd["p2a0"][:, 3 * D:4 * D], in_=s["dmix"]),
            ] + [e.dma_start(out=tapd["p2a0"][:, 4 * D + kc:4 * D + kc + 1],
                             in_=h1[kc][:, BNDC:BNDC + 1],
                             allow_slow_non_contiguous=True) for kc in range(KC)],
                wait=[tKk[0], tVe[(0, 1)], tXk[0], tDm[0]], dma=4 + KC)

    ccr = s["yacc"].bitcast(F32)[:, 1024:1536]
    tCCw = E("gp", lambda e: [e.dma_start(
        out=ccin.rearrange("a (p f) -> (a p) f", p=128), in_=s["spair"])],
        wait=[tSu[(M - 1, PAIRS - 1)]], dma=1)
    tCC = E("gp", lambda e: e.collective_compute(
        "AllGather", ALU.bypass, replica_groups=[[0, 1, 2, 3, 4, 5, 6, 7]],
        ins=[ccin], outs=[ccout]), wait=[tCCw])
    tSz = E("dv", lambda e: e.memset(s["spair"], 0.0), wait=[tCCw])
    tCI = tSz
    for r in range(8):
        tRd = E("gp", (lambda r=r: (lambda e: [e.dma_start(
            out=ccr, in_=ccout[r:r + 1, :].rearrange("a (p f) -> (a p) f", p=128))]))(),
            wait=[tCC, tCI, tAx[(M - 1, PAIRS - 1)]], dma=1)
        def fsin(r=r):
            def f(e):
                last = None
                for cc in range(PAIRS):
                    last = e.scalar_tensor_tensor(
                        s["spair"][:, 64 * cc:64 * (cc + 1)],
                        ccr[:, 64 * cc:64 * (cc + 1)],
                        s["C8P"][:, 8 * cc + r:8 * cc + r + 1],
                        s["spair"][:, 64 * cc:64 * (cc + 1)], ALU.mult, ALU.add)
                return last
            return f
        tCI = E("dv", fsin(), wait=[tRd, tC8e, tSz])
    tSb0 = E("ac", lambda e: e.copy(s["sbf"], s["spair"]), wait=[tCI])

    tw2b = []
    for kc in range(KC):
        tw2b.append(E("gp", fwk(2, kc * 1024, "WrT", 128 * kc),
                      wait=[tModMM[0], tModMM[1]], dma=1))
    for kc in range(KC):
        tw2b.append(E("gp", fwk(3, kc * 1024, "WgT", 128 * kc),
                      wait=[tModMM[0], tModMM[1]], dma=1))
    for kc in range(KC):
        tw2b.append(E("gp", fwk(4, kc * 1024, "WoT", 128 * kc), dma=1))
    tRtrL = E("gp", lambda e: [e.dma_start(out=s["rtrS"][:, 8 * kc:8 * (kc + 1)],
                                           in_=d["WrtrT"][128 * kc:128 * (kc + 1), :])
                               for kc in range(KC)], dma=KC)
    tW2B = tw2b[-1]

    tDm2 = {}; tXr = {}; tXk2 = {}; tXv2 = {}; tXg2 = {}
    tRt = {}; tRe = {}; tK2t = {}; tK2e = {}; tGt = {}; tGe = {}
    tV2t = {}; tV2e = {}; tTt = {}; tTe = {}; tKk2 = {}; tRb = {}
    tAtt = {}; tAw = {}; tOutg = {}; tYe = {}; tAx2 = {}; tSu2 = {}; tSbp = {}
    tSq = {}; tS1 = {}; tS2 = {}; tStE = {}; tSqS = {}; tU = {}; tQg = {}; tRst = {}
    tBc = {}; tGn = {}; tGna = {}; tGng = {}
    tWo = {}; tX2v = {}; tX2s = {}; tL2s = {}; tL2q = {}; tL2n = {}
    tTt2 = {}; tTe2 = {}; tRtr = {}; tX2A = {}
    tRo1 = {}; tRo2 = {}; tRo3 = {}; tCs1 = {}; tRo4 = {}
    tXL2 = {}
    tL2sqTok = {}

    tCz = E("dv", lambda e: e.memset(s["cnt"], 0.0), wait=[])

    for mm in range(M):
        xb = s["xb0"] if mm % 2 == 0 else s["xb1"]
        wl = [tH1tm[M - 1]] if mm < 2 else [tX2v[mm - 2]]
        tXL2[mm] = E("gp", (lambda mm=mm, xb=xb: (lambda e: [e.dma_start(
            out=xb, in_=d["xs"][1 + T * mm:1 + T * (mm + 1), :])]))(), wait=wl, dma=1)

        wl = [tH1done]
        if mm >= 1:
            wl += [tXr[mm - 1], tXk2[mm - 1], tXv2[mm - 1], tXg2[mm - 1],
                   tS2[mm - 1], tL2sqTok[mm - 1]]
        else:
            wl += [tXk[M - 1], tXv[M - 1]]
        tDm2[mm] = E("dv", fdmix(mm), wait=wl)
        wl = [tDm2[mm]] + ([tRt[(mm - 1, KC - 1)]] if mm >= 1 else [tKt[(M - 1, 1)]])
        tXr[mm] = E("dv", fmixv("mixA", 0, mm), wait=wl)
        for mc in range(KC):
            bank = 4 + mc % 2
            def fr(mc=mc, bank=bank):
                def f(e):
                    last = None
                    for kc in range(KC):
                        last = e.matmul(PRJ[bank - 4][:, 0:128],
                                        WS[2][:, kc * 1024 + 128 * mc:kc * 1024 + 128 * (mc + 1)],
                                        s["mixA"][:, 128 * kc:128 * (kc + 1)],
                                        start=(kc == 0), stop=(kc == KC - 1))
                    return last
                return f
            wl = [tXr[mm], tW2B]
            prev = (mm, mc - 2) if mc >= 2 else ((mm - 1, mc + KC - 2) if mm >= 1 else None)
            if prev is not None and prev in tRe:
                wl.append(tRe[prev])
            if mc < 2 and mm >= 1:
                wl += [tGe[(mm - 1, KC - 2 + mc)], tSu2[(mm - 1, PAIRS - 1)]]
            elif mc < 2:
                wl.append(tKe[(M - 1, mc)])
            tRt[(mm, mc)] = E("pe", fr(), wait=wl)
            wl = [tRt[(mm, mc)]]
            if mm >= 1:
                wl += [tAtt[(mm - 1, PAIRS - 1, 1)], tRb[mm - 1], tWo[(mm - 1, 1)]]
            tRe[(mm, mc)] = E("ac", (lambda mc=mc, bank=bank: (lambda e: e.copy(
                s["rfm"][:, 128 * mc:128 * (mc + 1)], PRJ[bank - 4][:, 0:128])))(), wait=wl)
        wl = [tDm2[mm]] + ([tK2t[(mm - 1, KC - 1)]] if mm >= 1 else [])
        tXk2[mm] = E("dv", fmixv("mixC", 8, mm), wait=wl)
        for mc in range(KC):
            bank = 4 + mc % 2
            def fk2(mc=mc, bank=bank):
                def f(e):
                    last = None
                    for kc in range(KC):
                        last = e.matmul(PRJ[bank - 4][:, 0:128],
                                        WS[0][:, kc * 1024 + 128 * mc:kc * 1024 + 128 * (mc + 1)],
                                        s["mixC"][:, 128 * kc:128 * (kc + 1)],
                                        start=(kc == 0), stop=(kc == KC - 1))
                    return last
                return f
            wl = [tXk2[mm]]
            if mc >= 2:
                wl.append(tK2e[(mm, mc - 2)])
            else:
                wl.append(tRe[(mm, KC - 2 + mc)])
            tK2t[(mm, mc)] = E("pe", fk2(), wait=wl)
            wl = [tK2t[(mm, mc)]]
            if mm >= 1:
                wl += [tAtt[(mm - 1, PAIRS - 1, 1)], tTt[(mm - 1, KC - 1)]]
            tK2e[(mm, mc)] = E("ac", (lambda mc=mc, bank=bank: (lambda e: e.copy(
                s["kfm"][:, 128 * mc:128 * (mc + 1)], PRJ[bank - 4][:, 0:128])))(), wait=wl)
        wl = [tDm2[mm]] + ([tV2t[(mm - 1, 1)]] if mm >= 1 else [tVt[(M - 1, 1)]])
        tXv2[mm] = E("dv", fmixv("mixB", 16, mm), wait=wl)
        for hf in range(2):
            bank = 4 + hf
            def fv2(hf=hf, bank=bank):
                def f(e):
                    last = None
                    for kc in range(KC):
                        last = e.matmul(PRJ[bank - 4][:, 0:512],
                                        s["mixB"][:, 128 * kc:128 * (kc + 1)],
                                        WS[1][:, kc * 1024 + 512 * hf:kc * 1024 + 512 * (hf + 1)],
                                        start=(kc == 0), stop=(kc == KC - 1))
                    return last
                return f
            wl = [tXv2[mm], tK2e[(mm, KC - 2 + hf)]]
            tV2t[(mm, hf)] = E("pe", fv2(), wait=wl)
            wl = [tV2t[(mm, hf)]]
            if mm >= 1:
                wl += [tOutg[(mm - 1, PAIRS - 1)], tAx2[(mm - 1, PAIRS - 1)]]
            else:
                wl.append(tAx[(M - 1, PAIRS - 1)])
            tV2e[(mm, hf)] = E("ac", (lambda hf=hf, bank=bank: (lambda e: e.copy(
                s["vtm"][:, 512 * hf:512 * (hf + 1)], PRJ[bank - 4][:, 0:512])))(), wait=wl)
        wl = [tDm2[mm]] + ([tGt[(mm - 1, KC - 1)]] if mm >= 1 else [])
        tXg2[mm] = E("dv", fmixv("mixD", 24, mm), wait=wl)
        for mc in range(KC):
            bank = 4 + mc % 2
            def fg(mc=mc, bank=bank):
                def f(e):
                    last = None
                    for kc in range(KC):
                        last = e.matmul(PRJ[bank - 4][:, 0:128],
                                        WS[3][:, kc * 1024 + 128 * mc:kc * 1024 + 128 * (mc + 1)],
                                        s["mixD"][:, 128 * kc:128 * (kc + 1)],
                                        start=(kc == 0), stop=(kc == KC - 1))
                    return last
                return f
            wl = [tXg2[mm], tW2B]
            if mc >= 2:
                wl.append(tGe[(mm, mc - 2)])
            else:
                wl.append(tV2e[(mm, mc)])
            tGt[(mm, mc)] = E("pe", fg(), wait=wl)
            wl = [tGt[(mm, mc)]]
            if mm >= 1:
                wl.append(tGng[(mm - 1, PAIRS - 1)])
            tGe[(mm, mc)] = E("ac", (lambda mc=mc, bank=bank: (lambda e: e.activation(
                s["gch"][:, 128 * mc:128 * (mc + 1)], PRJ[bank - 4][:, 0:128], AF.Silu)))(),
                wait=wl)
        for kc in range(KC):
            bank = kc % 2
            wl = [tK2e[(mm, kc)], ptTok[bank]]
            tTt[(mm, kc)] = E("pe", (lambda kc=kc, bank=bank: (lambda e: e.transpose(
                pt[bank], s["kfm"][:, 128 * kc:128 * (kc + 1)], s["IDNs"])))(), wait=wl)
            wl = [tTt[(mm, kc)]]
            if mm >= 1 and kc < 2:
                wl.append(tKk2[mm - 1])
            ptTok[bank] = tTe[(mm, kc)] = E("ac", (lambda kc=kc, bank=bank: (
                lambda e: e.copy(s["ktm"][:, 128 * kc:128 * (kc + 1)], pt[bank])))(),
                wait=wl)
        wl = [tTe[(mm, KC - 1)], tWKM]
        if mm >= 1:
            wl.append(tAx2[(mm - 1, PAIRS - 1)])
        else:
            wl.append(tAx[(M - 1, PAIRS - 1)])
        tKk2[mm] = E("dv", lambda e: e.tensor_tensor(s["ktm"], s["ktm"], s["WKM"],
                                                     ALU.mult), wait=wl)
        wl = [tRe[(mm, KC - 1)], tWBC]
        if mm >= 1:
            wl.append(tOutg[(mm - 1, PAIRS - 1)])
        tRb[mm] = E("dv", lambda e: e.tensor_tensor(s["rrb"], s["rfm"], s["WBCm"],
                                                    ALU.mult), wait=wl)

        for cc in range(PAIRS):
            for hs in range(2):
                bank = hs
                wl = [tK2e[(mm, cc)], tRe[(mm, cc)]]
                if (mm, cc - 1, hs) in tAw:
                    wl.append(tAw[(mm, cc - 1, hs)])
                elif mm >= 1:
                    wl.append(tAw[(mm - 1, PAIRS - 1, hs)])
                else:
                    wl.append(tSu[(M - 1, PAIRS - 1)])
                if hs == 1 and (mm, cc - 1) in tYe:
                    wl.append(tYe[(mm, cc - 1)])
                elif hs == 1 and mm >= 1:
                    wl.append(tYe[(mm - 1, PAIRS - 1)])
                r0 = 64 * hs
                tAtt[(mm, cc, hs)] = E("pe", (lambda cc=cc, r0=r0, hs=hs: (
                    lambda e: e.matmul(ATT[hs],
                                       s["kfm"][r0:r0 + 64, 128 * cc:128 * (cc + 1)],
                                       s["rfm"][r0:r0 + 64, 128 * cc:128 * (cc + 1)],
                                       start=True, stop=True)))(), wait=wl)
                wl = [tAtt[(mm, cc, hs)], tWmat]
                if (mm, cc - 1) in tOutg:
                    wl.append(tOutg[(mm, cc - 1)])
                elif mm >= 1:
                    wl.append(tOutg[(mm - 1, PAIRS - 1)])
                if cc == 0 and mm >= 1:
                    wl.append(tSqS[mm - 1])
                tAw[(mm, cc, hs)] = E("dv", (lambda cc=cc, hs=hs, bank=bank: (
                    lambda e: e.tensor_tensor(
                        s["attw"][:, 128 * hs:128 * (hs + 1)], ATT[hs],
                        s["wmat"][:, 128 * (2 * cc + hs):128 * (2 * cc + hs + 1)],
                        ALU.mult)))(), wait=wl)
            wl = [tAw[(mm, cc, 0)], tAw[(mm, cc, 1)], tV2e[(mm, 0)], tV2e[(mm, 1)],
                  tRb[mm], tSb0]
            if (mm - 1, cc) in tSbp:
                wl.append(tSbp[(mm - 1, cc)])
            if (mm, cc - 1) in tYe:
                wl.append(tYe[(mm, cc - 1)])
            elif mm >= 1:
                wl.append(tRo4[mm - 1])
            def fout(cc=cc):
                def f(e):
                    e.matmul(POUT[0:64, :], s["vtm"][:, 128 * cc:128 * cc + 64],
                             s["attw"][:, 0:128], start=True, stop=False)
                    e.matmul(POUT[0:64, :], s["sbf"][0:64, 64 * cc:64 * (cc + 1)],
                             s["rrb"][0:64, 128 * cc:128 * (cc + 1)],
                             start=False, stop=True)
                    e.matmul(POUT[64:128, :],
                             s["vtm"][:, 128 * cc + 64:128 * (cc + 1)],
                             s["attw"][:, 128:256], start=True, stop=False,
                             tile_position=(0, 64))
                    return e.matmul(POUT[64:128, :],
                                    s["sbf"][64:128, 64 * cc:64 * (cc + 1)],
                                    s["rrb"][64:128, 128 * cc:128 * (cc + 1)],
                                    start=False, stop=True, tile_position=(64, 64))
                return f
            tOutg[(mm, cc)] = E("pe", fout(), wait=wl)
            wl = [tOutg[(mm, cc)]]
            if mm >= 1 and cc == 0:
                wl.append(tGn[(mm - 1, PAIRS - 1)])
            tYe[(mm, cc)] = E("ac", (lambda cc=cc: (lambda e: e.copy(
                s["ych"][:, 128 * cc:128 * (cc + 1)], POUT)))(), wait=wl)
            wl = [tKk2[mm], tV2e[(mm, 0)], tV2e[(mm, 1)]]
            if (mm, cc - 1) in tSu2:
                wl.append(tSu2[(mm, cc - 1)])
            elif mm >= 1:
                wl.append(tSu2[(mm - 1, PAIRS - 1)])
            else:
                wl.append(tSu[(M - 1, PAIRS - 1)])
            tAx2[(mm, cc)] = E("pe", (lambda cc=cc: (lambda e: e.matmul(
                PAX2, s["ktm"][:, 128 * cc:128 * (cc + 1)],
                s["vtm"][:, 128 * cc:128 * (cc + 1)], start=True, stop=True)))(),
                wait=wl)
            def fsu2(cc=cc):
                def f(e):
                    e.scalar_tensor_tensor(
                        s["spair"][0:64, 64 * cc:64 * (cc + 1)],
                        s["spair"][0:64, 64 * cc:64 * (cc + 1)],
                        s["WSCOL"][0:64, cc:cc + 1], PAX2[0:64, 0:64],
                        ALU.mult, ALU.add)
                    return e.scalar_tensor_tensor(
                        s["spair"][64:128, 64 * cc:64 * (cc + 1)],
                        s["spair"][64:128, 64 * cc:64 * (cc + 1)],
                        s["WSCOL"][64:128, cc:cc + 1], PAX2[64:128, 64:128],
                        ALU.mult, ALU.add)
                return f
            tSu2[(mm, cc)] = E("dv", fsu2(), wait=[tAx2[(mm, cc)], tCI])
            tSbp[(mm, cc)] = E("ac", (lambda cc=cc: (lambda e: e.copy(
                s["sbf"][:, 64 * cc:64 * (cc + 1)],
                s["spair"][:, 64 * cc:64 * (cc + 1)])))(),
                wait=[tSu2[(mm, cc)], tOutg[(mm, cc)]])

        wl = [tYe[(mm, PAIRS - 1)], tXr[mm], tXk2[mm], tXv2[mm], tXg2[mm]]
        tSq[mm] = E("ac", lambda e: e.activation(s["dmix"], s["ych"], AF.Square),
                    wait=wl)
        wl = [tYe[(mm, PAIRS - 1)], tLD]
        if mm >= 1:
            wl += [tStE[mm - 1], tGn[(mm - 1, PAIRS - 1)], tRo4[mm - 1]]
        def fs1():
            def f(e):
                last = None
                for cc in range(PAIRS):
                    last = e.matmul(PS1,
                                    s["SELSs"][:, 16 * cc:16 * (cc + 1)],
                                    s["ych"][:, 128 * cc:128 * (cc + 1)],
                                    start=(cc == 0), stop=(cc == PAIRS - 1))
                return last
            return f
        tS1[mm] = E("pe", fs1(), wait=wl)
        wl = [tSq[mm]]
        def fs2():
            def f(e):
                last = None
                for cc in range(PAIRS):
                    last = e.matmul(PS2,
                                    s["SELSs"][:, 16 * cc:16 * (cc + 1)],
                                    s["dmix"][:, 128 * cc:128 * (cc + 1)],
                                    start=(cc == 0), stop=(cc == PAIRS - 1))
                return last
            return f
        tS2[mm] = E("pe", fs2(), wait=wl)
        wl = [tS1[mm], tS2[mm]]
        if mm >= 1:
            wl.append(tRst[mm - 1])
        tStE[mm] = E("ac", lambda e: (
            e.copy(s["stats"][:, 0:128], PS1),
            e.copy(s["rstd"], PS2))[-1], wait=wl)
        wl = [tStE[mm]]
        if mm >= 1:
            wl.append(tGna[(mm - 1, PAIRS - 1)])
        tSqS[mm] = E("ac", lambda e: e.activation(s["sq1"], s["stats"][:, 0:128],
                                                  AF.Square), wait=wl)
        wl = [tSqS[mm]]
        if mm >= 1:
            wl.append(tBc[(mm - 1, PAIRS - 1)])
        tU[mm] = E("dv", lambda e: e.scalar_tensor_tensor(
            s["rstd"], s["rstd"], 64.0, s["sq1"],
            ALU.mult, ALU.subtract), wait=wl)
        tQg[mm] = E("ac", lambda e: e.activation(s["rstd"], s["rstd"], AF.Sqrt,
                                                 bias=s["epsc"][0:16, 1:2]),
                    wait=[tU[mm], tEps])
        tRs1 = E("dv", lambda e: e.reciprocal(s["rstd"], s["rstd"]),
                 wait=[tQg[mm]])
        tRs2 = E("dv", lambda e: (
            e.tensor_copy(s["rstdb"], s["rstd"]),
            e.tensor_tensor(s["s1rb"], s["stats"][:, 0:128], s["rstd"],
                            ALU.mult))[-1], wait=[tRs1])
        tRst[mm] = E("dv", lambda e: e.tensor_scalar(
            s["s1rb"], s["s1rb"], -0.015625, None, ALU.mult), wait=[tRs2])
        for cc in range(PAIRS):
            wl = [tRst[mm], tLD]
            if (mm, cc - 1) in tGn:
                wl.append(tGn[(mm, cc - 1)])
            else:
                wl += [tGe[(mm, KC - 1)], tV2e[(mm, 1)]]
            def fbc(cc=cc):
                def f(e):
                    e.matmul(PB1, s["SELTS64s"][:, 128 * cc:128 * (cc + 1)],
                             s["rstdb"], start=True, stop=True)
                    return e.matmul(PB0,
                                    s["SELTS64s"][:, 128 * cc:128 * (cc + 1)],
                                    s["s1rb"], start=True, stop=True)
                return f
            tBc[(mm, cc)] = E("pe", fbc(), wait=wl)
            wl = [tBc[(mm, cc)], tYe[(mm, cc)], tOutg[(mm, PAIRS - 1)]]
            if (mm, cc - 1) in tGna:
                wl.append(tGna[(mm, cc - 1)])
            tGnA = E("dv", (lambda cc=cc: (lambda e: e.scalar_tensor_tensor(
                s["zt"], PB1, 1.0,
                s["ych"][:, 128 * cc:128 * (cc + 1)],
                ALU.mult, ALU.mult)))(), wait=wl)
            tGn[(mm, cc)] = E("dv", lambda e: e.tensor_tensor(
                s["zt"], s["zt"], PB0, ALU.add), wait=[tGnA])
            wl = [tGn[(mm, cc)]]
            if (mm, cc - 1) in tGng:
                wl.append(tGng[(mm, cc - 1)])
            elif mm >= 1:
                wl.append(tWo[(mm - 1, 1)])
            tGna[(mm, cc)] = E("dv", (lambda cc=cc: (lambda e: e.tensor_scalar(
                s["rfm"][:, 128 * cc:128 * (cc + 1)], s["zt"],
                s["lnwS"][:, cc:cc + 1], s["lnbS"][:, cc:cc + 1],
                ALU.mult, ALU.add)))(),
                wait=wl)
            tGng[(mm, cc)] = E("gp", (lambda cc=cc: (lambda e: e.tensor_tensor(
                s["rfm"][:, 128 * cc:128 * (cc + 1)],
                s["rfm"][:, 128 * cc:128 * (cc + 1)],
                s["gch"][:, 128 * cc:128 * (cc + 1)], ALU.mult)))(),
                wait=[tGna[(mm, cc)], tGe[(mm, cc)]])
        def fwo(hf):
            def f(e):
                last = None
                for kc in range(KC):
                    last = e.matmul(PWO[hf][:, 0:512],
                                    s["rfm"][:, 128 * kc:128 * (kc + 1)],
                                    WS[4][:, kc * 1024 + 512 * hf:kc * 1024 + 512 * (hf + 1)],
                                    start=(kc == 0), stop=(kc == KC - 1))
                return last
            return f
        def fx2ah(hf):
            def f(e):
                return e.scalar_tensor_tensor(
                    s["x2f"][:, 512 * hf:512 * (hf + 1)], PWO[hf][:, 0:512],
                    1.0, s["modG"][:, 512 * hf:512 * (hf + 1)],
                    ALU.mult, ALU.mult)
            return f
        wl = [tGng[(mm, PAIRS - 1)], tBc[(mm, PAIRS - 1)], tGn[(mm, PAIRS - 1)]]
        if mm >= 1:
            wl.append(tX2A[mm - 1])
        else:
            wl += [tVe[(M - 1, 0)], tVe[(M - 1, 1)]]
        tWo[(mm, 0)] = E("pe", fwo(0), wait=wl)
        wl = [tWo[(mm, 0)], tGA]
        if mm >= 1:
            wl.append(tL2n[mm - 1])
        tXa0 = E("dv", fx2ah(0), wait=wl)
        tWo[(mm, 1)] = E("pe", fwo(1), wait=[tXa0])
        tX2A[mm] = tX2a = E("dv", fx2ah(1), wait=[tWo[(mm, 1)]])
        wl = [tX2a, tXL2[mm]]
        def fx2b(xb=xb):
            def f(e):
                last = None
                for hf in range(2):
                    last = e.tensor_tensor(s["x2f"][:, 512 * hf:512 * (hf + 1)],
                                           s["x2f"][:, 512 * hf:512 * (hf + 1)],
                                           xb[:, 512 * hf:512 * (hf + 1)], ALU.add)
                return last
            return f
        tX2v[mm] = E("dv", fx2b(), wait=wl)
        tX2s[mm] = E("gp", (lambda mm=mm: (lambda e: [e.dma_start(
            out=x2s[T * mm:T * (mm + 1), :], in_=s["x2f"])]))(), wait=[tX2v[mm]], dma=1)
        tL2sqTok[mm] = tL2sq = E("ac", lambda e: e.activation(
            s["dmix"], s["x2f"], AF.Square, accum_out=s["lncol"][:, 2:3]),
            wait=[tX2v[mm], tS2[mm]])
        tL2s[mm] = E("dv", lambda e: e.reduce_sum(
            s["lncol"][:, 0:1], s["x2f"], mybir.AxisListType.X), wait=[tX2v[mm]])
        tL2ma = E("dv", lambda e: (
            e.tensor_scalar(s["lncol"][:, 12:13], s["lncol"][:, 0:1],
                            1.0 / 1024, None, ALU.mult),
            e.tensor_scalar(s["lncol"][:, 11:12], s["lncol"][:, 2:3],
                            1.0 / 1024, None, ALU.mult))[-1],
            wait=[tL2s[mm], tL2sq])
        tL2mb = E("dv", lambda e: e.tensor_tensor(
            s["lncol"][:, 1:2], s["lncol"][:, 12:13],
            s["lncol"][:, 12:13], ALU.mult), wait=[tL2ma])
        tL2m = E("dv", lambda e: e.tensor_tensor(
            s["lncol"][:, 13:14], s["lncol"][:, 11:12],
            s["lncol"][:, 1:2], ALU.subtract), wait=[tL2mb])
        tL2q[mm] = E("ac", lambda e: e.activation(s["lncol"][:, 14:15],
                                                  s["lncol"][:, 13:14], AF.Sqrt,
                                                  bias=s["epsc"][:, 0:1]),
                     wait=[tL2m, tEps])
        tL2r = E("dv", lambda e: e.reciprocal(s["lncol"][:, 15:16],
                                              s["lncol"][:, 14:15]),
                 wait=[tL2q[mm]])
        wl = [tL2r, tX2s[mm]]
        if mm >= 1:
            wl.append(tTt2[(mm - 1, KC - 1)])
        tL2n[mm] = E("dv", lambda e: e.tensor_scalar(
            s["h1tm"], s["x2f"], s["lncol"][:, 12:13],
            s["lncol"][:, 15:16], ALU.subtract, ALU.mult), wait=wl)
        for kc in range(KC):
            bank = kc % 2
            wl = [tL2n[mm], ptTok[bank]]
            tTt2[(mm, kc)] = E("pe", (lambda kc=kc, bank=bank: (lambda e: e.transpose(
                pt[bank], s["h1tm"][:, 128 * kc:128 * (kc + 1)], s["IDNs"])))(), wait=wl)
            ptTok[bank] = tTe2[(mm, kc)] = E("dv", (lambda mm=mm, kc=kc, bank=bank: (
                lambda e: e.tensor_scalar(h1[kc][:, h2c(mm)], pt[bank],
                                          s["CBc"][:, kc:kc + 1],
                                          s["SBc"][:, kc:kc + 1],
                                          ALU.mult, ALU.add)))(),
                wait=[tTt2[(mm, kc)], tCBc, tSBc])
        wl = [tTe2[(mm, KC - 1)], tRtrL, tStE[mm]]
        if mm >= 1:
            wl.append(tRo2[mm - 1])
        def frtr(mm=mm):
            def f(e):
                last = None
                for kc in range(KC):
                    last = e.matmul(PRT, h1[kc][:, h2c(mm)],
                                    s["rtrS"][:, 8 * kc:8 * (kc + 1)],
                                    start=(kc == 0), stop=(kc == KC - 1))
                return last
            return f
        tRtr[mm] = E("pe", frtr(), wait=wl)
        wl = [tRtr[mm]]
        if mm >= 1:
            wl.append(tRo2[mm - 1])
        tRo1a = E("dv", lambda e: e.reduce_max(
            s["rcol"][:, 0:1], PRT, mybir.AxisListType.X), wait=wl)
        tRo1[mm] = E("dv", lambda e: e.tensor_scalar(
            s["rcol"][:, 1:2], s["rcol"][:, 0:1], -1.0, None, ALU.mult),
            wait=[tRo1a])
        wl = [tRo1[mm]]
        if mm >= 1:
            wl.append(tRo4[mm - 1])
        tRo2[mm] = E("ac", lambda e: e.activation(
            s["gts"], PRT, AF.Exp, bias=s["rcol"][:, 1:2],
            accum_out=s["rcol"][:, 2:3]), wait=wl)
        tRo3a = E("dv", lambda e: (
            e.reciprocal(s["rcol"][:, 3:4], s["rcol"][:, 2:3]),
            e.tensor_tensor(s["pgt"], s["gts"], s["PERTs"], ALU.mult))[-1],
            wait=[tRo2[mm], tLD])
        tRo3b = E("dv", lambda e: (
            e.max(s["srt"], s["pgt"]),
            e.tensor_scalar(s["gts"], s["gts"], s["rcol"][:, 3:4], None,
                            ALU.mult))[-1], wait=[tRo3a])
        def fro3(mm=mm):
            def f(e):
                e.tensor_scalar(s["oh1st"][:, 8 * mm:8 * (mm + 1)], s["pgt"],
                                s["srt"][:, 0:1], None, ALU.is_equal)
                e.tensor_scalar(s["oh2st"][:, 8 * mm:8 * (mm + 1)], s["pgt"],
                                s["srt"][:, 1:2], None, ALU.is_equal)
                e.scalar_tensor_tensor(s["r8a"], s["pgt"], s["srt"][:, 0:1], s["gts"],
                                       ALU.is_equal, ALU.mult,
                                       accum_out=s["ggst"][:, 2 * mm:2 * mm + 1])
                return e.scalar_tensor_tensor(
                    s["r8a"], s["pgt"], s["srt"][:, 1:2], s["gts"],
                    ALU.is_equal, ALU.mult,
                    accum_out=s["ggst"][:, 2 * mm + 1:2 * mm + 2])
            return f
        tRo3c = E("dv", fro3(), wait=[tRo3b])
        tRo3[mm] = E("dv", (lambda mm=mm: (lambda e: e.tensor_tensor(
            s["ohsb"], s["oh1st"][:, 8 * mm:8 * (mm + 1)],
            s["oh2st"][:, 8 * mm:8 * (mm + 1)], ALU.add)))(), wait=[tRo3c])
        wl = [tRo3[mm], tStE[mm], tYe[(mm, PAIRS - 1)], tGn[(mm, PAIRS - 1)]]
        if mm >= 1:
            wl.append(tRo4[mm - 1])
        def fcs():
            def f(e):
                e.matmul(PCS1, s["LTs"], s["ohsb"], start=True, stop=True)
                return e.matmul(PCS2, s["ONESBs"], s["ohsb"],
                                start=True, stop=True)
            return f
        tCs1[mm] = E("pe", fcs(), wait=wl)
        tRo4a = E("dv", lambda e: e.tensor_tensor(
            s["r8a"], PCS1, s["cnt"], ALU.add), wait=[tCs1[mm], tCz])
        def fro4(mm=mm):
            def f(e):
                e.scalar_tensor_tensor(s["gts"], s["pgt"], s["srt"][:, 0:1],
                                       s["r8a"], ALU.is_equal, ALU.mult,
                                       accum_out=s["pgst"][:, 2 * mm:2 * mm + 1])
                e.scalar_tensor_tensor(s["pgt"], s["pgt"], s["srt"][:, 1:2],
                                       s["r8a"], ALU.is_equal, ALU.mult,
                                       accum_out=s["pgst"][:, 2 * mm + 1:2 * mm + 2])
                return e.tensor_tensor(s["cnt"], s["cnt"], PCS2, ALU.add)
            return f
        tRo4[mm] = E("dv", fro4(), wait=[tRo4a])

    tCC2w = E("gp", lambda e: [e.dma_start(out=ccin2, in_=s["cnt"][0:1, :])],
              wait=[tRo4[M - 1]], dma=1)
    tCC2 = E("gp", lambda e: e.collective_compute(
        "AllGather", ALU.bypass, replica_groups=[[0, 1, 2, 3, 4, 5, 6, 7]],
        ins=[ccin2], outs=[ccout2]), wait=[tCC2w])
    tCntR = E("gp", lambda e: [e.dma_start(out=s["cnts"], in_=ccout2)],
              wait=[tCC2], dma=1)
    tBase = E("pe", lambda e: e.matmul(PBASE, s["pmaskS"], s["cnts"],
                                       start=True, stop=True),
              wait=[tCntR, tSu2[(M - 1, PAIRS - 1)], tLD])
    tBaseE = E("ac", lambda e: e.copy(s["basebc"], PBASE), wait=[tBase])

    tCmb = {}
    for mm in range(M):
        def fca(mm=mm):
            def f(e):
                e.scalar_tensor_tensor(s["r8a"], s["oh1st"][:, 8 * mm:8 * (mm + 1)],
                                       1.0, s["basebc"], ALU.mult, ALU.mult,
                                       accum_out=s["rcol"][:, 4:5])
                return e.scalar_tensor_tensor(
                    s["gts"], s["oh2st"][:, 8 * mm:8 * (mm + 1)],
                    1.0, s["basebc"], ALU.mult, ALU.mult,
                    accum_out=s["rcol"][:, 5:6])
            return f
        ta_ = E("dv", fca(), wait=[tBaseE, tModB])
        ta2_ = E("dv", (lambda mm=mm: (lambda e: (
            e.tensor_scalar(s["rcol"][:, 6:7], s["rcol"][:, 4:5],
                            s["pgst"][:, 2 * mm:2 * mm + 1], CAP, ALU.add,
                            ALU.is_le),
            e.tensor_scalar(s["rcol"][:, 7:8], s["rcol"][:, 5:6],
                            s["pgst"][:, 2 * mm + 1:2 * mm + 2], CAP, ALU.add,
                            ALU.is_le))[-1]))(), wait=[ta_])
        ta3_ = E("dv", (lambda mm=mm: (lambda e: (
            e.tensor_scalar(s["rcol"][:, 9:10], s["rcol"][:, 7:8],
                            s["ggst"][:, 2 * mm + 1:2 * mm + 2], None, ALU.mult),
            e.tensor_scalar(s["rcol"][:, 8:9], s["rcol"][:, 6:7],
                            s["ggst"][:, 2 * mm:2 * mm + 1], None,
                            ALU.mult))[-1]))(), wait=[ta2_])
        tb1_ = E("dv", lambda e: e.tensor_scalar(
            s["rcol"][:, 10:11], s["rcol"][:, 8:9],
            s["rcol"][:, 9:10], 1e-6, ALU.add, ALU.add), wait=[ta3_])
        tb_ = E("dv", lambda e: e.reciprocal(
            s["rcol"][:, 11:12], s["rcol"][:, 10:11]), wait=[tb1_])
        tc_ = E("dv", lambda e: (
            e.tensor_scalar(s["rcol"][:, 12:13], s["rcol"][:, 8:9],
                            s["rcol"][:, 11:12], None, ALU.mult),
            e.tensor_scalar(s["rcol"][:, 13:14], s["rcol"][:, 9:10],
                            s["rcol"][:, 11:12], None, ALU.mult))[-1],
            wait=[tb_])
        tcd1 = E("dv", (lambda mm=mm: (lambda e: e.tensor_scalar(
            s["r8a"], s["oh1st"][:, 8 * mm:8 * (mm + 1)],
            s["rcol"][:, 12:13], None, ALU.mult)))(), wait=[tc_])
        tCmb[mm] = E("dv", (lambda mm=mm: (lambda e: e.scalar_tensor_tensor(
            s["combine"][:, 8 * mm:8 * (mm + 1)],
            s["oh2st"][:, 8 * mm:8 * (mm + 1)], s["rcol"][:, 13:14],
            s["r8a"], ALU.mult, ALU.add)))(), wait=[tcd1])

    tPEend = tRtr[M - 1]
    tg5w = []
    for kc in range(KC):
        slot = 0 if kc < 4 else 1
        col0 = (kc % 4) * 2048
        tg5w.append(E("gp", (lambda slot=slot, col0=col0, kc=kc: (lambda e: [e.dma_start(
            out=WS[slot][:, col0:col0 + 2048],
            in_=d["WgateT"][128 * kc:128 * (kc + 1), :])]))(),
            wait=[tPEend], dma=1))
    tG5W = tg5w[-1]
    tGB = E("gp", lambda e: [e.dma_start(
        out=s["modG"], in_=dram_ap(4096 + 1024, [[0, 128], [1, D]]))],
        wait=[tModB, tX2v[M - 1]], dma=1)

    tG5t = {}; tG5e = {}; tDn = {}; tYa = {}; tX5 = {}; tFo = {}; tOutD = {}
    wdidx = 0
    wdslotTok = {}
    for q in range(NE):
        for hk in range(HKC):
            gi = q * HKC + hk
            bank = 4 + gi % 2
            wl = [tG5W, tTe2[(4 * q + 3, KC - 1)]]
            if gi == 0:
                wl += [tX2v[M - 1], tSu2[(M - 1, PAIRS - 1)]]
            prev = gi - 2
            if prev >= 0:
                wl.append(tG5e[(prev // HKC, prev % HKC)])
            def fg5(q=q, hk=hk, bank=bank):
                def f(e):
                    last = None
                    for kc in range(KC):
                        wsl = WS[0] if kc < 4 else WS[1]
                        last = e.matmul(
                            PRJ[bank - 4][:, 0:512],
                            wsl[:, (kc % 4) * 2048 + 128 * hk:(kc % 4) * 2048 + 128 * (hk + 1)],
                            h1[kc][:, ET * q:ET * (q + 1)],
                            start=(kc == 0), stop=(kc == KC - 1))
                    return last
                return f
            tG5t[(q, hk)] = E("pe", fg5(), wait=wl)
            wl = [tG5t[(q, hk)]]
            if q == 0 and hk < 2:
                wl.append(tWo[(M - 1, 1)])
            if q >= 1:
                wl.append(tDn[(q - 1, EXP - 1, 1, 3)])
            tG5e[(q, hk)] = E("ac", (lambda hk=hk, bank=bank: (
                lambda e: e.activation(WS[4][:, 512 * hk:512 * (hk + 1)],
                                       PRJ[bank - 4][:, 0:512], AF.Silu)))(), wait=wl)
        for ex in range(EXP):
            for oh in range(2):
                slot = WS[2] if wdidx % 2 == 0 else WS[3]
                wl = [tPEend]
                if wdidx >= 2:
                    wl.append(wdslotTok[wdidx - 2])
                def fwd(slot=slot, ex=ex, oh=oh):
                    def f(e):
                        res = []
                        for part in range(4):
                            src = d["WdT"][ex, :, 512 * oh:512 * (oh + 1)] \
                                .rearrange("(o p) c -> p o c", p=128)[:, 4 * part:4 * (part + 1), :]
                            dst = slot[:, 2048 * part:2048 * (part + 1)] \
                                .rearrange("p (o c) -> p o c", c=512)
                            res.append(e.dma_start(out=dst, in_=src))
                        return res
                    return f
                tWdTok = E("gp", fwd(), wait=wl, dma=4)
                for t8 in range(4):
                    bank = (t8 + 4 * (oh + 2 * ex)) % 4
                    key = (q, ex, oh, t8)
                    wl = [tWdTok, tG5e[(q, HKC - 1)]]
                    pidx = (ex * 8 + oh * 4 + t8) - 4
                    if pidx >= 0:
                        pex, prem = divmod(pidx, 8)
                        poh, pt8 = divmod(prem, 4)
                        wl.append(tYa[(q, pex, poh, pt8)])
                    elif q >= 1:
                        wl.append(tYa[(q - 1, EXP - 1, 1, t8)])
                    else:
                        wl += [tRo4[M - 1], tBaseE]
                    def fdn(oh=oh, t8=t8, bank=bank, slot=slot):
                        def f(e):
                            last = None
                            for hk in range(HKC):
                                last = e.matmul(
                                    PB[bank][:, 0:512],
                                    WS[4][:, 512 * hk + 128 * t8:512 * hk + 128 * (t8 + 1)],
                                    slot[:, 512 * hk:512 * (hk + 1)],
                                    start=(hk == 0), stop=(hk == HKC - 1))
                            return last
                        return f
                    tDn[key] = E("pe", fdn(), wait=wl)
                    mm = 4 * q + t8
                    wl = [tDn[key], tCmb[mm]]
                    if q >= 1 and ex == 0 and oh == 0:
                        wl.append(tFo[4 * (q - 1) + t8])
                    def fya(ex=ex, oh=oh, t8=t8, bank=bank, mm=mm):
                        def f(e):
                            dstc = slice(1024 * t8 + 512 * oh,
                                         1024 * t8 + 512 * (oh + 1))
                            if ex == 0:
                                return e.tensor_scalar(
                                    s["yacc"][:, dstc], PB[bank][:, 0:512],
                                    s["combine"][:, 8 * mm + ex:8 * mm + ex + 1],
                                    None, ALU.mult)
                            return e.scalar_tensor_tensor(
                                s["yacc"][:, dstc], PB[bank][:, 0:512],
                                s["combine"][:, 8 * mm + ex:8 * mm + ex + 1],
                                s["yacc"][:, dstc], ALU.mult, ALU.add)
                        return f
                    tYa[key] = E("dv", fya(), wait=wl)
                wdslotTok[wdidx] = tDn[(q, ex, oh, 3)]
                wdidx += 1
        for t8 in range(4):
            mm = 4 * q + t8
            xb = s["xb0"] if mm % 2 == 0 else s["xb1"]
            wl = [tX2s[mm]]
            if mm >= 2:
                wl.append(tFo.get(mm - 2))
            else:
                wl.append(tX2v[M - 1])
            tX5[mm] = E("gp", (lambda mm=mm, xb=xb: (lambda e: [e.dma_start(
                out=xb, in_=x2s[T * mm:T * (mm + 1), :])]))(), wait=wl, dma=1)
            wl = [tYa[(q, EXP - 1, 0, t8)], tYa[(q, EXP - 1, 1, t8)], tX5[mm], tGB,
                  tL2s[M - 1]]
            if mm >= 1:
                wl.append(tOutD.get(mm - 1))
            tFoA = E("dv", (lambda t8=t8: (lambda e: e.tensor_tensor(
                s["x2f"], s["yacc"][:, 1024 * t8:1024 * (t8 + 1)],
                s["modG"], ALU.mult)))(), wait=wl)
            tFo[mm] = E("dv", (lambda xb=xb: (lambda e: e.tensor_tensor(
                s["x2f"], s["x2f"], xb, ALU.add)))(), wait=[tFoA])
            tOutD[mm] = E("sy", (lambda mm=mm: (lambda e: [e.dma_start(
                out=out[T * mm:T * (mm + 1), :], in_=s["x2f"])]))(),
                wait=[tFo[mm]], dma=1)

    if "h1" in tapd:
        E("sy", lambda e: [e.dma_start(out=tapd["h1"][:, kc * H1W:(kc + 1) * H1W],
                                       in_=h1[kc]) for kc in range(KC)],
          wait=[tH1done], dma=KC)
    if "dk" in tapd:
        E("sy", lambda e: [e.dma_start(out=tapd["dk"][:, 0:2048], in_=s["wmat"]),
                           e.dma_start(out=tapd["dk"][:, 2048:3072], in_=s["WBCm"]),
                           e.dma_start(out=tapd["dk"][:, 3072:4096], in_=s["WKM"])],
          wait=[tWmat, tWBC, tWKM], dma=3)
    if "dk2" in tapd:
        E("sy", lambda e: [e.dma_start(out=tapd["dk2"][:, 0:8], in_=s["LNDCOL"]),
                           e.dma_start(out=tapd["dk2"][:, 8:16], in_=s["WSCOL"]),
                           e.dma_start(out=tapd["dk2"][:, 16:80], in_=s["C8P"]),
                           e.dma_start(out=tapd["dk2"][:, 80:96], in_=s["lndbc"]),
                           e.dma_start(out=tapd["dk2"][:, 96:112], in_=s["UCOL"]),
                           e.dma_start(out=tapd["dk2"][:, 112:120], in_=s["CAc"]),
                           e.dma_start(out=tapd["dk2"][:, 120:128], in_=s["SAc"]),
                           e.dma_start(out=tapd["dk2"][:, 128:136], in_=s["CBc"]),
                           e.dma_start(out=tapd["dk2"][:, 136:144], in_=s["SBc"]),
                           e.dma_start(out=tapd["dk2"][:, 144:152], in_=s["basebc"])],
          wait=[tBr, tC8e, tCAc, tSAc, tCBc, tSBc, tBaseE], dma=10)
    if "st" in tapd:
        E("sy", lambda e: [e.dma_start(out=tapd["st"], in_=s["spair"])],
          wait=[tCI], dma=1)
    if "gn" in tapd:
        E("sy", lambda e: [e.dma_start(out=tapd["gn"], in_=s["rfm"])],
          wait=[tGng[(M - 1, PAIRS - 1)]], dma=1)
    if "gs" in tapd:
        E("sy", lambda e: [e.dma_start(out=tapd["gs"][:, 0:128],
                                       in_=s["stats"][:, 0:128]),
                           e.dma_start(out=tapd["gs"][:, 128:256],
                                       in_=s["rstd"])],
          wait=[tRst[M - 1]], dma=2)
    if "ccg" in tapd:
        E("sy", lambda e: [e.dma_start(out=tapd["ccg"][0:8, :], in_=ccout),
                           e.dma_start(out=tapd["ccg"][8:9, :], in_=ccin)],
          wait=[tCC], dma=2)
    if "rt" in tapd:
        E("sy", lambda e: [e.dma_start(out=tapd["rt"], in_=s["pgst"])],
          wait=[tRo4[M - 1]], dma=1)
    if "cmb" in tapd:
        E("sy", lambda e: [e.dma_start(out=tapd["cmb"], in_=s["combine"])],
          wait=[tCmb[M - 1]], dma=1)
    if "x2" in tapd:
        E("sy", lambda e: [e.dma_start(out=tapd["x2"], in_=x2s)],
          wait=[tX2s[M - 1]], dma=1)
    if "hid" in tapd:
        E("sy", lambda e: [e.dma_start(out=tapd["hid"], in_=WS[4])],
          wait=[tG5e[(0, HKC - 1)]], dma=1)
    if "y" in tapd:
        E("sy", lambda e: [e.dma_start(out=tapd["y"], in_=s["ych"])],
          wait=[tYe[(M - 1, PAIRS - 1)]], dma=1)
    if "gg" in tapd:
        E("sy", lambda e: [e.dma_start(out=tapd["gg"], in_=s["gch"])],
          wait=[tGe[(M - 1, KC - 1)]], dma=1)

    sems = {k: nc.semaphore("sem_" + k).__enter__() for k in Prog.ENGS}

    def replay(ename):
        def go(e):
            cnt = 0
            for eng, waits, dma, fn in pr.steps:
                if eng != ename:
                    continue
                for wn, wv in waits:
                    e.wait_ge(sems[wn], wv)
                res = fn(e)
                if dma:
                    res = res if isinstance(res, (list, tuple)) else [res]
                    assert len(res) == dma, (ename, len(res), dma)
                    for r in res:
                        r.then_inc(sems[ename], 16)
                    cnt += 16 * dma
                else:
                    res.then_inc(sems[ename], 1)
                    cnt += 1
            assert cnt == pr.cnt[ename], (ename, cnt, pr.cnt[ename])
            if ename == "sy":
                e.wait_ge(sems["sy"], pr.cnt["sy"])
        return go

    with nc.Block() as block:
        block.sync(replay("sy"))
        block.gpsimd(replay("gp"))
        block.scalar(replay("ac"))
        block.vector(replay("dv"))
        block.tensor(replay("pe"))

    return nc


_NC_CACHE = {}


def kernel(**inputs):
    key = tuple(DEBUG_TAPS)
    if key not in _NC_CACHE:
        _NC_CACHE[key] = build(key)
    nc = _NC_CACHE[key]
    consts = _host_consts()
    w = _shared_weights(inputs)
    in_maps = [_prep_core_inputs(c, inputs, consts, w) for c in range(8)]
    res = run_bass_kernel_spmd(nc, in_maps, core_ids=list(range(8)))
    kernel.last_results = res.results
    full = np.empty((2, 16384, D), dtype=np.float32)
    for c in range(8):
        b, j = c // 4, c % 4
        full[b, j * TOK:(j + 1) * TOK] = res.results[c]["out"]
    return full
